# version 02
# baseline (speedup 1.0000x reference)
import numpy as np
import concourse.bass as bass
import concourse.mybir as mybir
from concourse.bass_utils import run_bass_kernel_spmd

F32 = mybir.dt.float32
BF16 = mybir.dt.bfloat16
AF = mybir.ActivationFunctionType
ALU = mybir.AluOpType
AX = mybir.AxisListType

T = 2048
D = 1024
FC = 8
NE = 8

SEM_OF = {"tensor": "pe", "vector": "dve", "scalar": "act", "gpsimd": "pool", "sync": "sy"}


def relayout_w2(w2):
    sh = w2.shape
    I, Dd = sh[-2], sh[-1]
    lead = sh[:-2]
    r = w2.reshape(lead + (I // 128, 128, Dd // 128, 128))
    perm = tuple(range(len(lead))) + (len(lead) + 2, len(lead) + 1, len(lead), len(lead) + 3)
    return np.ascontiguousarray(r.transpose(perm))


def host_constants():
    c = {}
    c["c_ident"] = np.eye(128, dtype=np.float32)
    d = 64
    inv_freq = 1.0 / (10000.0 ** (np.arange(0, d, 2, dtype=np.float32) / d))
    pos = np.arange(T, dtype=np.float32)
    freqs = np.repeat(pos[:, None] * inv_freq[None, :], 2, axis=1)
    c["c_cos"] = np.cos(freqs).astype(np.float32)
    c["c_sin"] = np.sin(freqs).astype(np.float32)
    m16 = np.ones((128, T), np.float32); m16[:, 0::16] = 0.0
    m32 = np.ones((128, T), np.float32); m32[:, 0::32] = 0.0
    c["c_mask16"] = m16
    c["c_mask32"] = m32
    tri = (np.arange(16)[:, None] <= np.arange(16)[None, :]).astype(np.float32)
    am = np.zeros((32, 32), np.float32)
    am[0:16, 0:16] = tri
    am[0:16, 16:32] = 1.0
    am[16:32, 16:32] = tri
    c["c_amask"] = np.tile(am, (1, 16)).astype(np.float32)
    i64 = np.zeros((128, 64), np.float32)
    i64[0:64] = np.eye(64); i64[64:128] = np.eye(64)
    c["c_ident64"] = i64
    return c


class Prog:
    def __init__(self, name="prog"):
        self.nc = bass.Bass("TRN2", target_bir_lowering=False)
        self.name = name
        self._ctx = []
        self.sem = {}
        self.cnt = {}
        self.out_names = []
        self.in_names = []

    def din(self, name, shape, dtype=F32):
        self.in_names.append(name)
        return self.nc.dram_tensor(name, list(shape), dtype, kind="ExternalInput").ap()

    def dout(self, name, shape, dtype=F32):
        self.out_names.append(name)
        return self.nc.dram_tensor(name, list(shape), dtype, kind="ExternalOutput").ap()

    def dtmp(self, name, shape, dtype=F32):
        return self.nc.dram_tensor(name, list(shape), dtype, kind="Internal").ap()

    def sb(self, name, shape, dtype=F32):
        self._name_ctr = getattr(self, "_name_ctr", 0) + 1
        h = self.nc.sbuf_tensor(f"{name}_{self._name_ctr}", list(shape), dtype)
        t = h.__enter__(); self._ctx.append(h)
        return t

    def start(self):
        self._sem_ctx = []
        self._scopes = []
        self._epoch = 0
        self.new_epoch(first=True)
        h = self.nc.psum_tensor("ps_arena", [128, 4096], F32)
        self.psum = h.__enter__(); self._ctx.append(h)
        h = self.nc.Block()
        self.block = h.__enter__(); self._ctx.append(h)

    def new_epoch(self, first=False):
        for s in ["pe", "dve", "act", "pool", "sy"]:
            h = self.nc.semaphore(f"{s}{self._epoch}")
            self.sem[s] = h.__enter__(); self._sem_ctx.append(h)
            self.cnt[s] = 0
        self._epoch += 1

    def push_scope(self):
        self._scopes.append(len(self._ctx))

    def pop_scope(self):
        i = self._scopes.pop()
        for h in reversed(self._ctx[i:]):
            h.__exit__(None, None, None)
        del self._ctx[i:]

    def bank(self, b, n=1):
        return self.psum[:, 512 * b:512 * (b + n)]

    def finalize(self):
        for h in reversed(self._ctx):
            h.__exit__(None, None, None)
        self._ctx = []
        for h in reversed(self._sem_ctx):
            h.__exit__(None, None, None)
        self._sem_ctx = []

    def step(self, eng, waits, fn, inc=1):
        own = SEM_OF[eng]
        sem = self.sem
        def body(e):
            for s, v in waits:
                if v > 0:
                    e.wait_ge(sem[s], v)
            last = fn(e)
            if inc:
                last.then_inc(sem[own], inc)
        getattr(self.block, eng)(body)
        if inc:
            self.cnt[own] += inc
        return self.cnt[own]

    def multi(self, eng, waits, items):
        own = SEM_OF[eng]
        sem = self.sem
        def body(e):
            for s, v in waits:
                if v > 0:
                    e.wait_ge(sem[s], v)
            for fn, inc in items:
                inst = fn(e)
                if inc:
                    inst.then_inc(sem[own], inc)
        getattr(self.block, eng)(body)
        self.cnt[own] += sum(i for _, i in items)
        return self.cnt[own]

    def chain(self, eng, waits, fns, inc_last=1):
        own = SEM_OF[eng]
        sem = self.sem
        base = self.cnt[own]
        n = len(fns)
        def body(e):
            for s, v in waits:
                if v > 0:
                    e.wait_ge(sem[s], v)
            for i, fn in enumerate(fns):
                if i > 0:
                    e.wait_ge(sem[own], base + i)
                fn(e).then_inc(sem[own], 1)
        getattr(self.block, eng)(body)
        self.cnt[own] += n
        return self.cnt[own]

    def v(self, s):
        return self.cnt[s]

    def barrier(self):
        snap = dict(self.cnt)
        for eng in ["tensor", "vector", "scalar", "gpsimd", "sync"]:
            own = SEM_OF[eng]
            waits = [(s, v) for s, v in snap.items() if s != own and v > 0]
            self.step(eng, waits, lambda e: e.nop(), inc=1)

    def wait_all(self, exclude=()):
        return [(s, self.cnt[s]) for s in ["pe", "dve", "act", "pool", "sy"] if s not in exclude and self.cnt[s] > 0]



def alloc_common(p):
    cm = {}
    cm["X"] = p.sb("X", [128, FC * T], F32)
    cm["ident"] = p.sb("ident", [128, 128], F32)
    cm["ones_col_bf"] = p.sb("ones_col_bf", [128, 1], BF16)
    cm["ones_row"] = p.sb("ones_row", [1, 128], F32)
    cm["eps_ln"] = p.sb("eps_ln", [1, 1], F32)
    return cm


def emit_init_consts(p, cm, ident_dram):
    p.step("sync", [], lambda e: e.dma_start(out=cm["ident"][:], in_=ident_dram), inc=16)
    p.multi("vector", [], [
        (lambda e: e.memset(cm["ones_col_bf"][:], 1.0), 0),
        (lambda e: e.memset(cm["ones_row"][:], 1.0), 0),
        (lambda e: e.memset(cm["eps_ln"][:], 1e-5), 1),
    ])


def emit_load_x_fm(p, cm, x_fm):
    items = [(lambda e, c=c: e.dma_start(out=cm["X"][:, T * c:T * (c + 1)],
                                         in_=x_fm[128 * c:128 * (c + 1), :]), 16) for c in range(FC)]
    return p.multi("sync", [], items)


def emit_store_x_fm(p, cm, x_fm):
    items = [(lambda e, c=c: e.dma_start(out=x_fm[128 * c:128 * (c + 1), :],
                                         in_=cm["X"][:, T * c:T * (c + 1)]), 16) for c in range(FC)]
    v = p.multi("sync", p.wait_all(exclude=("sy",)), items)
    p.step("sync", [("sy", v)], lambda e: e.nop(), inc=0)


def emit_load_x_tm(p, cm, x_tm):
    stg = [p.sb("xin_stg0", [128, D], F32), p.sb("xin_stg1", [128, D], F32)]
    dma_v = {}
    pe_after_tile = {}
    for tt in range(16):
        b = tt % 2
        w = [("pe", pe_after_tile[tt - 2])] if tt >= 2 else []
        dma_v[tt] = p.step("sync", w,
                           lambda e, tt=tt, b=b: e.dma_start(out=stg[b][:], in_=x_tm[128 * tt:128 * (tt + 1), :]),
                           inc=16)
        for c in range(FC):
            pb = 6 + (c % 2)
            w = [("sy", dma_v[tt])]
            w.append(("dve", max(0, p.v("dve") - 1)))
            vpe = p.step("tensor", w,
                         lambda e, c=c, b=b, pb=pb: e.transpose(p.bank(pb)[:, 0:128],
                                                                stg[b][:, 128 * c:128 * (c + 1)], cm["ident"][:]),
                         inc=1)
            p.step("vector", [("pe", vpe)],
                   lambda e, c=c, tt=tt, pb=pb: e.tensor_copy(cm["X"][:, T * c + 128 * tt:T * c + 128 * (tt + 1)],
                                                              p.bank(pb)[:, 0:128]),
                   inc=1)
        pe_after_tile[tt] = p.v("pe")
    return p.v("dve")


def emit_store_x_tm(p, cm, x_tm):
    stg = [p.sb("xout_stg0", [128, D], F32), p.sb("xout_stg1", [128, D], F32)]
    first_waits = p.wait_all(exclude=("pe",))
    dma_v = {}
    for tt in range(16):
        b = tt % 2
        for c in range(FC):
            pb = 6 + (c % 2)
            w = list(first_waits) if (tt == 0 and c == 0) else []
            w.append(("dve", max(0, p.v("dve") - 1)))
            if tt >= 2 and c == 0:
                w.append(("sy", dma_v[tt - 2]))
            vpe = p.step("tensor", w,
                         lambda e, c=c, tt=tt, pb=pb: e.transpose(p.bank(pb)[:, 0:128],
                                                                  cm["X"][:, T * c + 128 * tt:T * c + 128 * (tt + 1)],
                                                                  cm["ident"][:]),
                         inc=1)
            p.step("vector", [("pe", vpe)],
                   lambda e, c=c, b=b, pb=pb: e.tensor_copy(stg[b][:, 128 * c:128 * (c + 1)], p.bank(pb)[:, 0:128]),
                   inc=1)
        dma_v[tt] = p.step("sync", [("dve", p.v("dve"))],
                           lambda e, tt=tt, b=b: e.dma_start(out=x_tm[128 * tt:128 * (tt + 1), :], in_=stg[b][:]),
                           inc=16)
    p.step("sync", [("sy", p.v("sy"))], lambda e: e.nop(), inc=0)



def alloc_ln(p, cm):
    ln = {}
    ln["xb"] = [p.sb("ln_xb0", [128, 512], BF16), p.sb("ln_xb1", [128, 512], BF16)]
    ln["xsq"] = [p.sb("ln_xsq0", [128, 512], BF16), p.sb("ln_xsq1", [128, 512], BF16)]
    ln["a_row"] = p.sb("ln_a", [1, T], F32)
    ln["b_row"] = p.sb("ln_b", [1, T], F32)
    ln["scr"] = p.sb("ln_scr", [128, 1024], F32)
    return ln


def emit_ln_stats(p, cm, ln):
    X = cm["X"]
    ps_s = p.psum[0:1, 0:T]
    ps_q = p.psum[0:1, T:2 * T]
    first = True
    for w4 in range(4):
        for c in range(FC):
            b = c % 2
            wv = p.wait_all(exclude=("dve",)) if first else [("pe", max(0, p.v("pe") - 1))]
            first = False
            vc = p.step("vector", wv,
                        lambda e, c=c, w4=w4, b=b: e.tensor_copy(ln["xb"][b][:],
                                                                 X[:, T * c + 512 * w4:T * c + 512 * (w4 + 1)]),
                        inc=1)
            p.step("tensor", [("dve", vc)],
                   lambda e, c=c, w4=w4, b=b: e.matmul(ps_s[:, 512 * w4:512 * (w4 + 1)], cm["ones_col_bf"][:],
                                                       ln["xb"][b][:], start=(c == 0), stop=(c == FC - 1)),
                   inc=1)
        for c in range(FC):
            b = c % 2
            vs = p.step("scalar", [("pe", max(0, p.v("pe") - 1))],
                        lambda e, c=c, w4=w4, b=b: e.activation(ln["xsq"][b][:],
                                                                X[:, T * c + 512 * w4:T * c + 512 * (w4 + 1)],
                                                                AF.Square), inc=1)
            p.step("tensor", [("act", vs)],
                   lambda e, c=c, w4=w4, b=b: e.matmul(ps_q[:, 512 * w4:512 * (w4 + 1)], cm["ones_col_bf"][:],
                                                       ln["xsq"][b][:], start=(c == 0), stop=(c == FC - 1)),
                   inc=1)
    vpe = p.v("pe")
    a_row, b_row = ln["a_row"], ln["b_row"]
    vv = p.chain("vector", [("pe", vpe)], [
        lambda e: e.tensor_scalar(ps_s, ps_s, 1.0 / D, None, ALU.mult, ALU.bypass),
        lambda e: e.tensor_scalar(ps_q, ps_q, 1.0 / D, None, ALU.mult, ALU.bypass),
        lambda e: e.tensor_copy(b_row[:], ps_s),
        lambda e: e.tensor_tensor(a_row[:], b_row[:], b_row[:], ALU.mult),
        lambda e: e.tensor_tensor(ps_q, ps_q, a_row[:], ALU.subtract),
    ])
    va = p.step("scalar", [("dve", vv)],
                lambda e: e.activation(a_row[:], ps_q, AF.Sqrt, bias=cm["eps_ln"][0:1, :], scale=1.0), inc=1)
    p.chain("vector", [("act", va)], [
        lambda e: e.reciprocal(a_row[:], a_row[:]),
        lambda e: e.scalar_tensor_tensor(b_row[:], b_row[:], -1.0, a_row[:], ALU.mult, ALU.mult),
    ])
    return p.v("dve")


def emit_ln_apply(p, cm, ln, t0, L, Xn, xn_stride=None):
    X = cm["X"]
    xn_stride = L if xn_stride is None else xn_stride
    assert L <= 512
    a_bc = p.bank(4)[:, 0:L]
    b_bc = p.bank(5)[:, 0:L]
    items = [
        (lambda e: e.matmul(a_bc, cm["ones_row"][:], ln["a_row"][0:1, t0:t0 + L], start=True, stop=True), 0),
        (lambda e: e.matmul(b_bc, cm["ones_row"][:], ln["b_row"][0:1, t0:t0 + L], start=True, stop=True), 1),
    ]
    vbc = p.multi("tensor", [("dve", p.v("dve"))], items)
    for c in range(FC):
        p.chain("vector", [("pe", vbc)] if c == 0 else [], [
            (lambda e, c=c: e.tensor_tensor(ln["scr"][:, 0:L], X[:, T * c + t0:T * c + t0 + L], a_bc, ALU.mult)),
            (lambda e, c=c: e.tensor_tensor(Xn[:, xn_stride * c:xn_stride * c + L], ln["scr"][:, 0:L], b_bc, ALU.add)),
        ])
    return p.v("dve")


def emit_ln_apply_off(p, cm, ln, t0, L, Xn, dst_off, xn_stride):
    X = cm["X"]
    a_bc = p.bank(4)[:, 0:L]
    b_bc = p.bank(5)[:, 0:L]
    items = [
        (lambda e: e.matmul(a_bc, cm["ones_row"][:], ln["a_row"][0:1, t0:t0 + L], start=True, stop=True), 0),
        (lambda e: e.matmul(b_bc, cm["ones_row"][:], ln["b_row"][0:1, t0:t0 + L], start=True, stop=True), 1),
    ]
    vbc = p.multi("tensor", [("dve", p.v("dve"))], items)
    for c in range(FC):
        p.chain("vector", [("pe", vbc)] if c == 0 else [], [
            (lambda e, c=c: e.tensor_tensor(ln["scr"][:, 0:L], X[:, T * c + t0:T * c + t0 + L], a_bc, ALU.mult)),
            (lambda e, c=c: e.tensor_tensor(Xn[:, xn_stride * c + dst_off:xn_stride * c + dst_off + L],
                                            ln["scr"][:, 0:L], b_bc, ALU.add)),
        ])
    return p.v("dve")



class MoECfg:
    def __init__(self, I, experts, router_w, eps_router, scale):
        self.I = I
        self.IC = I // 128
        self.experts = experts
        self.router_w = router_w
        self.eps_router = eps_router
        self.scale = scale


def alloc_moe(p, cm, maxI):
    ms = {}
    ms["Xn"] = p.sb("moe_xn", [128, FC * 1024], BF16)
    ms["h"] = p.sb("moe_h", [128, (maxI // 128) * 1024], BF16)
    ms["w1t"] = [p.sb("moe_w1t0", [128, 8 * 256], BF16), p.sb("moe_w1t1", [128, 8 * 256], BF16)]
    ms["w3t"] = [p.sb("moe_w3t0", [128, 8 * 256], BF16), p.sb("moe_w3t1", [128, 8 * 256], BF16)]
    ms["w2t"] = [p.sb("moe_w2t0", [128, (maxI // 128) * 128], BF16), p.sb("moe_w2t1", [128, (maxI // 128) * 128], BF16)]
    ms["bc"] = p.sb("moe_bc", [128, 1024], F32)
    ms["scr512"] = [p.sb("moe_scr512a", [128, 512], BF16), p.sb("moe_scr512b", [128, 512], BF16)]
    ms["st"] = {"gi": 0, "gu_evac": {0: 0, 1: 0}, "si_read": [0, 0],
                "w13_pe": {0: 0, 1: 0}, "w2_pe": {0: 0, 1: 0}, "yi": 0,
                "y_bank_evac": {5: 0, 6: 0}, "pe_bc": 0, "dve_after_y": 0}
    ms["rw"] = p.sb("moe_rw", [128, 8 * 8], BF16)
    ms["sc_sb"] = p.sb("moe_scsb", [8, 1024], F32)
    ms["cw"] = p.sb("moe_cw", [128, 8], F32)
    ms["p_sb"] = p.sb("moe_p", [128, 8], F32)
    ms["t_sb"] = p.sb("moe_t", [128, 8], F32)
    ms["m_sb"] = p.sb("moe_m", [128, 4], F32)
    ms["ct"] = p.sb("moe_ct", [8, T], F32)
    ms["ctrow"] = p.sb("moe_ctrow", [1, 1024], F32)
    return ms


def emit_moe(p, cm, ln, ms, cfg: MoECfg):
    X = cm["X"]
    IC = cfg.IC
    Xn = ms["Xn"]
    h = ms["h"]
    nwin = (cfg.I + 255) // 256
    wsz = [min(256, cfg.I - 256 * w) for w in range(nwin)]
    ngrp = (IC + 3) // 4
    gsz = [min(4, IC - 4 * g) for g in range(ngrp)]

    if cfg.router_w is not None:
        rw3 = cfg.router_w.rearrange("(kc p) e -> p kc e", p=128)
        v_rw = p.step("gpsimd", [], lambda e: e.dma_start(out=ms["rw"][:].rearrange("p (kc e) -> p kc e", e=8), in_=rw3),
                      inc=16)

    for half in range(2):
        t0 = half * 1024
        emit_ln_apply(p, cm, ln, t0, 512, Xn, xn_stride=1024)
        emit_ln_apply_off(p, cm, ln, t0 + 512, 512, Xn, 512, xn_stride=1024)
        v_xn = p.v("dve")

        if cfg.router_w is not None:
            ps_sc = p.psum[0:8, 0:1024]
            items = []
            for w in range(2):
                for kc in range(FC):
                    items.append((lambda e, w=w, kc=kc: e.matmul(
                        ps_sc[:, 512 * w:512 * (w + 1)],
                        ms["rw"][:].rearrange("p (kc e) -> p kc e", e=8)[:, kc:kc + 1, :].squeeze(),
                        Xn[:, 1024 * kc + 512 * w:1024 * kc + 512 * (w + 1)],
                        start=(kc == 0), stop=(kc == FC - 1)), 1 if (kc == FC - 1 and w == 1) else 0))
            v_sc = p.multi("tensor", [("dve", v_xn), ("pool", v_rw)], items)
            v_ev = p.step("vector", [("pe", v_sc)], lambda e: e.tensor_copy(ms["sc_sb"][:], ps_sc), inc=1)
            for tt in range(8):
                vpe = p.step("tensor", [("dve", v_ev if tt == 0 else p.v("dve"))],
                             lambda e, tt=tt: e.transpose(p.bank(2)[0:128, 0:8], ms["sc_sb"][0:8, 128 * tt:128 * (tt + 1)],
                                                          cm["ident"][0:8, 0:8]),
                             inc=1)
                pb, tb, mb, cwb = ms["p_sb"], ms["t_sb"], ms["m_sb"], ms["cw"]
                mx, m1, m2, den = mb[:, 0:1], mb[:, 1:2], mb[:, 2:3], mb[:, 3:4]
                p.chain("vector", [("pe", vpe)], [
                    lambda e: e.tensor_copy(pb[:], p.bank(2)[0:128, 0:8]),
                    lambda e: e.tensor_reduce(mx, pb[:], AX.X, ALU.max),
                    lambda e: e.tensor_scalar(mx, mx, -1.0, None, ALU.mult, ALU.bypass),
                ])
                va = p.step("scalar", [("dve", p.v("dve"))],
                            lambda e: e.activation(tb[:], pb[:], AF.Exp, bias=mx, scale=1.0), inc=1)
                p.chain("vector", [("act", va)], [
                    lambda e: e.tensor_reduce(m1, tb[:], AX.X, ALU.add),
                    lambda e: e.reciprocal(m1, m1),
                    lambda e: e.tensor_scalar(pb[:], tb[:], m1, None, ALU.mult, ALU.bypass),
                    lambda e: e.tensor_reduce(m1, pb[:], AX.X, ALU.max),
                    lambda e: e.tensor_scalar(tb[:], pb[:], m1, None, ALU.is_equal, ALU.bypass),
                    lambda e: e.scalar_tensor_tensor(tb[:], tb[:], -1e30, pb[:], ALU.mult, ALU.add),
                    lambda e: e.tensor_reduce(m2, tb[:], AX.X, ALU.max),
                    lambda e: e.tensor_scalar(tb[:], pb[:], m2, None, ALU.is_ge, ALU.bypass),
                    lambda e: e.tensor_tensor(den, m1, m2, ALU.add),
                    lambda e: e.tensor_scalar(den, den, cfg.eps_router, None, ALU.add, ALU.bypass),
                    lambda e: e.reciprocal(den, den),
                    lambda e: e.tensor_scalar(den, den, cfg.scale, None, ALU.mult, ALU.bypass),
                    lambda e: e.scalar_tensor_tensor(cwb[:], pb[:], den, tb[:], ALU.mult, ALU.mult),
                ])
                vpe2 = p.step("tensor", [("dve", p.v("dve"))],
                              lambda e: e.transpose(p.bank(2)[0:8, 0:128], cwb[:], cm["ident"][:]), inc=1)
                p.step("vector", [("pe", vpe2)],
                       lambda e, tt=tt: e.tensor_copy(ms["ct"][0:8, t0 + 128 * tt:t0 + 128 * (tt + 1)],
                                                      p.bank(2)[0:8, 0:128]),
                       inc=1)
            dbg = ms.get("dbg", {})
            if half == 0 and "ct" in dbg:
                p.step("sync", [("dve", p.v("dve"))], lambda e: e.dma_start(out=dbg["ct"], in_=ms["ct"][:, 0:1024]), inc=16)
            if half == 0 and "xn" in dbg:
                p.step("sync", [("dve", p.v("dve"))], lambda e: e.dma_start(out=dbg["xn"], in_=Xn[:]), inc=16)

        st = ms["st"]
        st["dve_after_y"] = p.v("dve")
        for ei, ex in enumerate(cfg.experts):
            shared = ex.get("shared", False)
            if shared:
                v_bcd = p.step("vector", [], lambda e: e.memset(ms["bc"][:], cfg.scale), inc=1)
            else:
                v_ctr = p.step("sync", [("pe", st["pe_bc"]), ("dve", p.v("dve"))],
                               lambda e, ei=ei: e.dma_start(out=ms["ctrow"][:], in_=ms["ct"][ei:ei + 1, t0:t0 + 1024]),
                               inc=16)
                v_mm0 = p.step("tensor", [("sy", v_ctr), ("dve", st["dve_after_y"])],
                               lambda e: e.matmul(p.bank(4)[:, 0:512], cm["ones_row"][:],
                                                  ms["ctrow"][0:1, 0:512], start=True, stop=True), inc=1)
                v_e0 = p.step("vector", [("pe", v_mm0)],
                              lambda e: e.tensor_copy(ms["bc"][:, 0:512], p.bank(4)[:, 0:512]), inc=1)
                v_mm1 = p.step("tensor", [("dve", v_e0)],
                               lambda e: e.matmul(p.bank(4)[:, 0:512], cm["ones_row"][:],
                                                  ms["ctrow"][0:1, 512:1024], start=True, stop=True), inc=1)
                v_bcd = p.step("vector", [("pe", v_mm1)],
                               lambda e: e.tensor_copy(ms["bc"][:, 512:1024], p.bank(4)[:, 0:512]), inc=1)
            st["pe_bc"] = p.v("pe")

            def dma_win(w):
                buf = w % 2
                wait = [("pe", st["w13_pe"][buf])]
                src1 = ex["w1"][:, 256 * w:256 * w + wsz[w]].rearrange("(kc p) m -> p kc m", p=128)
                src3 = ex["w3"][:, 256 * w:256 * w + wsz[w]].rearrange("(kc p) m -> p kc m", p=128)
                d1 = ms["w1t"][buf][:, 0:8 * wsz[w]].rearrange("p (kc m) -> p kc m", kc=8)
                d3 = ms["w3t"][buf][:, 0:8 * wsz[w]].rearrange("p (kc m) -> p kc m", kc=8)
                return p.multi("gpsimd", wait, [
                    (lambda e: e.dma_start(out=d1, in_=src1), 16),
                    (lambda e: e.dma_start(out=d3, in_=src3), 16),
                ])
            dma_w13 = {0: dma_win(0)}
            if nwin > 1:
                dma_w13[1] = dma_win(1)
            gi0 = st["gi"]
            for w in range(nwin):
                buf = w % 2
                nmc = wsz[w] // 128
                for mcw in range(nmc):
                    mc = 2 * w + mcw
                    for ts in range(2):
                        gi = st["gi"]
                        bg = gi % 2
                        waits = [("pool", dma_w13[w]), ("dve", st["gu_evac"][bg])]
                        if gi == gi0:
                            waits.append(("dve", v_bcd))
                        items = []
                        for kc in range(FC):
                            items.append((lambda e, kc=kc, buf=buf, mcw=mcw, ts=ts, bg=bg, w=w: e.matmul(
                                p.bank(bg)[:, 0:512],
                                ms["w1t"][buf][:, 0:8 * wsz[w]].rearrange("p (kc m) -> p kc m", kc=8)[:, kc:kc + 1, 128 * mcw:128 * (mcw + 1)].squeeze(),
                                Xn[:, 1024 * kc + 512 * ts:1024 * kc + 512 * (ts + 1)],
                                start=(kc == 0), stop=(kc == FC - 1)), 0))
                        for kc in range(FC):
                            items.append((lambda e, kc=kc, buf=buf, mcw=mcw, ts=ts, bg=bg, w=w: e.matmul(
                                p.bank(2 + bg)[:, 0:512],
                                ms["w3t"][buf][:, 0:8 * wsz[w]].rearrange("p (kc m) -> p kc m", kc=8)[:, kc:kc + 1, 128 * mcw:128 * (mcw + 1)].squeeze(),
                                Xn[:, 1024 * kc + 512 * ts:1024 * kc + 512 * (ts + 1)],
                                start=(kc == 0), stop=(kc == FC - 1)), 1 if kc == FC - 1 else 0))
                        vmm = p.multi("tensor", waits, items)
                        sbuf_si = gi % 2
                        va = p.step("scalar", [("pe", vmm), ("dve", st["si_read"][sbuf_si])],
                                    lambda e, bg=bg, sbuf_si=sbuf_si: e.activation(
                                        ms["scr512"][sbuf_si][:], p.bank(bg)[:, 0:512], AF.Silu),
                                    inc=1)
                        st["gu_evac"][bg] = p.step("vector", [("act", va)],
                                                   lambda e, mc=mc, ts=ts, bg=bg, sbuf_si=sbuf_si: e.tensor_tensor(
                                                       h[:, 1024 * mc + 512 * ts:1024 * mc + 512 * (ts + 1)],
                                                       ms["scr512"][sbuf_si][:], p.bank(2 + bg)[:, 0:512], ALU.mult),
                                                   inc=1)
                        st["si_read"][sbuf_si] = st["gu_evac"][bg]
                        st["gi"] += 1
                st["w13_pe"][buf] = p.v("pe")
                if w + 2 < nwin:
                    dma_w13[w + 2] = dma_win(w + 2)
            v_h_done = p.v("dve")
            dbg = ms.get("dbg", {})
            if half == 0 and ei == 0 and "h" in dbg:
                p.step("sync", [("dve", v_h_done)], lambda e: e.dma_start(out=dbg["h"], in_=h[:, 0:IC * 1024]), inc=16)
            if half == 0 and ei == 0 and "bc" in dbg:
                p.step("sync", [("dve", v_h_done)], lambda e: e.dma_start(out=dbg["bc"], in_=ms["bc"][:]), inc=16)

            def dma_hc(hc):
                buf = hc % 2
                src = ex["w2r"][hc:hc + 1].squeeze()
                dd = ms["w2t"][buf][:, 0:IC * 128].rearrange("p (kc m) -> p kc m", m=128)
                return p.step("gpsimd", [("pe", st["w2_pe"][buf])],
                              lambda e: e.dma_start(out=dd, in_=src), inc=16)
            dma_w2 = {0: dma_hc(0), 1: dma_hc(1)}
            yi = st["yi"]
            for hc in range(FC):
                buf = hc % 2
                for ts in range(2):
                    yb = 5 + (yi % 2)
                    waits = [("pool", dma_w2[hc]), ("dve", st["y_bank_evac"][yb])]
                    if hc == 0 and ts == 0:
                        waits.append(("dve", v_h_done))
                    items = []
                    for kc in range(IC):
                        items.append((lambda e, kc=kc, hc=hc, buf=buf, ts=ts, yb=yb: e.matmul(
                            p.bank(yb)[:, 0:512],
                            ms["w2t"][buf][:, 0:IC * 128].rearrange("p (kc m) -> p kc m", m=128)[:, kc:kc + 1, :].squeeze(),
                            h[:, 1024 * kc + 512 * ts:1024 * kc + 512 * (ts + 1)],
                            start=(kc == 0), stop=(kc == IC - 1)),
                            1 if kc == IC - 1 else 0))
                    v_y = p.multi("tensor", waits, items)
                    st["y_bank_evac"][yb] = p.multi("vector", [("pe", v_y)], [
                        (lambda e, hc=hc, ts=ts, yb=yb: e.tensor_tensor(ln["scr"][:, 0:512], p.bank(yb)[:, 0:512],
                                                                        ms["bc"][:, 512 * ts:512 * (ts + 1)], ALU.mult), 0),
                        (lambda e, hc=hc, ts=ts: e.tensor_tensor(X[:, T * hc + t0 + 512 * ts:T * hc + t0 + 512 * (ts + 1)],
                                                                 X[:, T * hc + t0 + 512 * ts:T * hc + t0 + 512 * (ts + 1)],
                                                                 ln["scr"][:, 0:512], ALU.add), 1),
                    ])
                    yi += 1
                st["w2_pe"][buf] = p.v("pe")
                if hc + 2 < FC:
                    dma_w2[hc + 2] = dma_hc(hc + 2)
            st["yi"] = yi
            st["dve_after_y"] = p.v("dve")
    return



SEG = 256
NCH = SEG // 32
GLA_STAGE = 4
GLA_SUB = {'A', 'O', 'S'}


def alloc_gla(p, cm):
    gs = {}
    for n in ["bA", "bB", "bC", "bD", "bE", "bF", "v32"]:
        gs[n] = p.sb("gla_" + n, [128, SEG * FC], F32)
    for n in ["qt", "ub", "G", "q32"]:
        gs[n] = p.sb("gla_" + n, [128, SEG * FC], BF16)
    gs["Xn"] = p.sb("gla_xn", [128, SEG * FC], BF16)
    gs["h"] = p.sb("gla_h", [128, SEG * FC], BF16)
    gs["vtm"] = p.sb("gla_vtm", [64, 2 * 1024], BF16)
    gs["ktm"] = p.sb("gla_ktm", [64, 2 * 1024], BF16)
    gs["Asb"] = p.sb("gla_asb", [64, 512], BF16)
    gs["hstg"] = p.sb("gla_hstg", [128, 256], F32)
    gs["i64"] = p.sb("gla_i64", [128, 64], F32)
    gs["S"] = p.sb("gla_S", [128, 512], F32)
    gs["Sbf"] = p.sb("gla_Sbf", [128, 512], BF16)
    gs["D32"] = p.sb("gla_D32", [128, 64], F32)
    gs["m16"] = p.sb("gla_m16", [128, SEG], BF16)
    gs["m32"] = p.sb("gla_m32", [128, SEG], BF16)
    gs["amask"] = p.sb("gla_amask", [32, 512], F32)
    gs["kb"] = p.sb("gla_kb", [128, 8], F32)
    gs["ob"] = p.sb("gla_ob", [128, 8], F32)
    gs["lneps"] = p.sb("gla_lneps", [128, 1], F32)
    gs["wbuf"] = [p.sb("gla_wb0", [128, 8 * 512], BF16), p.sb("gla_wb1", [128, 8 * 512], BF16)]
    return gs


def emit_gla(p, cm, ln, gs, W):
    X = cm["X"]
    p.multi("sync", [], [
        (lambda e: e.dma_start(out=gs["amask"][:], in_=W["c_amask"]), 16),
        (lambda e: e.dma_start(out=gs["kb"][:], in_=W["key_b_r"]), 16),
        (lambda e: e.dma_start(out=gs["ob"][:], in_=W["out_b_r"]), 16),
        (lambda e: e.dma_start(out=gs["i64"][:], in_=W["c_ident64"]), 16),
    ])
    p.multi("gpsimd", [], [
        (lambda e: e.dma_start(out=gs["m16"][:], in_=W["c_mask16"][:, 0:SEG]), 16),
        (lambda e: e.dma_start(out=gs["m32"][:], in_=W["c_mask32"][:, 0:SEG]), 16),
    ])
    init_sy = p.v("sy")
    init_pool = p.v("pool")
    p.multi("vector", [("dve", p.v("dve")), ("pe", p.v("pe"))], [
        (lambda e: e.memset(gs["S"][:], 0.0), 0),
        (lambda e: e.memset(gs["Sbf"][:], 0.0), 0),
        (lambda e: e.memset(gs["lneps"][:], 1e-35), 0),
        (lambda e: e.memset(p.psum[0:32, 0:512], 0.0), 0),
        (lambda e: e.memset(gs["vtm"][32:64, :], 0.0), 0),
        (lambda e: e.memset(gs["ktm"][32:64, :], 0.0), 0),
        (lambda e: e.memset(gs["Asb"][32:64, :], 0.0), 1),
    ])
    init_dve = p.v("dve")
    st = {"wb_pe": [0, 0], "A_evac": 0, "O_evac": 0, "S_evac": 0, "Sbf_cast": 0,
          "tr_evac": {4: 0, 5: 0, 6: 0, 7: 0}, "proj_evac": [0, 0], "proj_act": [0, 0]}

    def dma_w(src_ap, mh, buf, wait_pe):
        src = src_ap[:, 512 * mh:512 * (mh + 1)].rearrange("(kc p) m -> p kc m", p=128)
        dd = gs["wbuf"][buf][:].rearrange("p (kc m) -> p kc m", m=512)
        return p.step("gpsimd", [("pe", wait_pe)], lambda e: e.dma_start(out=dd, in_=src), inc=16)

    for seg in range(FC):
        t0 = SEG * seg
        emit_ln_apply(p, cm, ln, t0, SEG, gs["Xn"])
        v_xn = p.v("dve")

        d0 = dma_w(W["key_wT"], 0, 0, st["wb_pe"][0])
        d1 = dma_w(W["key_wT"], 1, 1, st["wb_pe"][1])
        dmav = {0: d0, 1: d1}

        for mc in range(FC):
            buf = mc // 4
            bk = mc % 2
            waits = [("pool", dmav[buf]), ("act", st["proj_act"][bk]), ("dve", v_xn), ("sy", init_sy),
                     ("pool", init_pool)]
            items = []
            for kc in range(FC):
                items.append((lambda e, kc=kc, buf=buf, mc=mc, bk=bk: e.matmul(
                    p.bank(bk)[:, 0:SEG],
                    gs["wbuf"][buf][:].rearrange("p (kc m) -> p kc m", m=512)[:, kc:kc + 1, 128 * (mc % 4):128 * (mc % 4 + 1)].squeeze(),
                    gs["Xn"][:, SEG * kc:SEG * (kc + 1)],
                    start=(kc == 0), stop=(kc == FC - 1)), 1 if kc == FC - 1 else 0))
            vmm = p.multi("tensor", waits, items)
            st["proj_act"][bk] = p.step("scalar", [("pe", vmm)],
                                        lambda e, mc=mc, bk=bk: e.activation(gs["bA"][:, SEG * mc:SEG * (mc + 1)],
                                                                             p.bank(bk)[:, 0:SEG], AF.Sigmoid,
                                                                             bias=gs["kb"][:, mc:mc + 1], scale=1.0),
                                        inc=1)
        for half4 in range(4):
            buf = half4 % 2
            wait_pe_users = p.v("pe")
            dmav[buf] = dma_w(W["vq_wT"], half4, buf, wait_pe_users)
            for sub in range(4):
                mc = half4 * 4 + sub
                bk = mc % 2
                dst = gs["bE"] if mc < 8 else gs["bF"]
                mco = mc % 8
                waits = [("pool", dmav[buf]), ("act", st["proj_act"][bk])]
                items = []
                for kc in range(FC):
                    items.append((lambda e, kc=kc, buf=buf, sub=sub, bk=bk: e.matmul(
                        p.bank(bk)[:, 0:SEG],
                        gs["wbuf"][buf][:].rearrange("p (kc m) -> p kc m", m=512)[:, kc:kc + 1, 128 * sub:128 * (sub + 1)].squeeze(),
                        gs["Xn"][:, SEG * kc:SEG * (kc + 1)],
                        start=(kc == 0), stop=(kc == FC - 1)), 1 if kc == FC - 1 else 0))
                vmm = p.multi("tensor", waits, items)
                st["proj_act"][bk] = p.step("scalar", [("pe", vmm)],
                                            lambda e, dst=dst, mco=mco, bk=bk: e.activation(
                                                dst[:, SEG * mco:SEG * (mco + 1)], p.bank(bk)[:, 0:SEG], AF.Copy),
                                            inc=1)
        v_proj = p.v("act")
        if GLA_STAGE < 1:
            continue

        k_, u_, es, cl32 = gs["bA"], gs["bB"], gs["bC"], gs["bD"]
        vv, q_ = gs["bE"], gs["bF"]
        dD, k32 = gs["bF"], gs["v32"]
        v1 = p.step("vector", [("act", v_proj)],
                    lambda e: e.tensor_scalar(u_[:], k_[:], -1.0, 1.0, ALU.mult, ALU.add), inc=1)
        v2 = p.step("scalar", [("dve", v1)],
                    lambda e: e.activation(es[:], k_[:], AF.Ln, bias=gs["lneps"][:], scale=1.0), inc=1)
        fns = []
        for fc in range(FC):
            fns.append(lambda e, fc=fc: e.tensor_tensor_scan(
                k_[:, SEG * fc:SEG * (fc + 1)],
                gs["m16"][:], es[:, SEG * fc:SEG * (fc + 1)], 0.0, ALU.mult, ALU.add))
        for fc in range(FC):
            fns.append(lambda e, fc=fc: e.tensor_tensor_scan(
                cl32[:, SEG * fc:SEG * (fc + 1)],
                gs["m32"][:], es[:, SEG * fc:SEG * (fc + 1)], 0.0, ALU.mult, ALU.add))
        v3 = p.chain("vector", [("act", v2)], fns)
        cl16 = k_
        v4 = p.step("scalar", [("dve", v3)], lambda e: e.activation(es[:], cl16[:], AF.Exp), inc=1)
        v5 = p.step("vector", [("act", v4)], lambda e: e.tensor_tensor(gs["qt"][:], q_[:], es[:], ALU.mult), inc=1)
        v6 = p.step("scalar", [("dve", v5)], lambda e: e.activation(es[:], cl32[:], AF.Exp), inc=1)
        v7 = p.step("vector", [("act", v6)], lambda e: e.tensor_tensor(gs["q32"][:], q_[:], es[:], ALU.mult), inc=1)
        cl32e = cl32[:].rearrange("p (a s) -> p a s", s=32)[:, :, 31]
        v8 = p.step("scalar", [("dve", v3)], lambda e: e.activation(gs["D32"][:], cl32e, AF.Exp), inc=1)
        v9 = p.step("scalar", [("dve", v7)], lambda e: e.activation(es[:], cl16[:], AF.Exp, scale=-1.0), inc=1)
        v10 = p.chain("vector", [("act", v9)], [
            lambda e: e.tensor_tensor(gs["ub"][:], u_[:], es[:], ALU.mult),
            lambda e: e.tensor_copy(gs["G"][:].rearrange("p (a s) -> p a s", s=32)[:, :, 16:32],
                                    gs["ub"][:].rearrange("p (a s) -> p a s", s=32)[:, :, 16:32]),
        ])
        cl16_blk = cl16[:].rearrange("p (a s) -> p a s", s=16)
        bcast16 = cl16_blk[:, :, 15].unsqueeze(2).broadcast_to((128, SEG * FC // 16, 16))
        v11 = p.step("vector", [("dve", v10)],
                     lambda e: e.tensor_tensor(dD[:].rearrange("p (a s) -> p a s", s=16), bcast16, cl16_blk,
                                               ALU.subtract), inc=1)
        v12 = p.step("scalar", [("dve", v11)], lambda e: e.activation(es[:], dD[:], AF.Exp), inc=1)
        v13 = p.step("vector", [("act", v12)],
                     lambda e: e.tensor_tensor(gs["G"][:].rearrange("p (a s) -> p a s", s=32)[:, :, 0:16],
                                               u_[:].rearrange("p (a s) -> p a s", s=32)[:, :, 0:16],
                                               es[:].rearrange("p (a s) -> p a s", s=32)[:, :, 0:16],
                                               ALU.mult), inc=1)
        cl32_blk = cl32[:].rearrange("p (a s) -> p a s", s=32)
        bcast32 = cl32_blk[:, :, 31].unsqueeze(2).broadcast_to((128, SEG * FC // 32, 32))
        v14 = p.step("vector", [("dve", v13)],
                     lambda e: e.tensor_tensor(dD[:].rearrange("p (a s) -> p a s", s=32), bcast32, cl32_blk,
                                               ALU.subtract), inc=1)
        v15 = p.step("scalar", [("dve", v14)], lambda e: e.activation(es[:], dD[:], AF.Exp), inc=1)
        v16 = p.step("vector", [("act", v15)], lambda e: e.tensor_tensor(k32[:], u_[:], es[:], ALU.mult), inc=1)
        v_prep = p.v("dve")
        if GLA_STAGE < 2:
            continue

        for c in range(NCH):
            sl = c % 2
            so = 1024 * sl
            for bgrp, r0 in ((0, 0), (1, 64)):
                items = []
                for (tsrc, pb) in ((vv, 4 + bgrp), (k32, 6 + bgrp)):
                    for fc in range(FC):
                        items.append((lambda e, fc=fc, c=c, tsrc=tsrc, pb=pb, r0=r0: e.transpose(
                            p.psum[0:32, 512 * pb + 64 * fc:512 * pb + 64 * (fc + 1)],
                            tsrc[r0:r0 + 64, SEG * fc + 32 * c:SEG * fc + 32 * (c + 1)],
                            gs["i64"][r0:r0 + 64, 0:64],
                            tile_position=(r0, 0)),
                            1 if (tsrc is k32 and fc == FC - 1) else 0))
                w = [("dve", st["tr_evac"][4 + bgrp]), ("pe", p.v("pe"))]
                if c == 0:
                    w.append(("dve", v_prep))
                vt = p.multi("tensor", w, items)
                ev_items = []
                for (packs, pb) in (("vtm", 4 + bgrp), ("ktm", 6 + bgrp)):
                    ev_items.append((lambda e, so=so, pb=pb, packs=packs, bgrp=bgrp:
                                     e.tensor_copy(gs[packs][0:32, so + 512 * bgrp:so + 512 * (bgrp + 1)],
                                                   p.psum[0:32, 512 * pb:512 * (pb + 1)]), 1))
                st["tr_evac"][4 + bgrp] = p.multi("vector", [("pe", vt), ("pe", st.setdefault("slot_pe", [0, 0])[sl])],
                                                  ev_items)
            v_packs = p.v("dve")
            vtm = gs["vtm"]
            ktm = gs["ktm"]
            if "A" not in GLA_SUB:
                continue
            vA = 0
            for hw in range(2):
                par = 64 * hw
                co = 256 * hw
                items = []
                for hh in range(8):
                    fc = hh
                    lo = SEG * fc + 32 * c
                    items.append((lambda e, hh=hh, par=par, lo=lo, co=co: e.matmul(
                        p.psum[0:16, co + 32 * hh:co + 32 * hh + 16],
                        gs["ub"][par:par + 64, lo:lo + 16],
                        gs["qt"][par:par + 64, lo:lo + 16], start=True, stop=True,
                        tile_position=(par, 0)), 0))
                    items.append((lambda e, hh=hh, par=par, lo=lo, co=co: e.matmul(
                        p.psum[0:32, co + 32 * hh + 16:co + 32 * hh + 32],
                        gs["G"][par:par + 64, lo:lo + 32],
                        gs["qt"][par:par + 64, lo + 16:lo + 32], start=True, stop=True,
                        tile_position=(par, 0)),
                        1 if hh == 7 else 0))
                w = [("pe", p.v("pe"))]
                if hw == 0:
                    w += [("dve", st["A_evac"] if st["A_evac"] else init_dve), ("dve", v_packs)]
                vA = p.multi("tensor", w, items)
            st["A_evac"] = p.step("vector", [("pe", vA)],
                                  lambda e: e.tensor_tensor(gs["Asb"][0:32, :], p.psum[0:32, 0:512],
                                                            gs["amask"][:], ALU.mult), inc=1)
            if "O" not in GLA_SUB:
                continue
            items = []
            for hh in range(8):
                lo = SEG * hh + 32 * c
                items.append((lambda e, hh=hh, so=so: e.matmul(
                    p.bank(2)[0:64, 32 * hh:32 * (hh + 1)],
                    vtm[0:64, so + 64 * hh:so + 64 * (hh + 1)],
                    gs["Asb"][0:64, 32 * hh:32 * (hh + 1)], start=True, stop=False,
                    tile_position=(0, 0)), 0))
                items.append((lambda e, hh=hh, lo=lo: e.matmul(
                    p.bank(2)[0:64, 32 * hh:32 * (hh + 1)],
                    gs["Sbf"][0:64, 64 * hh:64 * (hh + 1)],
                    gs["q32"][0:64, lo:lo + 32], start=False, stop=True,
                    tile_position=(0, 0)),
                    0))
            for hh in range(8):
                items.append((lambda e, hh=hh, so=so: e.matmul(
                    p.bank(2)[64:128, 32 * hh:32 * (hh + 1)],
                    vtm[0:64, so + 512 + 64 * hh:so + 512 + 64 * (hh + 1)],
                    gs["Asb"][0:64, 256 + 32 * hh:256 + 32 * (hh + 1)], start=True, stop=True,
                    tile_position=(0, 64)), 1))
            w = [("dve", st["A_evac"]), ("dve", st["O_evac"]), ("dve", st["Sbf_cast"]), ("pe", p.v("pe"))]
            vO1 = p.multi("tensor", w, items)
            items = []
            for hh in range(8):
                lo = SEG * hh + 32 * c
                items.append((lambda e, hh=hh, lo=lo: e.matmul(
                    p.bank(2)[64:128, 256 + 32 * hh:256 + 32 * (hh + 1)],
                    gs["Sbf"][64:128, 64 * hh:64 * (hh + 1)],
                    gs["q32"][64:128, lo:lo + 32], start=True, stop=True,
                    tile_position=(64, 64)),
                    1 if hh == 7 else 0))
            vO = p.multi("tensor", [("pe", vO1)], items)
            st["O_evac"] = p.chain("vector", [("pe", vO)], [
                lambda e, c=c: e.tensor_copy(
                    gs["h"][0:64, :].rearrange("p (a s) -> p a s", s=SEG)[:, :, 32 * c:32 * (c + 1)],
                    p.bank(2)[0:64, 0:256].rearrange("p (a s) -> p a s", s=32)),
                lambda e: e.tensor_copy(gs["hstg"][64:128, :], p.bank(2)[64:128, 0:256]),
                lambda e, c=c: e.tensor_tensor(
                    gs["h"][64:128, :].rearrange("p (a s) -> p a s", s=SEG)[:, :, 32 * c:32 * (c + 1)],
                    gs["hstg"][64:128, :].rearrange("p (a s) -> p a s", s=32),
                    p.bank(2)[64:128, 256:512].rearrange("p (a s) -> p a s", s=32), ALU.add),
            ])
            if "S" not in GLA_SUB:
                continue
            items = []
            for hw in range(2):
                for hh in range(8):
                    items.append((lambda e, hh=hh, hw=hw, so=so: e.matmul(
                        p.bank(3)[64 * hw:64 * hw + 64, 64 * hh:64 * (hh + 1)],
                        ktm[0:64, so + 512 * hw + 64 * hh:so + 512 * hw + 64 * (hh + 1)],
                        vtm[0:64, so + 512 * hw + 64 * hh:so + 512 * hw + 64 * (hh + 1)],
                        start=True, stop=True, tile_position=(0, 64 * hw)),
                        1 if (hw == 1 and hh == 7) else 0))
            vS = p.multi("tensor", [("dve", st["S_evac"]), ("pe", p.v("pe"))], items)
            fns = []
            for hw in range(2):
                par = 64 * hw
                for hh in range(8):
                    fns.append(lambda e, hh=hh, par=par, c=c: e.scalar_tensor_tensor(
                        gs["S"][par:par + 64, 64 * hh:64 * (hh + 1)], gs["S"][par:par + 64, 64 * hh:64 * (hh + 1)],
                        gs["D32"][par:par + 64, 8 * hh + c:8 * hh + c + 1],
                        p.bank(3)[par:par + 64, 64 * hh:64 * (hh + 1)],
                        ALU.mult, ALU.add))
            fns.append(lambda e: e.tensor_copy(gs["Sbf"][:], gs["S"][:]))
            st["S_evac"] = p.chain("vector", [("pe", vS)], fns)
            st["Sbf_cast"] = st["S_evac"]
            st["slot_pe"][sl] = p.v("pe")

        v_h = p.v("dve")
        if GLA_STAGE < 3:
            continue
        d0 = dma_w(W["out_wT"], 0, 0, p.v("pe"))
        d1 = dma_w(W["out_wT"], 1, 1, p.v("pe"))
        dmav = {0: d0, 1: d1}
        for mc in range(FC):
            buf = mc // 4
            bk = mc % 2
            waits = [("pool", dmav[buf]), ("dve", st["proj_evac"][bk]), ("act", st["proj_act"][bk])]
            if mc == 0:
                waits.append(("dve", v_h))
            items = []
            for kc in range(FC):
                items.append((lambda e, kc=kc, buf=buf, mc=mc, bk=bk: e.matmul(
                    p.bank(bk)[:, 0:SEG],
                    gs["wbuf"][buf][:].rearrange("p (kc m) -> p kc m", m=512)[:, kc:kc + 1, 128 * (mc % 4):128 * (mc % 4 + 1)].squeeze(),
                    gs["h"][:, SEG * kc:SEG * (kc + 1)],
                    start=(kc == 0), stop=(kc == FC - 1)), 1 if kc == FC - 1 else 0))
            vmm = p.multi("tensor", waits, items)
            st["proj_evac"][bk] = p.step("vector", [("pe", vmm)],
                                         lambda e, mc=mc, bk=bk, t0=t0: e.scalar_tensor_tensor(
                                             X[:, T * mc + t0:T * mc + t0 + SEG],
                                             p.bank(bk)[:, 0:SEG], gs["ob"][:, mc:mc + 1],
                                             X[:, T * mc + t0:T * mc + t0 + SEG], ALU.add, ALU.add), inc=1)
            st["proj_act"][bk] = st["proj_act"][bk]
        st["wb_pe"] = [p.v("pe"), p.v("pe")]
    return



SEG1 = 512
LAMBDA_INIT = 0.8 - 0.6 * float(np.exp(-0.3 * 12))


def alloc_attn1(p, cm):
    at = {}
    at["Xn"] = p.sb("at_xn", [128, FC * SEG1], BF16)
    at["q"] = p.sb("at_q", [128, 4 * D], F32)
    at["k"] = p.sb("at_k", [128, 4 * D], F32)
    at["v"] = p.sb("at_v", [128, 4 * D], BF16)
    at["attn"] = p.sb("at_attn", [128, 4 * D], F32)
    at["af"] = p.sb("at_af", [128, FC * SEG1], BF16)
    at["scr"] = p.sb("at_scr", [128, D], F32)
    at["scr2"] = p.sb("at_scr2", [128, D], F32)
    at["scr3"] = p.sb("at_scr3", [128, D], F32)
    at["sc"] = p.sb("at_sc", [128, 128], F32)
    at["mx"] = p.sb("at_mx", [128, 16], F32)
    at["rstd"] = p.sb("at_rstd", [128, 8], F32)
    at["wm"] = p.sb("at_wm", [128, 64], F32)
    at["cs_cos"] = p.sb("at_cos", [128, 4 * 64], F32)
    at["cs_sin"] = p.sb("at_sin", [128, 4 * 64], F32)
    at["lnw"] = p.sb("at_lnw", [128, 128], F32)
    at["lnb"] = p.sb("at_lnb", [128, 128], F32)
    at["lam"] = p.sb("at_lam", [1, 256], F32)
    at["lams"] = p.sb("at_lams", [1, 128], F32)
    at["neglam"] = p.sb("at_neglam", [128, 1], F32)
    at["eps8"] = p.sb("at_eps8", [128, 1], F32)
    at["wbuf"] = [p.sb("at_wb0", [128, 8 * 512], BF16), p.sb("at_wb1", [128, 8 * 512], BF16)]
    return at


def emit_attn1(p, cm, ln, at, W):
    X = cm["X"]
    HDQ = float(64.0 ** -0.25)
    p.multi("sync", [], [
        (lambda e: e.dma_start(out=at["lam"][0:1, 0:64], in_=W["lambda_q1"].unsqueeze(0)), 16),
        (lambda e: e.dma_start(out=at["lam"][0:1, 64:128], in_=W["lambda_k1"].unsqueeze(0)), 16),
        (lambda e: e.dma_start(out=at["lam"][0:1, 128:192], in_=W["lambda_q2"].unsqueeze(0)), 16),
        (lambda e: e.dma_start(out=at["lam"][0:1, 192:256], in_=W["lambda_k2"].unsqueeze(0)), 16),
        (lambda e: e.dma_start(out=at["lnw"][:], in_=W["ln_w_rep"]), 16),
        (lambda e: e.dma_start(out=at["lnb"][:], in_=W["ln_b_rep"]), 16),
    ])
    v_ld = p.v("sy")
    vx = p.chain("vector", [("sy", v_ld), ("pe", p.v("pe")), ("dve", p.v("dve"))], [
        lambda e: e.memset(at["eps8"][:], 1e-8),
        lambda e: e.tensor_tensor(at["lams"][0:1, 0:64], at["lam"][0:1, 0:64], at["lam"][0:1, 64:128], ALU.mult),
        lambda e: e.tensor_tensor(at["lams"][0:1, 64:128], at["lam"][0:1, 128:192], at["lam"][0:1, 192:256], ALU.mult),
        lambda e: e.tensor_reduce(at["lams"][0:1, 0:2], at["lams"][0:1, 0:128].rearrange("p (a d) -> p a d", d=64),
                                  AX.X, ALU.add),
    ])
    va = p.step("scalar", [("dve", vx)], lambda e: e.activation(at["lams"][0:1, 4:6], at["lams"][0:1, 0:2], AF.Exp),
                inc=1)
    vx = p.chain("vector", [("act", va)], [
        lambda e: e.tensor_tensor(at["lams"][0:1, 8:9], at["lams"][0:1, 4:5], at["lams"][0:1, 5:6], ALU.subtract),
        lambda e: e.tensor_scalar(at["lams"][0:1, 8:9], at["lams"][0:1, 8:9], LAMBDA_INIT, -1.0,
                                  ALU.add, ALU.mult),
        lambda e: e.tensor_scalar(at["lnw"][:], at["lnw"][:], 1.0 - LAMBDA_INIT, None, ALU.mult, ALU.bypass),
        lambda e: e.tensor_scalar(at["lnb"][:], at["lnb"][:], 1.0 - LAMBDA_INIT, None, ALU.mult, ALU.bypass),
    ])
    vneg = p.step("tensor", [("dve", vx)],
                  lambda e: e.matmul(p.bank(2)[:, 0:1], cm["ones_row"][:], at["lams"][0:1, 8:9],
                                     start=True, stop=True), inc=1)
    v_lam = p.step("vector", [("pe", vneg)], lambda e: e.tensor_copy(at["neglam"][:], p.bank(2)[:, 0:1]), inc=1)

    st = {"bk_act": [0, 0], "bk_dve": [0, 0], "wb_pe": [0, 0], "tr_dve": {6: 0, 7: 0}}

    def dma_w(src_ap, mh, buf):
        src = src_ap[:, 512 * mh:512 * (mh + 1)].rearrange("(kc p) m -> p kc m", p=128)
        dd = at["wbuf"][buf][:].rearrange("p (kc m) -> p kc m", m=512)
        return p.step("gpsimd", [("pe", st["wb_pe"][buf])], lambda e: e.dma_start(out=dd, in_=src), inc=16)

    for seg in range(4):
        t0 = SEG1 * seg
        emit_ln_apply(p, cm, ln, t0, SEG1, at["Xn"])
        v_xn = p.v("dve")
        v_cs = p.multi("gpsimd", [("dve", st["tr_dve"][6])], [
            (lambda e: e.dma_start(out=at["cs_cos"][:].rearrange("p (a d) -> p a d", d=64),
                                   in_=W["c_cos"][t0:t0 + SEG1, :].rearrange("(a p) d -> p a d", p=128)), 16),
            (lambda e: e.dma_start(out=at["cs_sin"][:].rearrange("p (a d) -> p a d", d=64),
                                   in_=W["c_sin"][t0:t0 + SEG1, :].rearrange("(a p) d -> p a d", p=128)), 16),
        ])
        for mi, (wd, dst, dt_) in enumerate(((W["Wq"], at["q"], F32), (W["Wk"], at["k"], F32),
                                             (W["Wv"], at["v"], BF16))):
            d0 = dma_w(wd, 0, 0)
            d1 = dma_w(wd, 1, 1)
            dmav = {0: d0, 1: d1}
            for tt in range(4):
                for win in range(2):
                    bk = (tt * 2 + win) % 2
                    waits = [("pool", dmav[win]), ("dve", st["bk_dve"][bk]), ("act", st["bk_act"][bk])]
                    if mi == 0 and tt == 0 and win == 0:
                        waits.append(("dve", v_xn))
                    items = []
                    for kc in range(FC):
                        items.append((lambda e, kc=kc, tt=tt, win=win, bk=bk: e.matmul(
                            p.bank(bk)[:, 0:512],
                            at["Xn"][:, SEG1 * kc + 128 * tt:SEG1 * kc + 128 * (tt + 1)],
                            at["wbuf"][win][:].rearrange("p (kc m) -> p kc m", m=512)[:, kc:kc + 1, :].squeeze(),
                            start=(kc == 0), stop=(kc == FC - 1)), 1 if kc == FC - 1 else 0))
                    vmm = p.multi("tensor", waits, items)
                    st["bk_dve"][bk] = p.step("vector", [("pe", vmm)],
                                              lambda e, tt=tt, win=win, bk=bk, dst=dst: e.tensor_copy(
                                                  dst[:, D * tt + 512 * win:D * tt + 512 * (win + 1)],
                                                  p.bank(bk)[:, 0:512]), inc=1)
            st["wb_pe"] = [p.v("pe"), p.v("pe")]
        v_qkv = p.v("dve")

        for tt in range(4):
            q_t = at["q"][:, D * tt:D * (tt + 1)]
            k_t = at["k"][:, D * tt:D * (tt + 1)]
            v_t = at["v"][:, D * tt:D * (tt + 1)]
            a_t = at["attn"][:, D * tt:D * (tt + 1)]
            cosAP = at["cs_cos"][:, 64 * tt:64 * (tt + 1)].unsqueeze(1).broadcast_to((128, 16, 64))
            sinAP = at["cs_sin"][:, 64 * tt:64 * (tt + 1)].unsqueeze(1).broadcast_to((128, 16, 64))
            for rt in (q_t, k_t):
                rview = rt.rearrange("p (a two) -> p a two", two=2)
                p.chain("vector", [("dve", p.v("dve")), ("sy", p.v("sy")), ("pool", p.v("pool"))], [
                    lambda e, rt=rt: e.tensor_scalar(at["scr"][:].rearrange("p (a two) -> p a two", two=2)[:, :, 0:1],
                                                     rt.rearrange("p (a two) -> p a two", two=2)[:, :, 1:2],
                                                     -1.0, None, ALU.mult, ALU.bypass),
                    lambda e, rt=rt: e.tensor_copy(at["scr"][:].rearrange("p (a two) -> p a two", two=2)[:, :, 1:2],
                                                   rt.rearrange("p (a two) -> p a two", two=2)[:, :, 0:1]),
                    lambda e, rt=rt: e.tensor_tensor(at["scr2"][:].rearrange("p (hh d) -> p hh d", d=64),
                                                     rt.rearrange("p (hh d) -> p hh d", d=64), cosAP, ALU.mult),
                    lambda e: e.tensor_tensor(at["scr"][:].rearrange("p (hh d) -> p hh d", d=64),
                                              at["scr"][:].rearrange("p (hh d) -> p hh d", d=64), sinAP, ALU.mult),
                    lambda e, rt=rt: e.tensor_tensor(rt, at["scr2"][:], at["scr"][:], ALU.add),
                ])
            items = []
            for g in range(2):
                for n in range(8):
                    for m in range(8):
                        col = 64 * g + 8 * n + m
                        qs = q_t[:, 64 * (2 * n + g):64 * (2 * n + g + 1)]
                        ks = k_t[:, 64 * (2 * m + g):64 * (2 * m + g + 1)]
                        items.append((lambda e, qs=qs, ks=ks, col=col: e.scalar_tensor_tensor(
                            at["scr"][:, 0:64], qs, HDQ, ks, ALU.mult, ALU.mult,
                            accum_out=at["sc"][:, col:col + 1]), 1))
            p.multi("vector", [("dve", p.v("dve"))], items)
            scv = at["sc"][:].rearrange("p (gn m) -> p gn m", m=8)
            vsm = p.chain("vector", [], [
                lambda e: e.tensor_reduce(at["mx"][:], scv, AX.X, ALU.max),
                lambda e: e.tensor_scalar(at["mx"][:], at["mx"][:], -1.0, None, ALU.mult, ALU.bypass),
                lambda e: e.tensor_tensor(scv, scv, at["mx"][:].unsqueeze(2).broadcast_to((128, 16, 8)), ALU.add),
            ])
            vexp = p.step("scalar", [("dve", vsm)], lambda e: e.activation(at["sc"][:], at["sc"][:], AF.Exp), inc=1)
            vsm = p.chain("vector", [("act", vexp)], [
                lambda e: e.tensor_reduce(at["mx"][:], scv, AX.X, ALU.add),
                lambda e: e.reciprocal(at["mx"][:], at["mx"][:]),
                lambda e: e.tensor_tensor(scv, scv, at["mx"][:].unsqueeze(2).broadcast_to((128, 16, 8)), ALU.mult),
                lambda e: e.scalar_tensor_tensor(at["wm"][:], at["sc"][:, 64:128], at["neglam"][:, 0:1],
                                                 at["sc"][:, 0:64], ALU.mult, ALU.add),
            ])
            for n in range(8):
                fns = [(lambda e, n=n: e.tensor_scalar(a_t[:, 128 * n:128 * (n + 1)],
                                                       v_t[:, 0:128], at["wm"][:, 8 * n:8 * n + 1],
                                                       None, ALU.mult, ALU.bypass))]
                for m in range(1, 8):
                    fns.append(lambda e, n=n, m=m: e.scalar_tensor_tensor(
                        a_t[:, 128 * n:128 * (n + 1)], v_t[:, 128 * m:128 * (m + 1)],
                        at["wm"][:, 8 * n + m:8 * n + m + 1], a_t[:, 128 * n:128 * (n + 1)],
                        ALU.mult, ALU.add))
                p.chain("vector", [] if n else [("dve", vsm)], fns)
            vsq = p.step("scalar", [("dve", p.v("dve"))],
                         lambda e: e.activation(at["scr"][:], a_t, AF.Square), inc=1)
            vr = p.chain("vector", [("act", vsq)], [
                lambda e: e.tensor_reduce(at["rstd"][:], at["scr"][:].rearrange("p (n e2) -> p n e2", e2=128),
                                          AX.X, ALU.add),
                lambda e: e.tensor_scalar(at["rstd"][:], at["rstd"][:], 1.0 / 128, None, ALU.mult, ALU.bypass),
            ])
            vsq = p.step("scalar", [("dve", vr)],
                         lambda e: e.activation(at["rstd"][:], at["rstd"][:], AF.Sqrt, bias=at["eps8"][:],
                                                scale=1.0), inc=1)
            p.chain("vector", [("act", vsq)], [
                lambda e: e.reciprocal(at["rstd"][:], at["rstd"][:]),
                lambda e: e.tensor_tensor(a_t.rearrange("p (n e2) -> p n e2", e2=128),
                                          a_t.rearrange("p (n e2) -> p n e2", e2=128),
                                          at["rstd"][:].unsqueeze(2).broadcast_to((128, 8, 128)), ALU.mult),
                lambda e: e.tensor_tensor(a_t.rearrange("p (n e2) -> p n e2", e2=128),
                                          a_t.rearrange("p (n e2) -> p n e2", e2=128),
                                          at["lnw"][:].unsqueeze(1).broadcast_to((128, 8, 128)), ALU.mult),
                lambda e: e.tensor_tensor(a_t.rearrange("p (n e2) -> p n e2", e2=128),
                                          a_t.rearrange("p (n e2) -> p n e2", e2=128),
                                          at["lnb"][:].unsqueeze(1).broadcast_to((128, 8, 128)), ALU.add),
            ])
        v_attn = p.v("dve")

        for tt in range(4):
            for fc in range(FC):
                pb = 6 + (fc % 2)
                w = [("dve", v_attn if (tt == 0 and fc == 0) else 0), ("dve", st["tr_dve"][pb])]
                vt = p.step("tensor", w,
                            lambda e, tt=tt, fc=fc, pb=pb: e.transpose(
                                p.bank(pb)[:, 0:128],
                                at["attn"][:, D * tt + 128 * fc:D * tt + 128 * (fc + 1)],
                                cm["ident"][:]), inc=1)
                st["tr_dve"][pb] = p.step("vector", [("pe", vt)],
                                          lambda e, tt=tt, fc=fc, pb=pb: e.tensor_copy(
                                              at["af"][:, SEG1 * fc + 128 * tt:SEG1 * fc + 128 * (tt + 1)],
                                              p.bank(pb)[:, 0:128]), inc=1)
        v_af = p.v("dve")

        d0 = dma_w(W["Wo"], 0, 0)
        d1 = dma_w(W["Wo"], 1, 1)
        dmav = {0: d0, 1: d1}
        for mc in range(FC):
            buf = mc // 4
            bk = mc % 2
            waits = [("pool", dmav[buf]), ("dve", st["bk_dve"][bk])]
            if mc == 0:
                waits.append(("dve", v_af))
            items = []
            for kc in range(FC):
                items.append((lambda e, kc=kc, mc=mc, bk=bk, buf=buf: e.matmul(
                    p.bank(bk)[:, 0:512],
                    at["wbuf"][buf][:].rearrange("p (kc m) -> p kc m", m=512)[:, kc:kc + 1, 128 * (mc % 4):128 * (mc % 4 + 1)].squeeze(),
                    at["af"][:, SEG1 * kc:SEG1 * (kc + 1)],
                    start=(kc == 0), stop=(kc == FC - 1)), 1 if kc == FC - 1 else 0))
            vmm = p.multi("tensor", waits, items)
            st["bk_dve"][bk] = p.step("vector", [("pe", vmm)],
                                      lambda e, mc=mc, bk=bk, t0=t0: e.tensor_tensor(
                                          X[:, T * mc + t0:T * mc + t0 + SEG1],
                                          X[:, T * mc + t0:T * mc + t0 + SEG1],
                                          p.bank(bk)[:, 0:512], ALU.add), inc=1)
        st["wb_pe"] = [p.v("pe"), p.v("pe")]
    return



def build_ffn_program(layer, x_in_kind, x_out_kind="fm"):
    p = Prog(f"ffn{layer}")
    if x_in_kind == "tm":
        x_in = p.din("x_in", [T, D])
    else:
        x_in = p.din("x_in", [D, T])
    x_out = p.dout("x_out", [T, D] if x_out_kind == "tm" else [D, T])
    ident = p.din("c_ident", [128, 128])
    if layer == 0:
        rw = p.din("L0_ffn_w_router", [D, NE])
        w1 = p.din("L0_ffn_w1", [NE, D, 2816])
        w2r = p.din("w2r_l0", [NE, FC, 128, 22, 128])
        w3 = p.din("L0_ffn_w3", [NE, D, 2816])
        experts = [{"w1": w1[e:e + 1].squeeze(), "w3": w3[e:e + 1].squeeze(), "w2r": w2r[e:e + 1].squeeze()}
                   for e in range(NE)]
        cfg = MoECfg(2816, experts, rw, 0.0, 0.5)
        maxI = 2816
    else:
        rw = p.din("L1_ffn_Wr", [D, NE])
        wg = p.din("L1_ffn_Wg", [NE, D, 2048])
        wu = p.din("L1_ffn_Wu", [NE, D, 2048])
        wdr = p.din("w2r_l1", [NE, FC, 128, 16, 128])
        sg = p.din("L1_ffn_Sg", [D, 2048])
        su = p.din("L1_ffn_Su", [D, 2048])
        sdr = p.din("w2r_sd", [FC, 128, 16, 128])
        experts = [{"w1": wg[e:e + 1].squeeze(), "w3": wu[e:e + 1].squeeze(), "w2r": wdr[e:e + 1].squeeze()}
                   for e in range(NE)]
        experts.append({"w1": sg, "w3": su, "w2r": sdr, "shared": True})
        cfg = MoECfg(2048, experts, rw, 1e-20, 0.5)
        maxI = 2048
    p.start()
    cm = alloc_common(p)
    ln = alloc_ln(p, cm)
    ms = alloc_moe(p, cm, maxI)
    emit_init_consts(p, cm, ident)
    if x_in_kind == "tm":
        emit_load_x_tm(p, cm, x_in)
    else:
        emit_load_x_fm(p, cm, x_in)
    emit_ln_stats(p, cm, ln)
    emit_moe(p, cm, ln, ms, cfg)
    if x_out_kind == "tm":
        emit_store_x_tm(p, cm, x_out)
    else:
        emit_store_x_fm(p, cm, x_out)
    p.finalize()
    return p


def build_gla_program(x_in_kind="fm"):
    p = Prog("gla")
    x_in = p.din("x_in", [T, D] if x_in_kind == "tm" else [D, T])
    x_out = p.dout("x_out", [D, T])
    ident = p.din("c_ident", [128, 128])
    W = {}
    W["c_amask"] = p.din("c_amask", [32, 512])
    W["c_mask16"] = p.din("c_mask16", [128, T])
    W["c_mask32"] = p.din("c_mask32", [128, T])
    W["key_wT"] = p.din("key_wT", [D, D])
    W["vq_wT"] = p.din("vq_wT", [D, 2 * D])
    W["out_wT"] = p.din("out_wT", [D, D])
    W["key_b_r"] = p.din("key_b_r", [128, 8])
    W["out_b_r"] = p.din("out_b_r", [128, 8])
    W["c_ident64"] = p.din("c_ident64", [128, 64])
    p.start()
    cm = alloc_common(p)
    ln = alloc_ln(p, cm)
    gs = alloc_gla(p, cm)
    emit_init_consts(p, cm, ident)
    if x_in_kind == "tm":
        emit_load_x_tm(p, cm, x_in)
    else:
        emit_load_x_fm(p, cm, x_in)
    emit_ln_stats(p, cm, ln)
    emit_gla(p, cm, ln, gs, W)
    emit_store_x_fm(p, cm, x_out)
    p.finalize()
    return p


def gla_perm():
    perm = np.zeros(1024, np.int64)
    i = 0
    for fc in range(8):
        for s in range(2):
            for e in range(64):
                perm[i] = 64 * (fc + 8 * s) + e
                i += 1
    return perm


def gla_host_inputs(inp):
    pm = gla_perm()
    f = {}
    f["key_wT"] = np.ascontiguousarray(inp["L0_attn_key_w"].T[:, pm])
    vqT = inp["L0_attn_vq_w"].T
    f["vq_wT"] = np.ascontiguousarray(np.concatenate([vqT[:, :1024][:, pm], vqT[:, 1024:][:, pm]], axis=1))
    f["out_wT"] = np.ascontiguousarray(inp["L0_attn_out_w"].T[pm, :])
    f["key_b_r"] = np.ascontiguousarray(inp["L0_attn_key_b"][pm].reshape(8, 128).T)
    f["out_b_r"] = np.ascontiguousarray(inp["L0_attn_out_b"].reshape(8, 128).T)
    return f



def build_attn1_program():
    p = Prog("attn1")
    x_in = p.din("x_in", [D, T])
    x_out = p.dout("x_out", [D, T])
    ident = p.din("c_ident", [128, 128])
    W = {}
    W["Wq"] = p.din("L1_attn_Wq", [D, D])
    W["Wk"] = p.din("L1_attn_Wk", [D, D])
    W["Wv"] = p.din("L1_attn_Wv", [D, D])
    W["Wo"] = p.din("L1_attn_Wo", [D, D])
    W["lambda_q1"] = p.din("L1_attn_lambda_q1", [64])
    W["lambda_k1"] = p.din("L1_attn_lambda_k1", [64])
    W["lambda_q2"] = p.din("L1_attn_lambda_q2", [64])
    W["lambda_k2"] = p.din("L1_attn_lambda_k2", [64])
    W["ln_w_rep"] = p.din("ln_w_rep", [128, 128])
    W["ln_b_rep"] = p.din("ln_b_rep", [128, 128])
    W["c_cos"] = p.din("c_cos", [T, 64])
    W["c_sin"] = p.din("c_sin", [T, 64])
    p.start()
    cm = alloc_common(p)
    ln = alloc_ln(p, cm)
    at = alloc_attn1(p, cm)
    emit_init_consts(p, cm, ident)
    emit_load_x_fm(p, cm, x_in)
    emit_ln_stats(p, cm, ln)
    emit_attn1(p, cm, ln, at, W)
    emit_store_x_fm(p, cm, x_out)
    p.finalize()
    return p


def attn1_host_inputs(inp):
    return {"ln_w_rep": np.ascontiguousarray(np.broadcast_to(inp["L1_attn_ln_w"], (128, 128))),
            "ln_b_rep": np.ascontiguousarray(np.broadcast_to(inp["L1_attn_ln_b"], (128, 128)))}



def build_fused_program():
    p = Prog("fused")
    x_in = p.din("x_in", [T, D])
    x_out = p.dout("x_out", [T, D])
    ident = p.din("c_ident", [128, 128])
    rw0 = p.din("L0_ffn_w_router", [D, NE])
    w1_0 = p.din("L0_ffn_w1", [NE, D, 2816])
    w2r_0 = p.din("w2r_l0", [NE, FC, 128, 22, 128])
    w3_0 = p.din("L0_ffn_w3", [NE, D, 2816])
    experts0 = [{"w1": w1_0[e:e + 1].squeeze(), "w3": w3_0[e:e + 1].squeeze(), "w2r": w2r_0[e:e + 1].squeeze()}
                for e in range(NE)]
    cfg0 = MoECfg(2816, experts0, rw0, 0.0, 0.5)
    rw1 = p.din("L1_ffn_Wr", [D, NE])
    wg = p.din("L1_ffn_Wg", [NE, D, 2048])
    wu = p.din("L1_ffn_Wu", [NE, D, 2048])
    wdr = p.din("w2r_l1", [NE, FC, 128, 16, 128])
    sg = p.din("L1_ffn_Sg", [D, 2048])
    su = p.din("L1_ffn_Su", [D, 2048])
    sdr = p.din("w2r_sd", [FC, 128, 16, 128])
    experts1 = [{"w1": wg[e:e + 1].squeeze(), "w3": wu[e:e + 1].squeeze(), "w2r": wdr[e:e + 1].squeeze()}
                for e in range(NE)]
    experts1.append({"w1": sg, "w3": su, "w2r": sdr, "shared": True})
    cfg1 = MoECfg(2048, experts1, rw1, 1e-20, 0.5)
    GW = {}
    GW["c_amask"] = p.din("c_amask", [32, 512])
    GW["c_mask16"] = p.din("c_mask16", [128, T])
    GW["c_mask32"] = p.din("c_mask32", [128, T])
    GW["key_wT"] = p.din("key_wT", [D, D])
    GW["vq_wT"] = p.din("vq_wT", [D, 2 * D])
    GW["out_wT"] = p.din("out_wT", [D, D])
    GW["key_b_r"] = p.din("key_b_r", [128, 8])
    GW["out_b_r"] = p.din("out_b_r", [128, 8])
    GW["c_ident64"] = p.din("c_ident64", [128, 64])
    AW = {}
    AW["Wq"] = p.din("L1_attn_Wq", [D, D])
    AW["Wk"] = p.din("L1_attn_Wk", [D, D])
    AW["Wv"] = p.din("L1_attn_Wv", [D, D])
    AW["Wo"] = p.din("L1_attn_Wo", [D, D])
    AW["lambda_q1"] = p.din("L1_attn_lambda_q1", [64])
    AW["lambda_k1"] = p.din("L1_attn_lambda_k1", [64])
    AW["lambda_q2"] = p.din("L1_attn_lambda_q2", [64])
    AW["lambda_k2"] = p.din("L1_attn_lambda_k2", [64])
    AW["ln_w_rep"] = p.din("ln_w_rep", [128, 128])
    AW["ln_b_rep"] = p.din("ln_b_rep", [128, 128])
    AW["c_cos"] = p.din("c_cos", [T, 64])
    AW["c_sin"] = p.din("c_sin", [T, 64])

    p.start()
    cm = alloc_common(p)
    ln = alloc_ln(p, cm)
    emit_init_consts(p, cm, ident)
    p.push_scope()
    emit_load_x_tm(p, cm, x_in)
    p.barrier()
    p.pop_scope()
    p.new_epoch()

    def section(emitfn, allocfn):
        p.push_scope()
        buf = allocfn()
        emit_ln_stats(p, cm, ln)
        emitfn(buf)
        p.barrier()
        p.pop_scope()
        p.new_epoch()

    section(lambda ms: emit_moe(p, cm, ln, ms, cfg0), lambda: alloc_moe(p, cm, 2816))
    section(lambda gs: emit_gla(p, cm, ln, gs, GW), lambda: alloc_gla(p, cm))
    section(lambda ms: emit_moe(p, cm, ln, ms, cfg0), lambda: alloc_moe(p, cm, 2816))
    section(lambda ms: emit_moe(p, cm, ln, ms, cfg1), lambda: alloc_moe(p, cm, 2048))
    section(lambda at: emit_attn1(p, cm, ln, at, AW), lambda: alloc_attn1(p, cm))
    section(lambda ms: emit_moe(p, cm, ln, ms, cfg1), lambda: alloc_moe(p, cm, 2048))

    p.push_scope()
    emit_store_x_tm(p, cm, x_out)
    p.pop_scope()
    p.finalize()
    return p


def kernel_fused(inputs):
    consts = host_constants()
    extra = host_extra(inputs)
    x = inputs["L0_attn_x"]
    prog = build_fused_program()
    in_maps = []
    for i in range(8):
        in_maps.append(make_feeds(prog, inputs, consts, extra, np.ascontiguousarray(x[i])))
    res = run_bass_kernel_spmd(prog.nc, in_maps, core_ids=list(range(8)))
    return np.stack([np.asarray(r["x_out"]) for r in res.results], axis=0).astype(np.float32)



def _run(p: Prog, per_core_inputs):
    res = run_bass_kernel_spmd(p.nc, per_core_inputs, core_ids=list(range(len(per_core_inputs))))
    return res.results


def make_feeds(p, inputs, consts, extra, x_shard):
    feed = {}
    for n in p.in_names:
        if n == "x_in":
            feed[n] = x_shard
        elif n in extra:
            feed[n] = extra[n]
        elif n in consts:
            feed[n] = consts[n]
        elif n in inputs:
            feed[n] = inputs[n]
        else:
            raise KeyError(n)
    return feed


def host_extra(inputs):
    extra = {}
    extra["w2r_l0"] = relayout_w2(inputs["L0_ffn_w2"])
    extra["w2r_l1"] = relayout_w2(inputs["L1_ffn_Wd"])
    extra["w2r_sd"] = relayout_w2(inputs["L1_ffn_Sd"])
    extra.update(gla_host_inputs(inputs))
    extra.update(attn1_host_inputs(inputs))
    return extra


def kernel(**inputs):
    inputs = {k: np.ascontiguousarray(v, dtype=np.float32) for k, v in inputs.items()}
    return kernel_fused(inputs)


def kernel_unfused(**inputs):
    inputs = {k: np.ascontiguousarray(v, dtype=np.float32) for k, v in inputs.items()}
    consts = host_constants()
    extra = host_extra(inputs)
    x = inputs["L0_attn_x"]
    shards = [np.ascontiguousarray(x[i]) for i in range(8)]
    progs = [
        build_ffn_program(0, "tm"),
        build_gla_program("fm"),
        build_ffn_program(0, "fm"),
        build_ffn_program(1, "fm"),
        build_attn1_program(),
        build_ffn_program(1, "fm", "tm"),
    ]
    cur = shards
    for prog in progs:
        in_maps = [make_feeds(prog, inputs, consts, extra, cur[i]) for i in range(8)]
        res = run_bass_kernel_spmd(prog.nc, in_maps, core_ids=list(range(8)))
        cur = [np.asarray(r["x_out"]) for r in res.results]
    return np.stack(cur, axis=0).astype(np.float32)


# version 04
# speedup vs baseline: 1.0457x; 1.0457x over previous
import numpy as np
import concourse.bass as bass
import concourse.mybir as mybir
from concourse.bass_utils import run_bass_kernel_spmd

F32 = mybir.dt.float32
BF16 = mybir.dt.bfloat16
AF = mybir.ActivationFunctionType
ALU = mybir.AluOpType
AX = mybir.AxisListType

T = 2048
D = 1024
FC = 8
NE = 8

SEM_OF = {"tensor": "pe", "vector": "dve", "scalar": "act", "gpsimd": "pool", "sync": "sy"}


def relayout_w2(w2):
    sh = w2.shape
    I, Dd = sh[-2], sh[-1]
    lead = sh[:-2]
    r = w2.reshape(lead + (I // 128, 128, Dd // 128, 128))
    perm = tuple(range(len(lead))) + (len(lead) + 2, len(lead) + 1, len(lead), len(lead) + 3)
    return np.ascontiguousarray(r.transpose(perm))


def host_constants():
    c = {}
    c["c_ident"] = np.eye(128, dtype=np.float32)
    d = 64
    inv_freq = 1.0 / (10000.0 ** (np.arange(0, d, 2, dtype=np.float32) / d))
    pos = np.arange(T, dtype=np.float32)
    freqs = np.repeat(pos[:, None] * inv_freq[None, :], 2, axis=1)
    c["c_cos"] = np.cos(freqs).astype(np.float32)
    c["c_sin"] = np.sin(freqs).astype(np.float32)
    m16 = np.ones((128, T), np.float32); m16[:, 0::16] = 0.0
    m32 = np.ones((128, T), np.float32); m32[:, 0::32] = 0.0
    c["c_mask16"] = m16
    c["c_mask32"] = m32
    tri = (np.arange(16)[:, None] <= np.arange(16)[None, :]).astype(np.float32)
    am = np.zeros((32, 32), np.float32)
    am[0:16, 0:16] = tri
    am[0:16, 16:32] = 1.0
    am[16:32, 16:32] = tri
    c["c_amask"] = np.tile(am, (1, 16)).astype(np.float32)
    i64 = np.zeros((128, 64), np.float32)
    i64[0:64] = np.eye(64); i64[64:128] = np.eye(64)
    c["c_ident64"] = i64
    return c


class Prog:
    def __init__(self, name="prog"):
        self.nc = bass.Bass("TRN2", target_bir_lowering=False)
        self.name = name
        self._ctx = []
        self.sem = {}
        self.cnt = {}
        self.out_names = []
        self.in_names = []

    def din(self, name, shape, dtype=F32):
        self.in_names.append(name)
        return self.nc.dram_tensor(name, list(shape), dtype, kind="ExternalInput").ap()

    def dout(self, name, shape, dtype=F32):
        self.out_names.append(name)
        return self.nc.dram_tensor(name, list(shape), dtype, kind="ExternalOutput").ap()

    def dtmp(self, name, shape, dtype=F32):
        return self.nc.dram_tensor(name, list(shape), dtype, kind="Internal").ap()

    def sb(self, name, shape, dtype=F32):
        self._name_ctr = getattr(self, "_name_ctr", 0) + 1
        h = self.nc.sbuf_tensor(f"{name}_{self._name_ctr}", list(shape), dtype)
        t = h.__enter__(); self._ctx.append(h)
        return t

    def start(self):
        self._sem_ctx = []
        self._scopes = []
        self._epoch = 0
        self.new_epoch(first=True)
        h = self.nc.psum_tensor("ps_arena", [128, 4096], F32)
        self.psum = h.__enter__(); self._ctx.append(h)
        h = self.nc.Block()
        self.block = h.__enter__(); self._ctx.append(h)

    def new_epoch(self, first=False):
        for s in ["pe", "dve", "act", "pool", "sy"]:
            h = self.nc.semaphore(f"{s}{self._epoch}")
            self.sem[s] = h.__enter__(); self._sem_ctx.append(h)
            self.cnt[s] = 0
        self._epoch += 1

    def push_scope(self):
        self._scopes.append(len(self._ctx))

    def pop_scope(self):
        i = self._scopes.pop()
        for h in reversed(self._ctx[i:]):
            h.__exit__(None, None, None)
        del self._ctx[i:]

    def bank(self, b, n=1):
        return self.psum[:, 512 * b:512 * (b + n)]

    def finalize(self):
        for h in reversed(self._ctx):
            h.__exit__(None, None, None)
        self._ctx = []
        for h in reversed(self._sem_ctx):
            h.__exit__(None, None, None)
        self._sem_ctx = []

    def step(self, eng, waits, fn, inc=1):
        own = SEM_OF[eng]
        sem = self.sem
        def body(e):
            for s, v in waits:
                if v > 0:
                    e.wait_ge(sem[s], v)
            last = fn(e)
            if inc:
                last.then_inc(sem[own], inc)
        getattr(self.block, eng)(body)
        if inc:
            self.cnt[own] += inc
        return self.cnt[own]

    def multi(self, eng, waits, items):
        own = SEM_OF[eng]
        sem = self.sem
        def body(e):
            for s, v in waits:
                if v > 0:
                    e.wait_ge(sem[s], v)
            for fn, inc in items:
                inst = fn(e)
                if inc:
                    inst.then_inc(sem[own], inc)
        getattr(self.block, eng)(body)
        self.cnt[own] += sum(i for _, i in items)
        return self.cnt[own]

    def chain(self, eng, waits, fns, inc_last=1):
        own = SEM_OF[eng]
        sem = self.sem
        base = self.cnt[own]
        n = len(fns)
        def body(e):
            for s, v in waits:
                if v > 0:
                    e.wait_ge(sem[s], v)
            for i, fn in enumerate(fns):
                if i > 0:
                    e.wait_ge(sem[own], base + i)
                fn(e).then_inc(sem[own], 1)
        getattr(self.block, eng)(body)
        self.cnt[own] += n
        return self.cnt[own]

    def v(self, s):
        return self.cnt[s]

    def barrier(self):
        snap = dict(self.cnt)
        for eng in ["tensor", "vector", "scalar", "gpsimd", "sync"]:
            own = SEM_OF[eng]
            waits = [(s, v) for s, v in snap.items() if s != own and v > 0]
            self.step(eng, waits, lambda e: e.nop(), inc=1)

    def wait_all(self, exclude=()):
        return [(s, self.cnt[s]) for s in ["pe", "dve", "act", "pool", "sy"] if s not in exclude and self.cnt[s] > 0]



def alloc_common(p):
    cm = {}
    cm["X"] = p.sb("X", [128, FC * T], F32)
    cm["ident"] = p.sb("ident", [128, 128], F32)
    cm["ones_col_bf"] = p.sb("ones_col_bf", [128, 1], BF16)
    cm["ones_row"] = p.sb("ones_row", [1, 128], F32)
    cm["eps_ln"] = p.sb("eps_ln", [1, 1], F32)
    return cm


def emit_init_consts(p, cm, ident_dram):
    p.step("sync", [], lambda e: e.dma_start(out=cm["ident"][:], in_=ident_dram), inc=16)
    p.multi("vector", [], [
        (lambda e: e.memset(cm["ones_col_bf"][:], 1.0), 0),
        (lambda e: e.memset(cm["ones_row"][:], 1.0), 0),
        (lambda e: e.memset(cm["eps_ln"][:], 1e-5), 1),
    ])


def emit_load_x_fm(p, cm, x_fm):
    items = [(lambda e, c=c: e.dma_start(out=cm["X"][:, T * c:T * (c + 1)],
                                         in_=x_fm[128 * c:128 * (c + 1), :]), 16) for c in range(FC)]
    return p.multi("sync", [], items)


def emit_store_x_fm(p, cm, x_fm):
    items = [(lambda e, c=c: e.dma_start(out=x_fm[128 * c:128 * (c + 1), :],
                                         in_=cm["X"][:, T * c:T * (c + 1)]), 16) for c in range(FC)]
    v = p.multi("sync", p.wait_all(exclude=("sy",)), items)
    p.step("sync", [("sy", v)], lambda e: e.nop(), inc=0)


def emit_load_x_tm(p, cm, x_tm):
    stg = [p.sb("xin_stg0", [128, D], F32), p.sb("xin_stg1", [128, D], F32)]
    dma_v = {}
    pe_after_tile = {}
    for tt in range(16):
        b = tt % 2
        w = [("pe", pe_after_tile[tt - 2])] if tt >= 2 else []
        dma_v[tt] = p.step("sync", w,
                           lambda e, tt=tt, b=b: e.dma_start(out=stg[b][:], in_=x_tm[128 * tt:128 * (tt + 1), :]),
                           inc=16)
        for c in range(FC):
            pb = 6 + (c % 2)
            w = [("sy", dma_v[tt])]
            w.append(("dve", max(0, p.v("dve") - 1)))
            vpe = p.step("tensor", w,
                         lambda e, c=c, b=b, pb=pb: e.transpose(p.bank(pb)[:, 0:128],
                                                                stg[b][:, 128 * c:128 * (c + 1)], cm["ident"][:]),
                         inc=1)
            p.step("vector", [("pe", vpe)],
                   lambda e, c=c, tt=tt, pb=pb: e.tensor_copy(cm["X"][:, T * c + 128 * tt:T * c + 128 * (tt + 1)],
                                                              p.bank(pb)[:, 0:128]),
                   inc=1)
        pe_after_tile[tt] = p.v("pe")
    return p.v("dve")


def emit_store_x_tm(p, cm, x_tm):
    stg = [p.sb("xout_stg0", [128, D], F32), p.sb("xout_stg1", [128, D], F32)]
    first_waits = p.wait_all(exclude=("pe",))
    dma_v = {}
    for tt in range(16):
        b = tt % 2
        for c in range(FC):
            pb = 6 + (c % 2)
            w = list(first_waits) if (tt == 0 and c == 0) else []
            w.append(("dve", max(0, p.v("dve") - 1)))
            if tt >= 2 and c == 0:
                w.append(("sy", dma_v[tt - 2]))
            vpe = p.step("tensor", w,
                         lambda e, c=c, tt=tt, pb=pb: e.transpose(p.bank(pb)[:, 0:128],
                                                                  cm["X"][:, T * c + 128 * tt:T * c + 128 * (tt + 1)],
                                                                  cm["ident"][:]),
                         inc=1)
            p.step("vector", [("pe", vpe)],
                   lambda e, c=c, b=b, pb=pb: e.tensor_copy(stg[b][:, 128 * c:128 * (c + 1)], p.bank(pb)[:, 0:128]),
                   inc=1)
        dma_v[tt] = p.step("sync", [("dve", p.v("dve"))],
                           lambda e, tt=tt, b=b: e.dma_start(out=x_tm[128 * tt:128 * (tt + 1), :], in_=stg[b][:]),
                           inc=16)
    p.step("sync", [("sy", p.v("sy"))], lambda e: e.nop(), inc=0)



def alloc_ln(p, cm):
    ln = {}
    ln["xb"] = [p.sb("ln_xb0", [128, 512], BF16), p.sb("ln_xb1", [128, 512], BF16)]
    ln["xsq"] = [p.sb("ln_xsq0", [128, 512], BF16), p.sb("ln_xsq1", [128, 512], BF16)]
    ln["a_row"] = p.sb("ln_a", [1, T], F32)
    ln["b_row"] = p.sb("ln_b", [1, T], F32)
    ln["scr"] = p.sb("ln_scr", [128, 1024], F32)
    return ln


def emit_ln_stats(p, cm, ln):
    X = cm["X"]
    ps_s = p.psum[0:1, 0:T]
    ps_q = p.psum[0:1, T:2 * T]
    first = True
    for w4 in range(4):
        for c in range(FC):
            b = c % 2
            wv = p.wait_all(exclude=("dve",)) if first else [("pe", max(0, p.v("pe") - 1))]
            first = False
            vc = p.step("vector", wv,
                        lambda e, c=c, w4=w4, b=b: e.tensor_copy(ln["xb"][b][:],
                                                                 X[:, T * c + 512 * w4:T * c + 512 * (w4 + 1)]),
                        inc=1)
            p.step("tensor", [("dve", vc)],
                   lambda e, c=c, w4=w4, b=b: e.matmul(ps_s[:, 512 * w4:512 * (w4 + 1)], cm["ones_col_bf"][:],
                                                       ln["xb"][b][:], start=(c == 0), stop=(c == FC - 1)),
                   inc=1)
        for c in range(FC):
            b = c % 2
            vs = p.step("scalar", [("pe", max(0, p.v("pe") - 1))],
                        lambda e, c=c, w4=w4, b=b: e.activation(ln["xsq"][b][:],
                                                                X[:, T * c + 512 * w4:T * c + 512 * (w4 + 1)],
                                                                AF.Square), inc=1)
            p.step("tensor", [("act", vs)],
                   lambda e, c=c, w4=w4, b=b: e.matmul(ps_q[:, 512 * w4:512 * (w4 + 1)], cm["ones_col_bf"][:],
                                                       ln["xsq"][b][:], start=(c == 0), stop=(c == FC - 1)),
                   inc=1)
    vpe = p.v("pe")
    a_row, b_row = ln["a_row"], ln["b_row"]
    vv = p.chain("vector", [("pe", vpe)], [
        lambda e: e.tensor_scalar(ps_s, ps_s, 1.0 / D, None, ALU.mult, ALU.bypass),
        lambda e: e.tensor_scalar(ps_q, ps_q, 1.0 / D, None, ALU.mult, ALU.bypass),
        lambda e: e.tensor_copy(b_row[:], ps_s),
        lambda e: e.tensor_tensor(a_row[:], b_row[:], b_row[:], ALU.mult),
        lambda e: e.tensor_tensor(ps_q, ps_q, a_row[:], ALU.subtract),
    ])
    va = p.step("scalar", [("dve", vv)],
                lambda e: e.activation(a_row[:], ps_q, AF.Sqrt, bias=cm["eps_ln"][0:1, :], scale=1.0), inc=1)
    p.chain("vector", [("act", va)], [
        lambda e: e.reciprocal(a_row[:], a_row[:]),
        lambda e: e.scalar_tensor_tensor(b_row[:], b_row[:], -1.0, a_row[:], ALU.mult, ALU.mult),
    ])
    return p.v("dve")


def emit_ln_apply(p, cm, ln, t0, L, Xn, xn_stride=None):
    X = cm["X"]
    xn_stride = L if xn_stride is None else xn_stride
    assert L <= 512
    a_bc = p.bank(4)[:, 0:L]
    b_bc = p.bank(5)[:, 0:L]
    items = [
        (lambda e: e.matmul(a_bc, cm["ones_row"][:], ln["a_row"][0:1, t0:t0 + L], start=True, stop=True), 0),
        (lambda e: e.matmul(b_bc, cm["ones_row"][:], ln["b_row"][0:1, t0:t0 + L], start=True, stop=True), 1),
    ]
    vbc = p.multi("tensor", [("dve", p.v("dve"))], items)
    for c in range(FC):
        p.chain("vector", [("pe", vbc)] if c == 0 else [], [
            (lambda e, c=c: e.tensor_tensor(ln["scr"][:, 0:L], X[:, T * c + t0:T * c + t0 + L], a_bc, ALU.mult)),
            (lambda e, c=c: e.tensor_tensor(Xn[:, xn_stride * c:xn_stride * c + L], ln["scr"][:, 0:L], b_bc, ALU.add)),
        ])
    return p.v("dve")


def emit_ln_apply_off(p, cm, ln, t0, L, Xn, dst_off, xn_stride):
    X = cm["X"]
    a_bc = p.bank(4)[:, 0:L]
    b_bc = p.bank(5)[:, 0:L]
    items = [
        (lambda e: e.matmul(a_bc, cm["ones_row"][:], ln["a_row"][0:1, t0:t0 + L], start=True, stop=True), 0),
        (lambda e: e.matmul(b_bc, cm["ones_row"][:], ln["b_row"][0:1, t0:t0 + L], start=True, stop=True), 1),
    ]
    vbc = p.multi("tensor", [("dve", p.v("dve"))], items)
    for c in range(FC):
        p.chain("vector", [("pe", vbc)] if c == 0 else [], [
            (lambda e, c=c: e.tensor_tensor(ln["scr"][:, 0:L], X[:, T * c + t0:T * c + t0 + L], a_bc, ALU.mult)),
            (lambda e, c=c: e.tensor_tensor(Xn[:, xn_stride * c + dst_off:xn_stride * c + dst_off + L],
                                            ln["scr"][:, 0:L], b_bc, ALU.add)),
        ])
    return p.v("dve")



class MoECfg:
    def __init__(self, I, experts, router_w, eps_router, scale):
        self.I = I
        self.IC = I // 128
        self.experts = experts
        self.router_w = router_w
        self.eps_router = eps_router
        self.scale = scale


def alloc_moe(p, cm, maxI):
    ms = {}
    ms["Xn"] = p.sb("moe_xn", [128, FC * 1024], BF16)
    ms["h"] = p.sb("moe_h", [128, (maxI // 128) * 1024], BF16)
    ms["w1t"] = [p.sb("moe_w1t0", [128, 8 * 256], BF16), p.sb("moe_w1t1", [128, 8 * 256], BF16)]
    ms["w3t"] = [p.sb("moe_w3t0", [128, 8 * 256], BF16), p.sb("moe_w3t1", [128, 8 * 256], BF16)]
    ms["w2t"] = [p.sb("moe_w2t0", [128, (maxI // 128) * 128], BF16), p.sb("moe_w2t1", [128, (maxI // 128) * 128], BF16)]
    ms["bc"] = p.sb("moe_bc", [128, 1024], F32)
    ms["ctrow2"] = [p.sb("moe_ctrow0", [1, 1024], F32), p.sb("moe_ctrow1", [1, 1024], F32)]
    ms["scr512"] = [p.sb("moe_scr512a", [128, 512], BF16), p.sb("moe_scr512b", [128, 512], BF16)]
    ms["st"] = {"gi": 0, "gu_evac": {0: 0, 1: 0}, "si_read": [0, 0],
                "w13_pe": {0: 0, 1: 0}, "w2_pe": {0: 0, 1: 0}, "yi": 0,
                "y_bank_evac": {5: 0, 6: 0}, "pe_bc": 0, "dve_after_y": 0,
                "bc_guard": {0: 0, 1: 0}, "bank4_dve": 0}
    ms["rw"] = p.sb("moe_rw", [128, 8 * 8], BF16)
    ms["sc_sb"] = p.sb("moe_scsb", [8, 1024], F32)
    ms["cw"] = p.sb("moe_cw", [128, 8], F32)
    ms["p_sb"] = p.sb("moe_p", [128, 8], F32)
    ms["t_sb"] = p.sb("moe_t", [128, 8], F32)
    ms["m_sb"] = p.sb("moe_m", [128, 4], F32)
    ms["ct"] = p.sb("moe_ct", [8, T], F32)
    return ms


def emit_moe(p, cm, ln, ms, cfg: MoECfg):
    X = cm["X"]
    IC = cfg.IC
    Xn = ms["Xn"]
    h = ms["h"]
    nwin = (cfg.I + 255) // 256
    wsz = [min(256, cfg.I - 256 * w) for w in range(nwin)]
    ngrp = (IC + 3) // 4
    gsz = [min(4, IC - 4 * g) for g in range(ngrp)]

    if cfg.router_w is not None:
        rw3 = cfg.router_w.rearrange("(kc p) e -> p kc e", p=128)
        v_rw = p.step("gpsimd", [], lambda e: e.dma_start(out=ms["rw"][:].rearrange("p (kc e) -> p kc e", e=8), in_=rw3),
                      inc=16)

    for half in range(2):
        t0 = half * 1024
        emit_ln_apply(p, cm, ln, t0, 512, Xn, xn_stride=1024)
        emit_ln_apply_off(p, cm, ln, t0 + 512, 512, Xn, 512, xn_stride=1024)
        v_xn = p.v("dve")

        if cfg.router_w is not None:
            ps_sc = p.psum[0:8, 0:1024]
            items = []
            for w in range(2):
                for kc in range(FC):
                    items.append((lambda e, w=w, kc=kc: e.matmul(
                        ps_sc[:, 512 * w:512 * (w + 1)],
                        ms["rw"][:].rearrange("p (kc e) -> p kc e", e=8)[:, kc:kc + 1, :].squeeze(),
                        Xn[:, 1024 * kc + 512 * w:1024 * kc + 512 * (w + 1)],
                        start=(kc == 0), stop=(kc == FC - 1)), 1 if (kc == FC - 1 and w == 1) else 0))
            v_sc = p.multi("tensor", [("dve", v_xn), ("pool", v_rw)], items)
            v_ev = p.step("vector", [("pe", v_sc)], lambda e: e.tensor_copy(ms["sc_sb"][:], ps_sc), inc=1)
            for tt in range(8):
                vpe = p.step("tensor", [("dve", v_ev if tt == 0 else p.v("dve"))],
                             lambda e, tt=tt: e.transpose(p.bank(2)[0:128, 0:8], ms["sc_sb"][0:8, 128 * tt:128 * (tt + 1)],
                                                          cm["ident"][0:8, 0:8]),
                             inc=1)
                pb, tb, mb, cwb = ms["p_sb"], ms["t_sb"], ms["m_sb"], ms["cw"]
                mx, m1, m2, den = mb[:, 0:1], mb[:, 1:2], mb[:, 2:3], mb[:, 3:4]
                p.chain("vector", [("pe", vpe)], [
                    lambda e: e.tensor_copy(pb[:], p.bank(2)[0:128, 0:8]),
                    lambda e: e.tensor_reduce(mx, pb[:], AX.X, ALU.max),
                    lambda e: e.tensor_scalar(mx, mx, -1.0, None, ALU.mult, ALU.bypass),
                ])
                va = p.step("scalar", [("dve", p.v("dve"))],
                            lambda e: e.activation(tb[:], pb[:], AF.Exp, bias=mx, scale=1.0), inc=1)
                p.chain("vector", [("act", va)], [
                    lambda e: e.tensor_reduce(m1, tb[:], AX.X, ALU.add),
                    lambda e: e.reciprocal(m1, m1),
                    lambda e: e.tensor_scalar(pb[:], tb[:], m1, None, ALU.mult, ALU.bypass),
                    lambda e: e.tensor_reduce(m1, pb[:], AX.X, ALU.max),
                    lambda e: e.tensor_scalar(tb[:], pb[:], m1, None, ALU.is_equal, ALU.bypass),
                    lambda e: e.scalar_tensor_tensor(tb[:], tb[:], -1e30, pb[:], ALU.mult, ALU.add),
                    lambda e: e.tensor_reduce(m2, tb[:], AX.X, ALU.max),
                    lambda e: e.tensor_scalar(tb[:], pb[:], m2, None, ALU.is_ge, ALU.bypass),
                    lambda e: e.tensor_tensor(den, m1, m2, ALU.add),
                    lambda e: e.tensor_scalar(den, den, cfg.eps_router, None, ALU.add, ALU.bypass),
                    lambda e: e.reciprocal(den, den),
                    lambda e: e.tensor_scalar(den, den, cfg.scale, None, ALU.mult, ALU.bypass),
                    lambda e: e.scalar_tensor_tensor(cwb[:], pb[:], den, tb[:], ALU.mult, ALU.mult),
                ])
                vpe2 = p.step("tensor", [("dve", p.v("dve"))],
                              lambda e: e.transpose(p.bank(2)[0:8, 0:128], cwb[:], cm["ident"][:]), inc=1)
                p.step("vector", [("pe", vpe2)],
                       lambda e, tt=tt: e.tensor_copy(ms["ct"][0:8, t0 + 128 * tt:t0 + 128 * (tt + 1)],
                                                      p.bank(2)[0:8, 0:128]),
                       inc=1)
            dbg = ms.get("dbg", {})
            if half == 0 and "ct" in dbg:
                p.step("sync", [("dve", p.v("dve"))], lambda e: e.dma_start(out=dbg["ct"], in_=ms["ct"][:, 0:1024]), inc=16)
            if half == 0 and "xn" in dbg:
                p.step("sync", [("dve", p.v("dve"))], lambda e: e.dma_start(out=dbg["xn"], in_=Xn[:]), inc=16)

        st = ms["st"]
        st["dve_after_y"] = p.v("dve")
        n_routed = sum(0 if ex.get("shared") else 1 for ex in cfg.experts)
        st["ctr_sy"] = {0: p.step("sync", [("pe", st["pe_bc"]), ("dve", p.v("dve"))],
                                  lambda e: e.dma_start(out=ms["ctrow2"][0][:], in_=ms["ct"][0:1, t0:t0 + 1024]),
                                  inc=16)}
        for ei, ex in enumerate(cfg.experts):
            shared = ex.get("shared", False)
            bcb = ms["bc"]
            if ei + 1 < n_routed:
                st["ctr_sy"][ei + 1] = p.step("sync", [("pe", st["pe_bc"]), ("dve", p.v("dve"))],
                                              lambda e, ei=ei: e.dma_start(out=ms["ctrow2"][(ei + 1) % 2][:],
                                                                           in_=ms["ct"][ei + 1:ei + 2, t0:t0 + 1024]),
                                              inc=16)
            if shared:
                v_bcd = p.step("vector", [("dve", st["dve_after_y"])],
                               lambda e, bcb=bcb: e.memset(bcb[:], cfg.scale), inc=1)
            else:
                ctr = ms["ctrow2"][ei % 2]
                v_mm0 = p.multi("tensor", [("sy", st["ctr_sy"][ei]), ("dve", st["bc_guard"][ei % 2]),
                                           ("dve", st["bank4_dve"])], [
                    (lambda e, ctr=ctr: e.matmul(p.bank(4)[:, 0:512], cm["ones_row"][:],
                                                 ctr[0:1, 0:512], start=True, stop=True), 0),
                    (lambda e, ctr=ctr: e.matmul(p.bank(4)[:, 512:1024] if False else p.bank(4)[:, 0:512],
                                                 cm["ones_row"][:], ctr[0:1, 512:1024],
                                                 start=True, stop=True), 1),
                ]) if False else None
                v_mm0 = p.step("tensor", [("sy", st["ctr_sy"][ei]), ("dve", st["dve_after_y"]),
                                          ("dve", st["bank4_dve"])],
                               lambda e, ctr=ctr: e.matmul(p.bank(4)[:, 0:512], cm["ones_row"][:],
                                                           ctr[0:1, 0:512], start=True, stop=True), inc=1)
                v_e0 = p.step("vector", [("pe", v_mm0)],
                              lambda e, bcb=bcb: e.tensor_copy(bcb[:, 0:512], p.bank(4)[:, 0:512]), inc=1)
                v_mm1 = p.step("tensor", [("dve", v_e0)],
                               lambda e, ctr=ctr: e.matmul(p.bank(4)[:, 0:512], cm["ones_row"][:],
                                                           ctr[0:1, 512:1024], start=True, stop=True), inc=1)
                v_bcd = p.step("vector", [("pe", v_mm1)],
                               lambda e, bcb=bcb: e.tensor_copy(bcb[:, 512:1024], p.bank(4)[:, 0:512]), inc=1)
                st["bank4_dve"] = v_bcd
            st["pe_bc"] = p.v("pe")

            def dma_win(w):
                buf = w % 2
                wait = [("pe", st["w13_pe"][buf])]
                src1 = ex["w1"][:, 256 * w:256 * w + wsz[w]].rearrange("(kc p) m -> p kc m", p=128)
                src3 = ex["w3"][:, 256 * w:256 * w + wsz[w]].rearrange("(kc p) m -> p kc m", p=128)
                d1 = ms["w1t"][buf][:, 0:8 * wsz[w]].rearrange("p (kc m) -> p kc m", kc=8)
                d3 = ms["w3t"][buf][:, 0:8 * wsz[w]].rearrange("p (kc m) -> p kc m", kc=8)
                return p.multi("gpsimd", wait, [
                    (lambda e: e.dma_start(out=d1, in_=src1), 16),
                    (lambda e: e.dma_start(out=d3, in_=src3), 16),
                ])
            dma_w13 = {0: dma_win(0)}
            if nwin > 1:
                dma_w13[1] = dma_win(1)
            gi0 = st["gi"]
            for w in range(nwin):
                buf = w % 2
                nmc = wsz[w] // 128
                for mcw in range(nmc):
                    mc = 2 * w + mcw
                    for ts in range(2):
                        gi = st["gi"]
                        bg = gi % 2
                        waits = [("pool", dma_w13[w]), ("dve", st["gu_evac"][bg])]
                        items = []
                        for kc in range(FC):
                            items.append((lambda e, kc=kc, buf=buf, mcw=mcw, ts=ts, bg=bg, w=w: e.matmul(
                                p.bank(bg)[:, 0:512],
                                ms["w1t"][buf][:, 0:8 * wsz[w]].rearrange("p (kc m) -> p kc m", kc=8)[:, kc:kc + 1, 128 * mcw:128 * (mcw + 1)].squeeze(),
                                Xn[:, 1024 * kc + 512 * ts:1024 * kc + 512 * (ts + 1)],
                                start=(kc == 0), stop=(kc == FC - 1)), 0))
                        for kc in range(FC):
                            items.append((lambda e, kc=kc, buf=buf, mcw=mcw, ts=ts, bg=bg, w=w: e.matmul(
                                p.bank(2 + bg)[:, 0:512],
                                ms["w3t"][buf][:, 0:8 * wsz[w]].rearrange("p (kc m) -> p kc m", kc=8)[:, kc:kc + 1, 128 * mcw:128 * (mcw + 1)].squeeze(),
                                Xn[:, 1024 * kc + 512 * ts:1024 * kc + 512 * (ts + 1)],
                                start=(kc == 0), stop=(kc == FC - 1)), 1 if kc == FC - 1 else 0))
                        vmm = p.multi("tensor", waits, items)
                        sbuf_si = gi % 2
                        va = p.step("scalar", [("pe", vmm), ("dve", st["si_read"][sbuf_si])],
                                    lambda e, bg=bg, sbuf_si=sbuf_si: e.activation(
                                        ms["scr512"][sbuf_si][:], p.bank(bg)[:, 0:512], AF.Silu),
                                    inc=1)
                        st["gu_evac"][bg] = p.step("vector", [("act", va)],
                                                   lambda e, mc=mc, ts=ts, bg=bg, sbuf_si=sbuf_si: e.tensor_tensor(
                                                       h[:, 1024 * mc + 512 * ts:1024 * mc + 512 * (ts + 1)],
                                                       ms["scr512"][sbuf_si][:], p.bank(2 + bg)[:, 0:512], ALU.mult),
                                                   inc=1)
                        st["si_read"][sbuf_si] = st["gu_evac"][bg]
                        st["gi"] += 1
                st["w13_pe"][buf] = p.v("pe")
                if w + 2 < nwin:
                    dma_w13[w + 2] = dma_win(w + 2)
            v_h_done = p.v("dve")
            dbg = ms.get("dbg", {})
            if half == 0 and ei == 0 and "h" in dbg:
                p.step("sync", [("dve", v_h_done)], lambda e: e.dma_start(out=dbg["h"], in_=h[:, 0:IC * 1024]), inc=16)
            if half == 0 and ei == 0 and "bc" in dbg:
                p.step("sync", [("dve", v_h_done)], lambda e: e.dma_start(out=dbg["bc"], in_=ms["bc"][:]), inc=16)

            def dma_hc(hc):
                buf = hc % 2
                src = ex["w2r"][hc:hc + 1].squeeze()
                dd = ms["w2t"][buf][:, 0:IC * 128].rearrange("p (kc m) -> p kc m", m=128)
                return p.step("gpsimd", [("pe", st["w2_pe"][buf])],
                              lambda e: e.dma_start(out=dd, in_=src), inc=16)
            dma_w2 = {0: dma_hc(0), 1: dma_hc(1)}
            yi = st["yi"]
            for hc in range(FC):
                buf = hc % 2
                for ts in range(2):
                    yb = 5 + (yi % 2)
                    waits = [("pool", dma_w2[hc]), ("dve", st["y_bank_evac"][yb])]
                    if hc == 0 and ts == 0:
                        waits.append(("dve", v_h_done))
                    items = []
                    for kc in range(IC):
                        items.append((lambda e, kc=kc, hc=hc, buf=buf, ts=ts, yb=yb: e.matmul(
                            p.bank(yb)[:, 0:512],
                            ms["w2t"][buf][:, 0:IC * 128].rearrange("p (kc m) -> p kc m", m=128)[:, kc:kc + 1, :].squeeze(),
                            h[:, 1024 * kc + 512 * ts:1024 * kc + 512 * (ts + 1)],
                            start=(kc == 0), stop=(kc == IC - 1)),
                            1 if kc == IC - 1 else 0))
                    v_y = p.multi("tensor", waits, items)
                    st["y_bank_evac"][yb] = p.multi("vector", [("pe", v_y)], [
                        (lambda e, hc=hc, ts=ts, yb=yb, bcb=bcb: e.tensor_tensor(ln["scr"][:, 0:512], p.bank(yb)[:, 0:512],
                                                                        bcb[:, 512 * ts:512 * (ts + 1)], ALU.mult), 0),
                        (lambda e, hc=hc, ts=ts: e.tensor_tensor(X[:, T * hc + t0 + 512 * ts:T * hc + t0 + 512 * (ts + 1)],
                                                                 X[:, T * hc + t0 + 512 * ts:T * hc + t0 + 512 * (ts + 1)],
                                                                 ln["scr"][:, 0:512], ALU.add), 1),
                    ])
                    yi += 1
                st["w2_pe"][buf] = p.v("pe")
                if hc + 2 < FC:
                    dma_w2[hc + 2] = dma_hc(hc + 2)
            st["yi"] = yi
            st["dve_after_y"] = p.v("dve")
    return



SEG = 256
NCH = SEG // 32
GLA_STAGE = 4
GLA_SUB = {'A', 'O', 'S'}


def alloc_gla(p, cm):
    gs = {}
    for n in ["bA", "bB", "bC", "bD", "bE", "bF", "v32"]:
        gs[n] = p.sb("gla_" + n, [128, SEG * FC], F32)
    for n in ["qt", "ub", "G", "q32"]:
        gs[n] = p.sb("gla_" + n, [128, SEG * FC], BF16)
    gs["Xn"] = p.sb("gla_xn", [128, SEG * FC], BF16)
    gs["h"] = p.sb("gla_h", [128, SEG * FC], BF16)
    gs["vtm"] = p.sb("gla_vtm", [64, 2 * 1024], BF16)
    gs["ktm"] = p.sb("gla_ktm", [64, 2 * 1024], BF16)
    gs["Asb"] = p.sb("gla_asb", [64, 512], BF16)
    gs["hstg"] = p.sb("gla_hstg", [128, 256], F32)
    gs["i64"] = p.sb("gla_i64", [128, 64], F32)
    gs["S"] = p.sb("gla_S", [128, 512], F32)
    gs["Sbf"] = p.sb("gla_Sbf", [128, 512], BF16)
    gs["D32"] = p.sb("gla_D32", [128, 64], F32)
    gs["m16"] = p.sb("gla_m16", [128, SEG], BF16)
    gs["m32"] = p.sb("gla_m32", [128, SEG], BF16)
    gs["amask"] = p.sb("gla_amask", [32, 512], F32)
    gs["kb"] = p.sb("gla_kb", [128, 8], F32)
    gs["ob"] = p.sb("gla_ob", [128, 8], F32)
    gs["lneps"] = p.sb("gla_lneps", [128, 1], F32)
    gs["wbuf"] = [p.sb("gla_wb0", [128, 8 * 512], BF16), p.sb("gla_wb1", [128, 8 * 512], BF16)]
    return gs


def emit_gla(p, cm, ln, gs, W):
    X = cm["X"]
    p.multi("sync", [], [
        (lambda e: e.dma_start(out=gs["amask"][:], in_=W["c_amask"]), 16),
        (lambda e: e.dma_start(out=gs["kb"][:], in_=W["key_b_r"]), 16),
        (lambda e: e.dma_start(out=gs["ob"][:], in_=W["out_b_r"]), 16),
        (lambda e: e.dma_start(out=gs["i64"][:], in_=W["c_ident64"]), 16),
    ])
    p.multi("gpsimd", [], [
        (lambda e: e.dma_start(out=gs["m16"][:], in_=W["c_mask16"][:, 0:SEG]), 16),
        (lambda e: e.dma_start(out=gs["m32"][:], in_=W["c_mask32"][:, 0:SEG]), 16),
    ])
    init_sy = p.v("sy")
    init_pool = p.v("pool")
    p.multi("vector", [("dve", p.v("dve")), ("pe", p.v("pe"))], [
        (lambda e: e.memset(gs["S"][:], 0.0), 0),
        (lambda e: e.memset(gs["Sbf"][:], 0.0), 0),
        (lambda e: e.memset(gs["lneps"][:], 1e-35), 0),
        (lambda e: e.memset(p.psum[0:32, 0:512], 0.0), 0),
        (lambda e: e.memset(gs["vtm"][32:64, :], 0.0), 0),
        (lambda e: e.memset(gs["ktm"][32:64, :], 0.0), 0),
        (lambda e: e.memset(gs["Asb"][32:64, :], 0.0), 1),
    ])
    init_dve = p.v("dve")
    st = {"wb_pe": [0, 0], "A_evac": 0, "O_evac": 0, "S_evac": 0, "Sbf_cast": 0,
          "tr_evac": {4: 0, 5: 0, 6: 0, 7: 0}, "proj_evac": [0, 0], "proj_act": [0, 0]}

    def dma_w(src_ap, mh, buf, wait_pe):
        src = src_ap[:, 512 * mh:512 * (mh + 1)].rearrange("(kc p) m -> p kc m", p=128)
        dd = gs["wbuf"][buf][:].rearrange("p (kc m) -> p kc m", m=512)
        return p.step("gpsimd", [("pe", wait_pe)], lambda e: e.dma_start(out=dd, in_=src), inc=16)

    for seg in range(FC):
        t0 = SEG * seg
        emit_ln_apply(p, cm, ln, t0, SEG, gs["Xn"])
        v_xn = p.v("dve")

        d0 = dma_w(W["key_wT"], 0, 0, st["wb_pe"][0])
        d1 = dma_w(W["key_wT"], 1, 1, st["wb_pe"][1])
        dmav = {0: d0, 1: d1}

        for mc in range(FC):
            buf = mc // 4
            bk = mc % 2
            waits = [("pool", dmav[buf]), ("act", st["proj_act"][bk]), ("dve", v_xn), ("sy", init_sy),
                     ("pool", init_pool)]
            items = []
            for kc in range(FC):
                items.append((lambda e, kc=kc, buf=buf, mc=mc, bk=bk: e.matmul(
                    p.bank(bk)[:, 0:SEG],
                    gs["wbuf"][buf][:].rearrange("p (kc m) -> p kc m", m=512)[:, kc:kc + 1, 128 * (mc % 4):128 * (mc % 4 + 1)].squeeze(),
                    gs["Xn"][:, SEG * kc:SEG * (kc + 1)],
                    start=(kc == 0), stop=(kc == FC - 1)), 1 if kc == FC - 1 else 0))
            vmm = p.multi("tensor", waits, items)
            st["proj_act"][bk] = p.step("scalar", [("pe", vmm)],
                                        lambda e, mc=mc, bk=bk: e.activation(gs["bA"][:, SEG * mc:SEG * (mc + 1)],
                                                                             p.bank(bk)[:, 0:SEG], AF.Sigmoid,
                                                                             bias=gs["kb"][:, mc:mc + 1], scale=1.0),
                                        inc=1)
        for half4 in range(4):
            buf = half4 % 2
            wait_pe_users = p.v("pe")
            dmav[buf] = dma_w(W["vq_wT"], half4, buf, wait_pe_users)
            for sub in range(4):
                mc = half4 * 4 + sub
                bk = mc % 2
                dst = gs["bE"] if mc < 8 else gs["bF"]
                mco = mc % 8
                waits = [("pool", dmav[buf]), ("act", st["proj_act"][bk])]
                items = []
                for kc in range(FC):
                    items.append((lambda e, kc=kc, buf=buf, sub=sub, bk=bk: e.matmul(
                        p.bank(bk)[:, 0:SEG],
                        gs["wbuf"][buf][:].rearrange("p (kc m) -> p kc m", m=512)[:, kc:kc + 1, 128 * sub:128 * (sub + 1)].squeeze(),
                        gs["Xn"][:, SEG * kc:SEG * (kc + 1)],
                        start=(kc == 0), stop=(kc == FC - 1)), 1 if kc == FC - 1 else 0))
                vmm = p.multi("tensor", waits, items)
                st["proj_act"][bk] = p.step("scalar", [("pe", vmm)],
                                            lambda e, dst=dst, mco=mco, bk=bk: e.activation(
                                                dst[:, SEG * mco:SEG * (mco + 1)], p.bank(bk)[:, 0:SEG], AF.Copy),
                                            inc=1)
        v_proj = p.v("act")
        if GLA_STAGE < 1:
            continue

        k_, u_, es, cl32 = gs["bA"], gs["bB"], gs["bC"], gs["bD"]
        vv, q_ = gs["bE"], gs["bF"]
        dD, k32 = gs["bF"], gs["v32"]
        v1 = p.step("vector", [("act", v_proj)],
                    lambda e: e.tensor_scalar(u_[:], k_[:], -1.0, 1.0, ALU.mult, ALU.add), inc=1)
        v2 = p.step("scalar", [("dve", v1)],
                    lambda e: e.activation(es[:], k_[:], AF.Ln, bias=gs["lneps"][:], scale=1.0), inc=1)
        fns = []
        for fc in range(FC):
            fns.append(lambda e, fc=fc: e.tensor_tensor_scan(
                k_[:, SEG * fc:SEG * (fc + 1)],
                gs["m16"][:], es[:, SEG * fc:SEG * (fc + 1)], 0.0, ALU.mult, ALU.add))
        for fc in range(FC):
            fns.append(lambda e, fc=fc: e.tensor_tensor_scan(
                cl32[:, SEG * fc:SEG * (fc + 1)],
                gs["m32"][:], es[:, SEG * fc:SEG * (fc + 1)], 0.0, ALU.mult, ALU.add))
        v3 = p.chain("vector", [("act", v2)], fns)
        cl16 = k_
        v4 = p.step("scalar", [("dve", v3)], lambda e: e.activation(es[:], cl16[:], AF.Exp), inc=1)
        v5 = p.step("vector", [("act", v4)], lambda e: e.tensor_tensor(gs["qt"][:], q_[:], es[:], ALU.mult), inc=1)
        v6 = p.step("scalar", [("dve", v5)], lambda e: e.activation(es[:], cl32[:], AF.Exp), inc=1)
        v7 = p.step("vector", [("act", v6)], lambda e: e.tensor_tensor(gs["q32"][:], q_[:], es[:], ALU.mult), inc=1)
        cl32e = cl32[:].rearrange("p (a s) -> p a s", s=32)[:, :, 31]
        v8 = p.step("scalar", [("dve", v3)], lambda e: e.activation(gs["D32"][:], cl32e, AF.Exp), inc=1)
        v9 = p.step("scalar", [("dve", v7)], lambda e: e.activation(es[:], cl16[:], AF.Exp, scale=-1.0), inc=1)
        v10 = p.chain("vector", [("act", v9)], [
            lambda e: e.tensor_tensor(gs["ub"][:], u_[:], es[:], ALU.mult),
            lambda e: e.tensor_copy(gs["G"][:].rearrange("p (a s) -> p a s", s=32)[:, :, 16:32],
                                    gs["ub"][:].rearrange("p (a s) -> p a s", s=32)[:, :, 16:32]),
        ])
        cl16_blk = cl16[:].rearrange("p (a s) -> p a s", s=16)
        bcast16 = cl16_blk[:, :, 15].unsqueeze(2).broadcast_to((128, SEG * FC // 16, 16))
        v11 = p.step("vector", [("dve", v10)],
                     lambda e: e.tensor_tensor(dD[:].rearrange("p (a s) -> p a s", s=16), bcast16, cl16_blk,
                                               ALU.subtract), inc=1)
        v12 = p.step("scalar", [("dve", v11)], lambda e: e.activation(es[:], dD[:], AF.Exp), inc=1)
        v13 = p.step("vector", [("act", v12)],
                     lambda e: e.tensor_tensor(gs["G"][:].rearrange("p (a s) -> p a s", s=32)[:, :, 0:16],
                                               u_[:].rearrange("p (a s) -> p a s", s=32)[:, :, 0:16],
                                               es[:].rearrange("p (a s) -> p a s", s=32)[:, :, 0:16],
                                               ALU.mult), inc=1)
        cl32_blk = cl32[:].rearrange("p (a s) -> p a s", s=32)
        bcast32 = cl32_blk[:, :, 31].unsqueeze(2).broadcast_to((128, SEG * FC // 32, 32))
        v14 = p.step("vector", [("dve", v13)],
                     lambda e: e.tensor_tensor(dD[:].rearrange("p (a s) -> p a s", s=32), bcast32, cl32_blk,
                                               ALU.subtract), inc=1)
        v15 = p.step("scalar", [("dve", v14)], lambda e: e.activation(es[:], dD[:], AF.Exp), inc=1)
        v16 = p.step("vector", [("act", v15)], lambda e: e.tensor_tensor(k32[:], u_[:], es[:], ALU.mult), inc=1)
        v_prep = p.v("dve")
        if GLA_STAGE < 2:
            continue

        for c in range(NCH):
            sl = c % 2
            so = 1024 * sl
            for bgrp, r0 in ((0, 0), (1, 64)):
                items = []
                for (tsrc, pb) in ((vv, 4 + bgrp), (k32, 6 + bgrp)):
                    for fc in range(FC):
                        items.append((lambda e, fc=fc, c=c, tsrc=tsrc, pb=pb, r0=r0: e.transpose(
                            p.psum[0:32, 512 * pb + 64 * fc:512 * pb + 64 * (fc + 1)],
                            tsrc[r0:r0 + 64, SEG * fc + 32 * c:SEG * fc + 32 * (c + 1)],
                            gs["i64"][r0:r0 + 64, 0:64],
                            tile_position=(r0, 0)),
                            1 if (tsrc is k32 and fc == FC - 1) else 0))
                w = [("dve", st["tr_evac"][4 + bgrp]), ("pe", p.v("pe"))]
                if c == 0:
                    w.append(("dve", v_prep))
                vt = p.multi("tensor", w, items)
                ev_items = []
                for (packs, pb) in (("vtm", 4 + bgrp), ("ktm", 6 + bgrp)):
                    ev_items.append((lambda e, so=so, pb=pb, packs=packs, bgrp=bgrp:
                                     e.tensor_copy(gs[packs][0:32, so + 512 * bgrp:so + 512 * (bgrp + 1)],
                                                   p.psum[0:32, 512 * pb:512 * (pb + 1)]), 1))
                st["tr_evac"][4 + bgrp] = p.multi("vector", [("pe", vt), ("pe", st.setdefault("slot_pe", [0, 0])[sl])],
                                                  ev_items)
            v_packs = p.v("dve")
            vtm = gs["vtm"]
            ktm = gs["ktm"]
            if "A" not in GLA_SUB:
                continue
            vA = 0
            for hw in range(2):
                par = 64 * hw
                co = 256 * hw
                items = []
                for hh in range(8):
                    fc = hh
                    lo = SEG * fc + 32 * c
                    items.append((lambda e, hh=hh, par=par, lo=lo, co=co: e.matmul(
                        p.psum[0:16, co + 32 * hh:co + 32 * hh + 16],
                        gs["ub"][par:par + 64, lo:lo + 16],
                        gs["qt"][par:par + 64, lo:lo + 16], start=True, stop=True,
                        tile_position=(par, 0)), 0))
                    items.append((lambda e, hh=hh, par=par, lo=lo, co=co: e.matmul(
                        p.psum[0:32, co + 32 * hh + 16:co + 32 * hh + 32],
                        gs["G"][par:par + 64, lo:lo + 32],
                        gs["qt"][par:par + 64, lo + 16:lo + 32], start=True, stop=True,
                        tile_position=(par, 0)),
                        1 if hh == 7 else 0))
                w = [("pe", p.v("pe"))]
                if hw == 0:
                    w += [("dve", st["A_evac"] if st["A_evac"] else init_dve), ("dve", v_packs)]
                vA = p.multi("tensor", w, items)
            st["A_evac"] = p.step("vector", [("pe", vA)],
                                  lambda e: e.tensor_tensor(gs["Asb"][0:32, :], p.psum[0:32, 0:512],
                                                            gs["amask"][:], ALU.mult), inc=1)
            if "O" not in GLA_SUB:
                continue
            items = []
            for hh in range(8):
                lo = SEG * hh + 32 * c
                items.append((lambda e, hh=hh, so=so: e.matmul(
                    p.bank(2)[0:64, 32 * hh:32 * (hh + 1)],
                    vtm[0:64, so + 64 * hh:so + 64 * (hh + 1)],
                    gs["Asb"][0:64, 32 * hh:32 * (hh + 1)], start=True, stop=False,
                    tile_position=(0, 0)), 0))
                items.append((lambda e, hh=hh, lo=lo: e.matmul(
                    p.bank(2)[0:64, 32 * hh:32 * (hh + 1)],
                    gs["Sbf"][0:64, 64 * hh:64 * (hh + 1)],
                    gs["q32"][0:64, lo:lo + 32], start=False, stop=True,
                    tile_position=(0, 0)),
                    0))
            for hh in range(8):
                items.append((lambda e, hh=hh, so=so: e.matmul(
                    p.bank(2)[64:128, 32 * hh:32 * (hh + 1)],
                    vtm[0:64, so + 512 + 64 * hh:so + 512 + 64 * (hh + 1)],
                    gs["Asb"][0:64, 256 + 32 * hh:256 + 32 * (hh + 1)], start=True, stop=True,
                    tile_position=(0, 64)), 1))
            w = [("dve", st["A_evac"]), ("dve", st["O_evac"]), ("dve", st["Sbf_cast"]), ("pe", p.v("pe"))]
            vO1 = p.multi("tensor", w, items)
            items = []
            for hh in range(8):
                lo = SEG * hh + 32 * c
                items.append((lambda e, hh=hh, lo=lo: e.matmul(
                    p.bank(2)[64:128, 256 + 32 * hh:256 + 32 * (hh + 1)],
                    gs["Sbf"][64:128, 64 * hh:64 * (hh + 1)],
                    gs["q32"][64:128, lo:lo + 32], start=True, stop=True,
                    tile_position=(64, 64)),
                    1 if hh == 7 else 0))
            vO = p.multi("tensor", [("pe", vO1)], items)
            st["O_evac"] = p.chain("vector", [("pe", vO)], [
                lambda e, c=c: e.tensor_copy(
                    gs["h"][0:64, :].rearrange("p (a s) -> p a s", s=SEG)[:, :, 32 * c:32 * (c + 1)],
                    p.bank(2)[0:64, 0:256].rearrange("p (a s) -> p a s", s=32)),
                lambda e: e.tensor_copy(gs["hstg"][64:128, :], p.bank(2)[64:128, 0:256]),
                lambda e, c=c: e.tensor_tensor(
                    gs["h"][64:128, :].rearrange("p (a s) -> p a s", s=SEG)[:, :, 32 * c:32 * (c + 1)],
                    gs["hstg"][64:128, :].rearrange("p (a s) -> p a s", s=32),
                    p.bank(2)[64:128, 256:512].rearrange("p (a s) -> p a s", s=32), ALU.add),
            ])
            if "S" not in GLA_SUB:
                continue
            items = []
            for hw in range(2):
                for hh in range(8):
                    items.append((lambda e, hh=hh, hw=hw, so=so: e.matmul(
                        p.bank(3)[64 * hw:64 * hw + 64, 64 * hh:64 * (hh + 1)],
                        ktm[0:64, so + 512 * hw + 64 * hh:so + 512 * hw + 64 * (hh + 1)],
                        vtm[0:64, so + 512 * hw + 64 * hh:so + 512 * hw + 64 * (hh + 1)],
                        start=True, stop=True, tile_position=(0, 64 * hw)),
                        1 if (hw == 1 and hh == 7) else 0))
            vS = p.multi("tensor", [("dve", st["S_evac"]), ("pe", p.v("pe"))], items)
            fns = []
            for hw in range(2):
                par = 64 * hw
                for hh in range(8):
                    fns.append(lambda e, hh=hh, par=par, c=c: e.scalar_tensor_tensor(
                        gs["S"][par:par + 64, 64 * hh:64 * (hh + 1)], gs["S"][par:par + 64, 64 * hh:64 * (hh + 1)],
                        gs["D32"][par:par + 64, 8 * hh + c:8 * hh + c + 1],
                        p.bank(3)[par:par + 64, 64 * hh:64 * (hh + 1)],
                        ALU.mult, ALU.add))
            fns.append(lambda e: e.tensor_copy(gs["Sbf"][:], gs["S"][:]))
            st["S_evac"] = p.chain("vector", [("pe", vS)], fns)
            st["Sbf_cast"] = st["S_evac"]
            st["slot_pe"][sl] = p.v("pe")

        v_h = p.v("dve")
        if GLA_STAGE < 3:
            continue
        d0 = dma_w(W["out_wT"], 0, 0, p.v("pe"))
        d1 = dma_w(W["out_wT"], 1, 1, p.v("pe"))
        dmav = {0: d0, 1: d1}
        for mc in range(FC):
            buf = mc // 4
            bk = mc % 2
            waits = [("pool", dmav[buf]), ("dve", st["proj_evac"][bk]), ("act", st["proj_act"][bk])]
            if mc == 0:
                waits.append(("dve", v_h))
            items = []
            for kc in range(FC):
                items.append((lambda e, kc=kc, buf=buf, mc=mc, bk=bk: e.matmul(
                    p.bank(bk)[:, 0:SEG],
                    gs["wbuf"][buf][:].rearrange("p (kc m) -> p kc m", m=512)[:, kc:kc + 1, 128 * (mc % 4):128 * (mc % 4 + 1)].squeeze(),
                    gs["h"][:, SEG * kc:SEG * (kc + 1)],
                    start=(kc == 0), stop=(kc == FC - 1)), 1 if kc == FC - 1 else 0))
            vmm = p.multi("tensor", waits, items)
            st["proj_evac"][bk] = p.step("vector", [("pe", vmm)],
                                         lambda e, mc=mc, bk=bk, t0=t0: e.scalar_tensor_tensor(
                                             X[:, T * mc + t0:T * mc + t0 + SEG],
                                             p.bank(bk)[:, 0:SEG], gs["ob"][:, mc:mc + 1],
                                             X[:, T * mc + t0:T * mc + t0 + SEG], ALU.add, ALU.add), inc=1)
            st["proj_act"][bk] = st["proj_act"][bk]
        st["wb_pe"] = [p.v("pe"), p.v("pe")]
    return



SEG1 = 512
LAMBDA_INIT = 0.8 - 0.6 * float(np.exp(-0.3 * 12))


def alloc_attn1(p, cm):
    at = {}
    at["Xn"] = p.sb("at_xn", [128, FC * SEG1], BF16)
    at["q"] = p.sb("at_q", [128, 4 * D], F32)
    at["k"] = p.sb("at_k", [128, 4 * D], F32)
    at["v"] = p.sb("at_v", [128, 4 * D], BF16)
    at["attn"] = p.sb("at_attn", [128, 4 * D], F32)
    at["af"] = p.sb("at_af", [128, FC * SEG1], BF16)
    at["scr"] = p.sb("at_scr", [128, D], F32)
    at["scr2"] = p.sb("at_scr2", [128, D], F32)
    at["scr3"] = p.sb("at_scr3", [128, D], F32)
    at["sc"] = p.sb("at_sc", [128, 128], F32)
    at["mx"] = p.sb("at_mx", [128, 16], F32)
    at["rstd"] = p.sb("at_rstd", [128, 8], F32)
    at["wm"] = p.sb("at_wm", [128, 64], F32)
    at["cs_cos"] = p.sb("at_cos", [128, 4 * 64], F32)
    at["cs_sin"] = p.sb("at_sin", [128, 4 * 64], F32)
    at["lnw"] = p.sb("at_lnw", [128, 128], F32)
    at["lnb"] = p.sb("at_lnb", [128, 128], F32)
    at["lam"] = p.sb("at_lam", [1, 256], F32)
    at["lams"] = p.sb("at_lams", [1, 128], F32)
    at["neglam"] = p.sb("at_neglam", [128, 1], F32)
    at["eps8"] = p.sb("at_eps8", [128, 1], F32)
    at["wbuf"] = [p.sb("at_wb0", [128, 8 * 512], BF16), p.sb("at_wb1", [128, 8 * 512], BF16)]
    return at


def emit_attn1(p, cm, ln, at, W):
    X = cm["X"]
    HDQ = float(64.0 ** -0.25)
    p.multi("sync", [], [
        (lambda e: e.dma_start(out=at["lam"][0:1, 0:64], in_=W["lambda_q1"].unsqueeze(0)), 16),
        (lambda e: e.dma_start(out=at["lam"][0:1, 64:128], in_=W["lambda_k1"].unsqueeze(0)), 16),
        (lambda e: e.dma_start(out=at["lam"][0:1, 128:192], in_=W["lambda_q2"].unsqueeze(0)), 16),
        (lambda e: e.dma_start(out=at["lam"][0:1, 192:256], in_=W["lambda_k2"].unsqueeze(0)), 16),
        (lambda e: e.dma_start(out=at["lnw"][:], in_=W["ln_w_rep"]), 16),
        (lambda e: e.dma_start(out=at["lnb"][:], in_=W["ln_b_rep"]), 16),
    ])
    v_ld = p.v("sy")
    vx = p.chain("vector", [("sy", v_ld), ("pe", p.v("pe")), ("dve", p.v("dve"))], [
        lambda e: e.memset(at["eps8"][:], 1e-8),
        lambda e: e.tensor_tensor(at["lams"][0:1, 0:64], at["lam"][0:1, 0:64], at["lam"][0:1, 64:128], ALU.mult),
        lambda e: e.tensor_tensor(at["lams"][0:1, 64:128], at["lam"][0:1, 128:192], at["lam"][0:1, 192:256], ALU.mult),
        lambda e: e.tensor_reduce(at["lams"][0:1, 0:2], at["lams"][0:1, 0:128].rearrange("p (a d) -> p a d", d=64),
                                  AX.X, ALU.add),
    ])
    va = p.step("scalar", [("dve", vx)], lambda e: e.activation(at["lams"][0:1, 4:6], at["lams"][0:1, 0:2], AF.Exp),
                inc=1)
    vx = p.chain("vector", [("act", va)], [
        lambda e: e.tensor_tensor(at["lams"][0:1, 8:9], at["lams"][0:1, 4:5], at["lams"][0:1, 5:6], ALU.subtract),
        lambda e: e.tensor_scalar(at["lams"][0:1, 8:9], at["lams"][0:1, 8:9], LAMBDA_INIT, -1.0,
                                  ALU.add, ALU.mult),
        lambda e: e.tensor_scalar(at["lnw"][:], at["lnw"][:], 1.0 - LAMBDA_INIT, None, ALU.mult, ALU.bypass),
        lambda e: e.tensor_scalar(at["lnb"][:], at["lnb"][:], 1.0 - LAMBDA_INIT, None, ALU.mult, ALU.bypass),
    ])
    vneg = p.step("tensor", [("dve", vx)],
                  lambda e: e.matmul(p.bank(2)[:, 0:1], cm["ones_row"][:], at["lams"][0:1, 8:9],
                                     start=True, stop=True), inc=1)
    v_lam = p.step("vector", [("pe", vneg)], lambda e: e.tensor_copy(at["neglam"][:], p.bank(2)[:, 0:1]), inc=1)

    st = {"bk_act": [0, 0], "bk_dve": [0, 0], "wb_pe": [0, 0], "tr_dve": {6: 0, 7: 0}}

    def dma_w(src_ap, mh, buf):
        src = src_ap[:, 512 * mh:512 * (mh + 1)].rearrange("(kc p) m -> p kc m", p=128)
        dd = at["wbuf"][buf][:].rearrange("p (kc m) -> p kc m", m=512)
        return p.step("gpsimd", [("pe", st["wb_pe"][buf])], lambda e: e.dma_start(out=dd, in_=src), inc=16)

    for seg in range(4):
        t0 = SEG1 * seg
        emit_ln_apply(p, cm, ln, t0, SEG1, at["Xn"])
        v_xn = p.v("dve")
        v_cs = p.multi("gpsimd", [("dve", st["tr_dve"][6])], [
            (lambda e: e.dma_start(out=at["cs_cos"][:].rearrange("p (a d) -> p a d", d=64),
                                   in_=W["c_cos"][t0:t0 + SEG1, :].rearrange("(a p) d -> p a d", p=128)), 16),
            (lambda e: e.dma_start(out=at["cs_sin"][:].rearrange("p (a d) -> p a d", d=64),
                                   in_=W["c_sin"][t0:t0 + SEG1, :].rearrange("(a p) d -> p a d", p=128)), 16),
        ])
        for mi, (wd, dst, dt_) in enumerate(((W["Wq"], at["q"], F32), (W["Wk"], at["k"], F32),
                                             (W["Wv"], at["v"], BF16))):
            d0 = dma_w(wd, 0, 0)
            d1 = dma_w(wd, 1, 1)
            dmav = {0: d0, 1: d1}
            for tt in range(4):
                for win in range(2):
                    bk = (tt * 2 + win) % 2
                    waits = [("pool", dmav[win]), ("dve", st["bk_dve"][bk]), ("act", st["bk_act"][bk])]
                    if mi == 0 and tt == 0 and win == 0:
                        waits.append(("dve", v_xn))
                    items = []
                    for kc in range(FC):
                        items.append((lambda e, kc=kc, tt=tt, win=win, bk=bk: e.matmul(
                            p.bank(bk)[:, 0:512],
                            at["Xn"][:, SEG1 * kc + 128 * tt:SEG1 * kc + 128 * (tt + 1)],
                            at["wbuf"][win][:].rearrange("p (kc m) -> p kc m", m=512)[:, kc:kc + 1, :].squeeze(),
                            start=(kc == 0), stop=(kc == FC - 1)), 1 if kc == FC - 1 else 0))
                    vmm = p.multi("tensor", waits, items)
                    st["bk_dve"][bk] = p.step("vector", [("pe", vmm)],
                                              lambda e, tt=tt, win=win, bk=bk, dst=dst: e.tensor_copy(
                                                  dst[:, D * tt + 512 * win:D * tt + 512 * (win + 1)],
                                                  p.bank(bk)[:, 0:512]), inc=1)
            st["wb_pe"] = [p.v("pe"), p.v("pe")]
        v_qkv = p.v("dve")

        for tt in range(4):
            q_t = at["q"][:, D * tt:D * (tt + 1)]
            k_t = at["k"][:, D * tt:D * (tt + 1)]
            v_t = at["v"][:, D * tt:D * (tt + 1)]
            a_t = at["attn"][:, D * tt:D * (tt + 1)]
            cosAP = at["cs_cos"][:, 64 * tt:64 * (tt + 1)].unsqueeze(1).broadcast_to((128, 16, 64))
            sinAP = at["cs_sin"][:, 64 * tt:64 * (tt + 1)].unsqueeze(1).broadcast_to((128, 16, 64))
            for rt in (q_t, k_t):
                rview = rt.rearrange("p (a two) -> p a two", two=2)
                p.chain("vector", [("dve", p.v("dve")), ("sy", p.v("sy")), ("pool", p.v("pool"))], [
                    lambda e, rt=rt: e.tensor_scalar(at["scr"][:].rearrange("p (a two) -> p a two", two=2)[:, :, 0:1],
                                                     rt.rearrange("p (a two) -> p a two", two=2)[:, :, 1:2],
                                                     -1.0, None, ALU.mult, ALU.bypass),
                    lambda e, rt=rt: e.tensor_copy(at["scr"][:].rearrange("p (a two) -> p a two", two=2)[:, :, 1:2],
                                                   rt.rearrange("p (a two) -> p a two", two=2)[:, :, 0:1]),
                    lambda e, rt=rt: e.tensor_tensor(at["scr2"][:].rearrange("p (hh d) -> p hh d", d=64),
                                                     rt.rearrange("p (hh d) -> p hh d", d=64), cosAP, ALU.mult),
                    lambda e: e.tensor_tensor(at["scr"][:].rearrange("p (hh d) -> p hh d", d=64),
                                              at["scr"][:].rearrange("p (hh d) -> p hh d", d=64), sinAP, ALU.mult),
                    lambda e, rt=rt: e.tensor_tensor(rt, at["scr2"][:], at["scr"][:], ALU.add),
                ])
            items = []
            for g in range(2):
                for n in range(8):
                    for m in range(8):
                        col = 64 * g + 8 * n + m
                        qs = q_t[:, 64 * (2 * n + g):64 * (2 * n + g + 1)]
                        ks = k_t[:, 64 * (2 * m + g):64 * (2 * m + g + 1)]
                        items.append((lambda e, qs=qs, ks=ks, col=col: e.scalar_tensor_tensor(
                            at["scr"][:, 0:64], qs, HDQ, ks, ALU.mult, ALU.mult,
                            accum_out=at["sc"][:, col:col + 1]), 1))
            p.multi("vector", [("dve", p.v("dve"))], items)
            scv = at["sc"][:].rearrange("p (gn m) -> p gn m", m=8)
            vsm = p.chain("vector", [], [
                lambda e: e.tensor_reduce(at["mx"][:], scv, AX.X, ALU.max),
                lambda e: e.tensor_scalar(at["mx"][:], at["mx"][:], -1.0, None, ALU.mult, ALU.bypass),
                lambda e: e.tensor_tensor(scv, scv, at["mx"][:].unsqueeze(2).broadcast_to((128, 16, 8)), ALU.add),
            ])
            vexp = p.step("scalar", [("dve", vsm)], lambda e: e.activation(at["sc"][:], at["sc"][:], AF.Exp), inc=1)
            vsm = p.chain("vector", [("act", vexp)], [
                lambda e: e.tensor_reduce(at["mx"][:], scv, AX.X, ALU.add),
                lambda e: e.reciprocal(at["mx"][:], at["mx"][:]),
                lambda e: e.tensor_tensor(scv, scv, at["mx"][:].unsqueeze(2).broadcast_to((128, 16, 8)), ALU.mult),
                lambda e: e.scalar_tensor_tensor(at["wm"][:], at["sc"][:, 64:128], at["neglam"][:, 0:1],
                                                 at["sc"][:, 0:64], ALU.mult, ALU.add),
            ])
            for n in range(8):
                fns = [(lambda e, n=n: e.tensor_scalar(a_t[:, 128 * n:128 * (n + 1)],
                                                       v_t[:, 0:128], at["wm"][:, 8 * n:8 * n + 1],
                                                       None, ALU.mult, ALU.bypass))]
                for m in range(1, 8):
                    fns.append(lambda e, n=n, m=m: e.scalar_tensor_tensor(
                        a_t[:, 128 * n:128 * (n + 1)], v_t[:, 128 * m:128 * (m + 1)],
                        at["wm"][:, 8 * n + m:8 * n + m + 1], a_t[:, 128 * n:128 * (n + 1)],
                        ALU.mult, ALU.add))
                p.chain("vector", [] if n else [("dve", vsm)], fns)
            vsq = p.step("scalar", [("dve", p.v("dve"))],
                         lambda e: e.activation(at["scr"][:], a_t, AF.Square), inc=1)
            vr = p.chain("vector", [("act", vsq)], [
                lambda e: e.tensor_reduce(at["rstd"][:], at["scr"][:].rearrange("p (n e2) -> p n e2", e2=128),
                                          AX.X, ALU.add),
                lambda e: e.tensor_scalar(at["rstd"][:], at["rstd"][:], 1.0 / 128, None, ALU.mult, ALU.bypass),
            ])
            vsq = p.step("scalar", [("dve", vr)],
                         lambda e: e.activation(at["rstd"][:], at["rstd"][:], AF.Sqrt, bias=at["eps8"][:],
                                                scale=1.0), inc=1)
            p.chain("vector", [("act", vsq)], [
                lambda e: e.reciprocal(at["rstd"][:], at["rstd"][:]),
                lambda e: e.tensor_tensor(a_t.rearrange("p (n e2) -> p n e2", e2=128),
                                          a_t.rearrange("p (n e2) -> p n e2", e2=128),
                                          at["rstd"][:].unsqueeze(2).broadcast_to((128, 8, 128)), ALU.mult),
                lambda e: e.tensor_tensor(a_t.rearrange("p (n e2) -> p n e2", e2=128),
                                          a_t.rearrange("p (n e2) -> p n e2", e2=128),
                                          at["lnw"][:].unsqueeze(1).broadcast_to((128, 8, 128)), ALU.mult),
                lambda e: e.tensor_tensor(a_t.rearrange("p (n e2) -> p n e2", e2=128),
                                          a_t.rearrange("p (n e2) -> p n e2", e2=128),
                                          at["lnb"][:].unsqueeze(1).broadcast_to((128, 8, 128)), ALU.add),
            ])
        v_attn = p.v("dve")

        for tt in range(4):
            for fc in range(FC):
                pb = 6 + (fc % 2)
                w = [("dve", v_attn if (tt == 0 and fc == 0) else 0), ("dve", st["tr_dve"][pb])]
                vt = p.step("tensor", w,
                            lambda e, tt=tt, fc=fc, pb=pb: e.transpose(
                                p.bank(pb)[:, 0:128],
                                at["attn"][:, D * tt + 128 * fc:D * tt + 128 * (fc + 1)],
                                cm["ident"][:]), inc=1)
                st["tr_dve"][pb] = p.step("vector", [("pe", vt)],
                                          lambda e, tt=tt, fc=fc, pb=pb: e.tensor_copy(
                                              at["af"][:, SEG1 * fc + 128 * tt:SEG1 * fc + 128 * (tt + 1)],
                                              p.bank(pb)[:, 0:128]), inc=1)
        v_af = p.v("dve")

        d0 = dma_w(W["Wo"], 0, 0)
        d1 = dma_w(W["Wo"], 1, 1)
        dmav = {0: d0, 1: d1}
        for mc in range(FC):
            buf = mc // 4
            bk = mc % 2
            waits = [("pool", dmav[buf]), ("dve", st["bk_dve"][bk])]
            if mc == 0:
                waits.append(("dve", v_af))
            items = []
            for kc in range(FC):
                items.append((lambda e, kc=kc, mc=mc, bk=bk, buf=buf: e.matmul(
                    p.bank(bk)[:, 0:512],
                    at["wbuf"][buf][:].rearrange("p (kc m) -> p kc m", m=512)[:, kc:kc + 1, 128 * (mc % 4):128 * (mc % 4 + 1)].squeeze(),
                    at["af"][:, SEG1 * kc:SEG1 * (kc + 1)],
                    start=(kc == 0), stop=(kc == FC - 1)), 1 if kc == FC - 1 else 0))
            vmm = p.multi("tensor", waits, items)
            st["bk_dve"][bk] = p.step("vector", [("pe", vmm)],
                                      lambda e, mc=mc, bk=bk, t0=t0: e.tensor_tensor(
                                          X[:, T * mc + t0:T * mc + t0 + SEG1],
                                          X[:, T * mc + t0:T * mc + t0 + SEG1],
                                          p.bank(bk)[:, 0:512], ALU.add), inc=1)
        st["wb_pe"] = [p.v("pe"), p.v("pe")]
    return



def build_ffn_program(layer, x_in_kind, x_out_kind="fm"):
    p = Prog(f"ffn{layer}")
    if x_in_kind == "tm":
        x_in = p.din("x_in", [T, D])
    else:
        x_in = p.din("x_in", [D, T])
    x_out = p.dout("x_out", [T, D] if x_out_kind == "tm" else [D, T])
    ident = p.din("c_ident", [128, 128])
    if layer == 0:
        rw = p.din("L0_ffn_w_router", [D, NE])
        w1 = p.din("L0_ffn_w1", [NE, D, 2816])
        w2r = p.din("w2r_l0", [NE, FC, 128, 22, 128])
        w3 = p.din("L0_ffn_w3", [NE, D, 2816])
        experts = [{"w1": w1[e:e + 1].squeeze(), "w3": w3[e:e + 1].squeeze(), "w2r": w2r[e:e + 1].squeeze()}
                   for e in range(NE)]
        cfg = MoECfg(2816, experts, rw, 0.0, 0.5)
        maxI = 2816
    else:
        rw = p.din("L1_ffn_Wr", [D, NE])
        wg = p.din("L1_ffn_Wg", [NE, D, 2048])
        wu = p.din("L1_ffn_Wu", [NE, D, 2048])
        wdr = p.din("w2r_l1", [NE, FC, 128, 16, 128])
        sg = p.din("L1_ffn_Sg", [D, 2048])
        su = p.din("L1_ffn_Su", [D, 2048])
        sdr = p.din("w2r_sd", [FC, 128, 16, 128])
        experts = [{"w1": wg[e:e + 1].squeeze(), "w3": wu[e:e + 1].squeeze(), "w2r": wdr[e:e + 1].squeeze()}
                   for e in range(NE)]
        experts.append({"w1": sg, "w3": su, "w2r": sdr, "shared": True})
        cfg = MoECfg(2048, experts, rw, 1e-20, 0.5)
        maxI = 2048
    p.start()
    cm = alloc_common(p)
    ln = alloc_ln(p, cm)
    ms = alloc_moe(p, cm, maxI)
    emit_init_consts(p, cm, ident)
    if x_in_kind == "tm":
        emit_load_x_tm(p, cm, x_in)
    else:
        emit_load_x_fm(p, cm, x_in)
    emit_ln_stats(p, cm, ln)
    emit_moe(p, cm, ln, ms, cfg)
    if x_out_kind == "tm":
        emit_store_x_tm(p, cm, x_out)
    else:
        emit_store_x_fm(p, cm, x_out)
    p.finalize()
    return p


def build_gla_program(x_in_kind="fm"):
    p = Prog("gla")
    x_in = p.din("x_in", [T, D] if x_in_kind == "tm" else [D, T])
    x_out = p.dout("x_out", [D, T])
    ident = p.din("c_ident", [128, 128])
    W = {}
    W["c_amask"] = p.din("c_amask", [32, 512])
    W["c_mask16"] = p.din("c_mask16", [128, T])
    W["c_mask32"] = p.din("c_mask32", [128, T])
    W["key_wT"] = p.din("key_wT", [D, D])
    W["vq_wT"] = p.din("vq_wT", [D, 2 * D])
    W["out_wT"] = p.din("out_wT", [D, D])
    W["key_b_r"] = p.din("key_b_r", [128, 8])
    W["out_b_r"] = p.din("out_b_r", [128, 8])
    W["c_ident64"] = p.din("c_ident64", [128, 64])
    p.start()
    cm = alloc_common(p)
    ln = alloc_ln(p, cm)
    gs = alloc_gla(p, cm)
    emit_init_consts(p, cm, ident)
    if x_in_kind == "tm":
        emit_load_x_tm(p, cm, x_in)
    else:
        emit_load_x_fm(p, cm, x_in)
    emit_ln_stats(p, cm, ln)
    emit_gla(p, cm, ln, gs, W)
    emit_store_x_fm(p, cm, x_out)
    p.finalize()
    return p


def gla_perm():
    perm = np.zeros(1024, np.int64)
    i = 0
    for fc in range(8):
        for s in range(2):
            for e in range(64):
                perm[i] = 64 * (fc + 8 * s) + e
                i += 1
    return perm


def gla_host_inputs(inp):
    pm = gla_perm()
    f = {}
    f["key_wT"] = np.ascontiguousarray(inp["L0_attn_key_w"].T[:, pm])
    vqT = inp["L0_attn_vq_w"].T
    f["vq_wT"] = np.ascontiguousarray(np.concatenate([vqT[:, :1024][:, pm], vqT[:, 1024:][:, pm]], axis=1))
    f["out_wT"] = np.ascontiguousarray(inp["L0_attn_out_w"].T[pm, :])
    f["key_b_r"] = np.ascontiguousarray(inp["L0_attn_key_b"][pm].reshape(8, 128).T)
    f["out_b_r"] = np.ascontiguousarray(inp["L0_attn_out_b"].reshape(8, 128).T)
    return f



def build_attn1_program():
    p = Prog("attn1")
    x_in = p.din("x_in", [D, T])
    x_out = p.dout("x_out", [D, T])
    ident = p.din("c_ident", [128, 128])
    W = {}
    W["Wq"] = p.din("L1_attn_Wq", [D, D])
    W["Wk"] = p.din("L1_attn_Wk", [D, D])
    W["Wv"] = p.din("L1_attn_Wv", [D, D])
    W["Wo"] = p.din("L1_attn_Wo", [D, D])
    W["lambda_q1"] = p.din("L1_attn_lambda_q1", [64])
    W["lambda_k1"] = p.din("L1_attn_lambda_k1", [64])
    W["lambda_q2"] = p.din("L1_attn_lambda_q2", [64])
    W["lambda_k2"] = p.din("L1_attn_lambda_k2", [64])
    W["ln_w_rep"] = p.din("ln_w_rep", [128, 128])
    W["ln_b_rep"] = p.din("ln_b_rep", [128, 128])
    W["c_cos"] = p.din("c_cos", [T, 64])
    W["c_sin"] = p.din("c_sin", [T, 64])
    p.start()
    cm = alloc_common(p)
    ln = alloc_ln(p, cm)
    at = alloc_attn1(p, cm)
    emit_init_consts(p, cm, ident)
    emit_load_x_fm(p, cm, x_in)
    emit_ln_stats(p, cm, ln)
    emit_attn1(p, cm, ln, at, W)
    emit_store_x_fm(p, cm, x_out)
    p.finalize()
    return p


def attn1_host_inputs(inp):
    return {"ln_w_rep": np.ascontiguousarray(np.broadcast_to(inp["L1_attn_ln_w"], (128, 128))),
            "ln_b_rep": np.ascontiguousarray(np.broadcast_to(inp["L1_attn_ln_b"], (128, 128)))}



def build_fused_program():
    p = Prog("fused")
    x_in = p.din("x_in", [T, D])
    x_out = p.dout("x_out", [T, D])
    ident = p.din("c_ident", [128, 128])
    rw0 = p.din("L0_ffn_w_router", [D, NE])
    w1_0 = p.din("L0_ffn_w1", [NE, D, 2816])
    w2r_0 = p.din("w2r_l0", [NE, FC, 128, 22, 128])
    w3_0 = p.din("L0_ffn_w3", [NE, D, 2816])
    experts0 = [{"w1": w1_0[e:e + 1].squeeze(), "w3": w3_0[e:e + 1].squeeze(), "w2r": w2r_0[e:e + 1].squeeze()}
                for e in range(NE)]
    cfg0 = MoECfg(2816, experts0, rw0, 0.0, 0.5)
    rw1 = p.din("L1_ffn_Wr", [D, NE])
    wg = p.din("L1_ffn_Wg", [NE, D, 2048])
    wu = p.din("L1_ffn_Wu", [NE, D, 2048])
    wdr = p.din("w2r_l1", [NE, FC, 128, 16, 128])
    sg = p.din("L1_ffn_Sg", [D, 2048])
    su = p.din("L1_ffn_Su", [D, 2048])
    sdr = p.din("w2r_sd", [FC, 128, 16, 128])
    experts1 = [{"w1": wg[e:e + 1].squeeze(), "w3": wu[e:e + 1].squeeze(), "w2r": wdr[e:e + 1].squeeze()}
                for e in range(NE)]
    experts1.append({"w1": sg, "w3": su, "w2r": sdr, "shared": True})
    cfg1 = MoECfg(2048, experts1, rw1, 1e-20, 0.5)
    GW = {}
    GW["c_amask"] = p.din("c_amask", [32, 512])
    GW["c_mask16"] = p.din("c_mask16", [128, T])
    GW["c_mask32"] = p.din("c_mask32", [128, T])
    GW["key_wT"] = p.din("key_wT", [D, D])
    GW["vq_wT"] = p.din("vq_wT", [D, 2 * D])
    GW["out_wT"] = p.din("out_wT", [D, D])
    GW["key_b_r"] = p.din("key_b_r", [128, 8])
    GW["out_b_r"] = p.din("out_b_r", [128, 8])
    GW["c_ident64"] = p.din("c_ident64", [128, 64])
    AW = {}
    AW["Wq"] = p.din("L1_attn_Wq", [D, D])
    AW["Wk"] = p.din("L1_attn_Wk", [D, D])
    AW["Wv"] = p.din("L1_attn_Wv", [D, D])
    AW["Wo"] = p.din("L1_attn_Wo", [D, D])
    AW["lambda_q1"] = p.din("L1_attn_lambda_q1", [64])
    AW["lambda_k1"] = p.din("L1_attn_lambda_k1", [64])
    AW["lambda_q2"] = p.din("L1_attn_lambda_q2", [64])
    AW["lambda_k2"] = p.din("L1_attn_lambda_k2", [64])
    AW["ln_w_rep"] = p.din("ln_w_rep", [128, 128])
    AW["ln_b_rep"] = p.din("ln_b_rep", [128, 128])
    AW["c_cos"] = p.din("c_cos", [T, 64])
    AW["c_sin"] = p.din("c_sin", [T, 64])

    p.start()
    cm = alloc_common(p)
    ln = alloc_ln(p, cm)
    emit_init_consts(p, cm, ident)
    p.push_scope()
    emit_load_x_tm(p, cm, x_in)
    p.barrier()
    p.pop_scope()
    p.new_epoch()

    def section(emitfn, allocfn):
        p.push_scope()
        buf = allocfn()
        emit_ln_stats(p, cm, ln)
        emitfn(buf)
        p.barrier()
        p.pop_scope()
        p.new_epoch()

    section(lambda ms: emit_moe(p, cm, ln, ms, cfg0), lambda: alloc_moe(p, cm, 2816))
    section(lambda gs: emit_gla(p, cm, ln, gs, GW), lambda: alloc_gla(p, cm))
    section(lambda ms: emit_moe(p, cm, ln, ms, cfg0), lambda: alloc_moe(p, cm, 2816))
    section(lambda ms: emit_moe(p, cm, ln, ms, cfg1), lambda: alloc_moe(p, cm, 2048))
    section(lambda at: emit_attn1(p, cm, ln, at, AW), lambda: alloc_attn1(p, cm))
    section(lambda ms: emit_moe(p, cm, ln, ms, cfg1), lambda: alloc_moe(p, cm, 2048))

    p.push_scope()
    emit_store_x_tm(p, cm, x_out)
    p.pop_scope()
    p.finalize()
    return p


def kernel_fused(inputs):
    consts = host_constants()
    extra = host_extra(inputs)
    x = inputs["L0_attn_x"]
    prog = build_fused_program()
    in_maps = []
    for i in range(8):
        in_maps.append(make_feeds(prog, inputs, consts, extra, np.ascontiguousarray(x[i])))
    res = run_bass_kernel_spmd(prog.nc, in_maps, core_ids=list(range(8)))
    return np.stack([np.asarray(r["x_out"]) for r in res.results], axis=0).astype(np.float32)



def _run(p: Prog, per_core_inputs):
    res = run_bass_kernel_spmd(p.nc, per_core_inputs, core_ids=list(range(len(per_core_inputs))))
    return res.results


def make_feeds(p, inputs, consts, extra, x_shard):
    feed = {}
    for n in p.in_names:
        if n == "x_in":
            feed[n] = x_shard
        elif n in extra:
            feed[n] = extra[n]
        elif n in consts:
            feed[n] = consts[n]
        elif n in inputs:
            feed[n] = inputs[n]
        else:
            raise KeyError(n)
    return feed


def host_extra(inputs):
    extra = {}
    extra["w2r_l0"] = relayout_w2(inputs["L0_ffn_w2"])
    extra["w2r_l1"] = relayout_w2(inputs["L1_ffn_Wd"])
    extra["w2r_sd"] = relayout_w2(inputs["L1_ffn_Sd"])
    extra.update(gla_host_inputs(inputs))
    extra.update(attn1_host_inputs(inputs))
    return extra


def kernel(**inputs):
    inputs = {k: np.ascontiguousarray(v, dtype=np.float32) for k, v in inputs.items()}
    return kernel_fused(inputs)


def kernel_unfused(**inputs):
    inputs = {k: np.ascontiguousarray(v, dtype=np.float32) for k, v in inputs.items()}
    consts = host_constants()
    extra = host_extra(inputs)
    x = inputs["L0_attn_x"]
    shards = [np.ascontiguousarray(x[i]) for i in range(8)]
    progs = [
        build_ffn_program(0, "tm"),
        build_gla_program("fm"),
        build_ffn_program(0, "fm"),
        build_ffn_program(1, "fm"),
        build_attn1_program(),
        build_ffn_program(1, "fm", "tm"),
    ]
    cur = shards
    for prog in progs:
        in_maps = [make_feeds(prog, inputs, consts, extra, cur[i]) for i in range(8)]
        res = run_bass_kernel_spmd(prog.nc, in_maps, core_ids=list(range(8)))
        cur = [np.asarray(r["x_out"]) for r in res.results]
    return np.stack(cur, axis=0).astype(np.float32)


# version 05
# speedup vs baseline: 1.0765x; 1.0294x over previous
import numpy as np
import concourse.bass as bass
import concourse.mybir as mybir
from concourse.bass_utils import run_bass_kernel_spmd
import concourse.bass_utils as _BU

if not getattr(_BU, "_ldw_patched", False):
    _orig_run_command = _BU.run_command
    def _run_command_ldw(cmd, *a, **kw):
        if isinstance(cmd, list):
            cmd = ["--enable-ldw-opt=true" if c == "--enable-ldw-opt=false" else c for c in cmd]
        return _orig_run_command(cmd, *a, **kw)
    _BU.run_command = _run_command_ldw
    _BU._ldw_patched = True

F32 = mybir.dt.float32
BF16 = mybir.dt.bfloat16
AF = mybir.ActivationFunctionType
ALU = mybir.AluOpType
AX = mybir.AxisListType

T = 2048
D = 1024
FC = 8
NE = 8

SEM_OF = {"tensor": "pe", "vector": "dve", "scalar": "act", "gpsimd": "pool", "sync": "sy"}


def relayout_w2(w2):
    sh = w2.shape
    I, Dd = sh[-2], sh[-1]
    lead = sh[:-2]
    r = w2.reshape(lead + (I // 128, 128, Dd // 128, 128))
    perm = tuple(range(len(lead))) + (len(lead) + 2, len(lead) + 1, len(lead), len(lead) + 3)
    return np.ascontiguousarray(r.transpose(perm))


def host_constants():
    c = {}
    c["c_ident"] = np.eye(128, dtype=np.float32)
    d = 64
    inv_freq = 1.0 / (10000.0 ** (np.arange(0, d, 2, dtype=np.float32) / d))
    pos = np.arange(T, dtype=np.float32)
    freqs = np.repeat(pos[:, None] * inv_freq[None, :], 2, axis=1)
    c["c_cos"] = np.cos(freqs).astype(np.float32)
    c["c_sin"] = np.sin(freqs).astype(np.float32)
    m16 = np.ones((128, T), np.float32); m16[:, 0::16] = 0.0
    m32 = np.ones((128, T), np.float32); m32[:, 0::32] = 0.0
    c["c_mask16"] = m16
    c["c_mask32"] = m32
    tri = (np.arange(16)[:, None] <= np.arange(16)[None, :]).astype(np.float32)
    am = np.zeros((32, 32), np.float32)
    am[0:16, 0:16] = tri
    am[0:16, 16:32] = 1.0
    am[16:32, 16:32] = tri
    c["c_amask"] = np.tile(am, (1, 16)).astype(np.float32)
    i64 = np.zeros((128, 64), np.float32)
    i64[0:64] = np.eye(64); i64[64:128] = np.eye(64)
    c["c_ident64"] = i64
    return c


class Prog:
    def __init__(self, name="prog"):
        self.nc = bass.Bass("TRN2", target_bir_lowering=False)
        self.name = name
        self._ctx = []
        self.sem = {}
        self.cnt = {}
        self.out_names = []
        self.in_names = []

    def din(self, name, shape, dtype=F32):
        self.in_names.append(name)
        return self.nc.dram_tensor(name, list(shape), dtype, kind="ExternalInput").ap()

    def dout(self, name, shape, dtype=F32):
        self.out_names.append(name)
        return self.nc.dram_tensor(name, list(shape), dtype, kind="ExternalOutput").ap()

    def dtmp(self, name, shape, dtype=F32):
        return self.nc.dram_tensor(name, list(shape), dtype, kind="Internal").ap()

    def sb(self, name, shape, dtype=F32):
        self._name_ctr = getattr(self, "_name_ctr", 0) + 1
        h = self.nc.sbuf_tensor(f"{name}_{self._name_ctr}", list(shape), dtype)
        t = h.__enter__(); self._ctx.append(h)
        return t

    def start(self):
        self._sem_ctx = []
        self._scopes = []
        self._epoch = 0
        self.new_epoch(first=True)
        h = self.nc.psum_tensor("ps_arena", [128, 4096], F32)
        self.psum = h.__enter__(); self._ctx.append(h)
        h = self.nc.Block()
        self.block = h.__enter__(); self._ctx.append(h)

    def new_epoch(self, first=False):
        for s in ["pe", "dve", "act", "pool", "sy"]:
            h = self.nc.semaphore(f"{s}{self._epoch}")
            self.sem[s] = h.__enter__(); self._sem_ctx.append(h)
            self.cnt[s] = 0
        self._epoch += 1

    def push_scope(self):
        self._scopes.append(len(self._ctx))

    def pop_scope(self):
        i = self._scopes.pop()
        for h in reversed(self._ctx[i:]):
            h.__exit__(None, None, None)
        del self._ctx[i:]

    def bank(self, b, n=1):
        return self.psum[:, 512 * b:512 * (b + n)]

    def finalize(self):
        for h in reversed(self._ctx):
            h.__exit__(None, None, None)
        self._ctx = []
        for h in reversed(self._sem_ctx):
            h.__exit__(None, None, None)
        self._sem_ctx = []

    def step(self, eng, waits, fn, inc=1):
        own = SEM_OF[eng]
        sem = self.sem
        def body(e):
            for s, v in waits:
                if v > 0:
                    e.wait_ge(sem[s], v)
            last = fn(e)
            if inc:
                last.then_inc(sem[own], inc)
        getattr(self.block, eng)(body)
        if inc:
            self.cnt[own] += inc
        return self.cnt[own]

    def multi(self, eng, waits, items):
        own = SEM_OF[eng]
        sem = self.sem
        def body(e):
            for s, v in waits:
                if v > 0:
                    e.wait_ge(sem[s], v)
            for fn, inc in items:
                inst = fn(e)
                if inc:
                    inst.then_inc(sem[own], inc)
        getattr(self.block, eng)(body)
        self.cnt[own] += sum(i for _, i in items)
        return self.cnt[own]

    def chain(self, eng, waits, fns, inc_last=1):
        own = SEM_OF[eng]
        sem = self.sem
        base = self.cnt[own]
        n = len(fns)
        def body(e):
            for s, v in waits:
                if v > 0:
                    e.wait_ge(sem[s], v)
            for i, fn in enumerate(fns):
                if i > 0:
                    e.wait_ge(sem[own], base + i)
                fn(e).then_inc(sem[own], 1)
        getattr(self.block, eng)(body)
        self.cnt[own] += n
        return self.cnt[own]

    def v(self, s):
        return self.cnt[s]

    def barrier(self):
        snap = dict(self.cnt)
        for eng in ["tensor", "vector", "scalar", "gpsimd", "sync"]:
            own = SEM_OF[eng]
            waits = [(s, v) for s, v in snap.items() if s != own and v > 0]
            self.step(eng, waits, lambda e: e.nop(), inc=1)

    def wait_all(self, exclude=()):
        return [(s, self.cnt[s]) for s in ["pe", "dve", "act", "pool", "sy"] if s not in exclude and self.cnt[s] > 0]



def alloc_common(p):
    cm = {}
    cm["X"] = p.sb("X", [128, FC * T], F32)
    cm["ident"] = p.sb("ident", [128, 128], F32)
    cm["ones_col_bf"] = p.sb("ones_col_bf", [128, 1], BF16)
    cm["ones_row"] = p.sb("ones_row", [1, 128], F32)
    cm["eps_ln"] = p.sb("eps_ln", [1, 1], F32)
    return cm


def emit_init_consts(p, cm, ident_dram):
    p.step("sync", [], lambda e: e.dma_start(out=cm["ident"][:], in_=ident_dram), inc=16)
    p.multi("vector", [], [
        (lambda e: e.memset(cm["ones_col_bf"][:], 1.0), 0),
        (lambda e: e.memset(cm["ones_row"][:], 1.0), 0),
        (lambda e: e.memset(cm["eps_ln"][:], 1e-5), 1),
    ])


def emit_load_x_fm(p, cm, x_fm):
    items = [(lambda e, c=c: e.dma_start(out=cm["X"][:, T * c:T * (c + 1)],
                                         in_=x_fm[128 * c:128 * (c + 1), :]), 16) for c in range(FC)]
    return p.multi("sync", [], items)


def emit_store_x_fm(p, cm, x_fm):
    items = [(lambda e, c=c: e.dma_start(out=x_fm[128 * c:128 * (c + 1), :],
                                         in_=cm["X"][:, T * c:T * (c + 1)]), 16) for c in range(FC)]
    v = p.multi("sync", p.wait_all(exclude=("sy",)), items)
    p.step("sync", [("sy", v)], lambda e: e.nop(), inc=0)


def emit_load_x_tm(p, cm, x_tm):
    stg = [p.sb("xin_stg0", [128, D], F32), p.sb("xin_stg1", [128, D], F32)]
    dma_v = {}
    pe_after_tile = {}
    for tt in range(16):
        b = tt % 2
        w = [("pe", pe_after_tile[tt - 2])] if tt >= 2 else []
        dma_v[tt] = p.step("sync", w,
                           lambda e, tt=tt, b=b: e.dma_start(out=stg[b][:], in_=x_tm[128 * tt:128 * (tt + 1), :]),
                           inc=16)
        for c in range(FC):
            pb = 6 + (c % 2)
            w = [("sy", dma_v[tt])]
            w.append(("dve", max(0, p.v("dve") - 1)))
            vpe = p.step("tensor", w,
                         lambda e, c=c, b=b, pb=pb: e.transpose(p.bank(pb)[:, 0:128],
                                                                stg[b][:, 128 * c:128 * (c + 1)], cm["ident"][:]),
                         inc=1)
            p.step("vector", [("pe", vpe)],
                   lambda e, c=c, tt=tt, pb=pb: e.tensor_copy(cm["X"][:, T * c + 128 * tt:T * c + 128 * (tt + 1)],
                                                              p.bank(pb)[:, 0:128]),
                   inc=1)
        pe_after_tile[tt] = p.v("pe")
    return p.v("dve")


def emit_store_x_tm(p, cm, x_tm):
    stg = [p.sb("xout_stg0", [128, D], F32), p.sb("xout_stg1", [128, D], F32)]
    first_waits = p.wait_all(exclude=("pe",))
    dma_v = {}
    for tt in range(16):
        b = tt % 2
        for c in range(FC):
            pb = 6 + (c % 2)
            w = list(first_waits) if (tt == 0 and c == 0) else []
            w.append(("dve", max(0, p.v("dve") - 1)))
            if tt >= 2 and c == 0:
                w.append(("sy", dma_v[tt - 2]))
            vpe = p.step("tensor", w,
                         lambda e, c=c, tt=tt, pb=pb: e.transpose(p.bank(pb)[:, 0:128],
                                                                  cm["X"][:, T * c + 128 * tt:T * c + 128 * (tt + 1)],
                                                                  cm["ident"][:]),
                         inc=1)
            p.step("vector", [("pe", vpe)],
                   lambda e, c=c, b=b, pb=pb: e.tensor_copy(stg[b][:, 128 * c:128 * (c + 1)], p.bank(pb)[:, 0:128]),
                   inc=1)
        dma_v[tt] = p.step("sync", [("dve", p.v("dve"))],
                           lambda e, tt=tt, b=b: e.dma_start(out=x_tm[128 * tt:128 * (tt + 1), :], in_=stg[b][:]),
                           inc=16)
    p.step("sync", [("sy", p.v("sy"))], lambda e: e.nop(), inc=0)



def alloc_ln(p, cm):
    ln = {}
    ln["xb"] = [p.sb("ln_xb0", [128, 512], BF16), p.sb("ln_xb1", [128, 512], BF16)]
    ln["xsq"] = [p.sb("ln_xsq0", [128, 512], BF16), p.sb("ln_xsq1", [128, 512], BF16)]
    ln["a_row"] = p.sb("ln_a", [1, T], F32)
    ln["b_row"] = p.sb("ln_b", [1, T], F32)
    ln["scr"] = p.sb("ln_scr", [128, 1024], F32)
    return ln


def emit_ln_stats(p, cm, ln):
    X = cm["X"]
    ps_s = p.psum[0:1, 0:T]
    ps_q = p.psum[0:1, T:2 * T]
    first = True
    for w4 in range(4):
        for c in range(FC):
            b = c % 2
            wv = p.wait_all(exclude=("dve",)) if first else [("pe", max(0, p.v("pe") - 1))]
            first = False
            vc = p.step("vector", wv,
                        lambda e, c=c, w4=w4, b=b: e.tensor_copy(ln["xb"][b][:],
                                                                 X[:, T * c + 512 * w4:T * c + 512 * (w4 + 1)]),
                        inc=1)
            p.step("tensor", [("dve", vc)],
                   lambda e, c=c, w4=w4, b=b: e.matmul(ps_s[:, 512 * w4:512 * (w4 + 1)], cm["ones_col_bf"][:],
                                                       ln["xb"][b][:], start=(c == 0), stop=(c == FC - 1)),
                   inc=1)
        for c in range(FC):
            b = c % 2
            vs = p.step("scalar", [("pe", max(0, p.v("pe") - 1))],
                        lambda e, c=c, w4=w4, b=b: e.activation(ln["xsq"][b][:],
                                                                X[:, T * c + 512 * w4:T * c + 512 * (w4 + 1)],
                                                                AF.Square), inc=1)
            p.step("tensor", [("act", vs)],
                   lambda e, c=c, w4=w4, b=b: e.matmul(ps_q[:, 512 * w4:512 * (w4 + 1)], cm["ones_col_bf"][:],
                                                       ln["xsq"][b][:], start=(c == 0), stop=(c == FC - 1)),
                   inc=1)
    vpe = p.v("pe")
    a_row, b_row = ln["a_row"], ln["b_row"]
    vv = p.chain("vector", [("pe", vpe)], [
        lambda e: e.tensor_scalar(ps_s, ps_s, 1.0 / D, None, ALU.mult, ALU.bypass),
        lambda e: e.tensor_scalar(ps_q, ps_q, 1.0 / D, None, ALU.mult, ALU.bypass),
        lambda e: e.tensor_copy(b_row[:], ps_s),
        lambda e: e.tensor_tensor(a_row[:], b_row[:], b_row[:], ALU.mult),
        lambda e: e.tensor_tensor(ps_q, ps_q, a_row[:], ALU.subtract),
    ])
    va = p.step("scalar", [("dve", vv)],
                lambda e: e.activation(a_row[:], ps_q, AF.Sqrt, bias=cm["eps_ln"][0:1, :], scale=1.0), inc=1)
    p.chain("vector", [("act", va)], [
        lambda e: e.reciprocal(a_row[:], a_row[:]),
        lambda e: e.scalar_tensor_tensor(b_row[:], b_row[:], -1.0, a_row[:], ALU.mult, ALU.mult),
    ])
    return p.v("dve")


def emit_ln_apply(p, cm, ln, t0, L, Xn, xn_stride=None):
    X = cm["X"]
    xn_stride = L if xn_stride is None else xn_stride
    assert L <= 512
    a_bc = p.bank(4)[:, 0:L]
    b_bc = p.bank(5)[:, 0:L]
    items = [
        (lambda e: e.matmul(a_bc, cm["ones_row"][:], ln["a_row"][0:1, t0:t0 + L], start=True, stop=True), 0),
        (lambda e: e.matmul(b_bc, cm["ones_row"][:], ln["b_row"][0:1, t0:t0 + L], start=True, stop=True), 1),
    ]
    vbc = p.multi("tensor", [("dve", p.v("dve"))], items)
    for c in range(FC):
        p.chain("vector", [("pe", vbc)] if c == 0 else [], [
            (lambda e, c=c: e.tensor_tensor(ln["scr"][:, 0:L], X[:, T * c + t0:T * c + t0 + L], a_bc, ALU.mult)),
            (lambda e, c=c: e.tensor_tensor(Xn[:, xn_stride * c:xn_stride * c + L], ln["scr"][:, 0:L], b_bc, ALU.add)),
        ])
    return p.v("dve")


def emit_ln_apply_off(p, cm, ln, t0, L, Xn, dst_off, xn_stride):
    X = cm["X"]
    a_bc = p.bank(4)[:, 0:L]
    b_bc = p.bank(5)[:, 0:L]
    items = [
        (lambda e: e.matmul(a_bc, cm["ones_row"][:], ln["a_row"][0:1, t0:t0 + L], start=True, stop=True), 0),
        (lambda e: e.matmul(b_bc, cm["ones_row"][:], ln["b_row"][0:1, t0:t0 + L], start=True, stop=True), 1),
    ]
    vbc = p.multi("tensor", [("dve", p.v("dve"))], items)
    for c in range(FC):
        p.chain("vector", [("pe", vbc)] if c == 0 else [], [
            (lambda e, c=c: e.tensor_tensor(ln["scr"][:, 0:L], X[:, T * c + t0:T * c + t0 + L], a_bc, ALU.mult)),
            (lambda e, c=c: e.tensor_tensor(Xn[:, xn_stride * c + dst_off:xn_stride * c + dst_off + L],
                                            ln["scr"][:, 0:L], b_bc, ALU.add)),
        ])
    return p.v("dve")



class MoECfg:
    def __init__(self, I, experts, router_w, eps_router, scale):
        self.I = I
        self.IC = I // 128
        self.experts = experts
        self.router_w = router_w
        self.eps_router = eps_router
        self.scale = scale


def alloc_moe(p, cm, maxI):
    ms = {}
    ms["Xn"] = p.sb("moe_xn", [128, FC * 1024], BF16)
    ms["h"] = p.sb("moe_h", [128, (maxI // 128) * 1024], BF16)
    ms["w1t"] = [p.sb("moe_w1t0", [128, 8 * 256], BF16), p.sb("moe_w1t1", [128, 8 * 256], BF16)]
    ms["w3t"] = [p.sb("moe_w3t0", [128, 8 * 256], BF16), p.sb("moe_w3t1", [128, 8 * 256], BF16)]
    ms["w2t"] = [p.sb("moe_w2t0", [128, (maxI // 128) * 128], BF16), p.sb("moe_w2t1", [128, (maxI // 128) * 128], BF16)]
    ms["bc"] = p.sb("moe_bc", [128, 1024], F32)
    ms["ctrow2"] = [p.sb("moe_ctrow0", [1, 1024], F32), p.sb("moe_ctrow1", [1, 1024], F32)]
    ms["scr512"] = [p.sb("moe_scr512a", [128, 512], BF16), p.sb("moe_scr512b", [128, 512], BF16)]
    ms["st"] = {"gi": 0, "gu_evac": {0: 0, 1: 0}, "si_read": [0, 0],
                "w13_pe": {0: 0, 1: 0}, "w2_pe": {0: 0, 1: 0}, "yi": 0,
                "y_bank_evac": {5: 0, 6: 0}, "pe_bc": 0, "dve_after_y": 0,
                "bc_guard": {0: 0, 1: 0}, "bank4_dve": 0}
    ms["rw"] = p.sb("moe_rw", [128, 8 * 8], BF16)
    ms["sc_sb"] = p.sb("moe_scsb", [8, 1024], F32)
    ms["cw"] = p.sb("moe_cw", [128, 8], F32)
    ms["p_sb"] = p.sb("moe_p", [128, 8], F32)
    ms["t_sb"] = p.sb("moe_t", [128, 8], F32)
    ms["m_sb"] = p.sb("moe_m", [128, 4], F32)
    ms["ct"] = p.sb("moe_ct", [8, T], F32)
    return ms


def emit_moe(p, cm, ln, ms, cfg: MoECfg):
    X = cm["X"]
    IC = cfg.IC
    Xn = ms["Xn"]
    h = ms["h"]
    nwin = (cfg.I + 255) // 256
    wsz = [min(256, cfg.I - 256 * w) for w in range(nwin)]
    ngrp = (IC + 3) // 4
    gsz = [min(4, IC - 4 * g) for g in range(ngrp)]

    if cfg.router_w is not None:
        rw3 = cfg.router_w.rearrange("(kc p) e -> p kc e", p=128)
        v_rw = p.step("gpsimd", [], lambda e: e.dma_start(out=ms["rw"][:].rearrange("p (kc e) -> p kc e", e=8), in_=rw3),
                      inc=16)

    for half in range(2):
        t0 = half * 1024
        emit_ln_apply(p, cm, ln, t0, 512, Xn, xn_stride=1024)
        emit_ln_apply_off(p, cm, ln, t0 + 512, 512, Xn, 512, xn_stride=1024)
        v_xn = p.v("dve")

        if cfg.router_w is not None:
            ps_sc = p.psum[0:8, 0:1024]
            items = []
            for w in range(2):
                for kc in range(FC):
                    items.append((lambda e, w=w, kc=kc: e.matmul(
                        ps_sc[:, 512 * w:512 * (w + 1)],
                        ms["rw"][:].rearrange("p (kc e) -> p kc e", e=8)[:, kc:kc + 1, :].squeeze(),
                        Xn[:, 1024 * kc + 512 * w:1024 * kc + 512 * (w + 1)],
                        start=(kc == 0), stop=(kc == FC - 1)), 1 if (kc == FC - 1 and w == 1) else 0))
            v_sc = p.multi("tensor", [("dve", v_xn), ("pool", v_rw)], items)
            v_ev = p.step("vector", [("pe", v_sc)], lambda e: e.tensor_copy(ms["sc_sb"][:], ps_sc), inc=1)
            for tt in range(8):
                vpe = p.step("tensor", [("dve", v_ev if tt == 0 else p.v("dve"))],
                             lambda e, tt=tt: e.transpose(p.bank(2)[0:128, 0:8], ms["sc_sb"][0:8, 128 * tt:128 * (tt + 1)],
                                                          cm["ident"][0:8, 0:8]),
                             inc=1)
                pb, tb, mb, cwb = ms["p_sb"], ms["t_sb"], ms["m_sb"], ms["cw"]
                mx, m1, m2, den = mb[:, 0:1], mb[:, 1:2], mb[:, 2:3], mb[:, 3:4]
                p.chain("vector", [("pe", vpe)], [
                    lambda e: e.tensor_copy(pb[:], p.bank(2)[0:128, 0:8]),
                    lambda e: e.tensor_reduce(mx, pb[:], AX.X, ALU.max),
                    lambda e: e.tensor_scalar(mx, mx, -1.0, None, ALU.mult, ALU.bypass),
                ])
                va = p.step("scalar", [("dve", p.v("dve"))],
                            lambda e: e.activation(tb[:], pb[:], AF.Exp, bias=mx, scale=1.0), inc=1)
                p.chain("vector", [("act", va)], [
                    lambda e: e.tensor_reduce(m1, tb[:], AX.X, ALU.add),
                    lambda e: e.reciprocal(m1, m1),
                    lambda e: e.tensor_scalar(pb[:], tb[:], m1, None, ALU.mult, ALU.bypass),
                    lambda e: e.tensor_reduce(m1, pb[:], AX.X, ALU.max),
                    lambda e: e.tensor_scalar(tb[:], pb[:], m1, None, ALU.is_equal, ALU.bypass),
                    lambda e: e.scalar_tensor_tensor(tb[:], tb[:], -1e30, pb[:], ALU.mult, ALU.add),
                    lambda e: e.tensor_reduce(m2, tb[:], AX.X, ALU.max),
                    lambda e: e.tensor_scalar(tb[:], pb[:], m2, None, ALU.is_ge, ALU.bypass),
                    lambda e: e.tensor_tensor(den, m1, m2, ALU.add),
                    lambda e: e.tensor_scalar(den, den, cfg.eps_router, None, ALU.add, ALU.bypass),
                    lambda e: e.reciprocal(den, den),
                    lambda e: e.tensor_scalar(den, den, cfg.scale, None, ALU.mult, ALU.bypass),
                    lambda e: e.scalar_tensor_tensor(cwb[:], pb[:], den, tb[:], ALU.mult, ALU.mult),
                ])
                vpe2 = p.step("tensor", [("dve", p.v("dve"))],
                              lambda e: e.transpose(p.bank(2)[0:8, 0:128], cwb[:], cm["ident"][:]), inc=1)
                p.step("vector", [("pe", vpe2)],
                       lambda e, tt=tt: e.tensor_copy(ms["ct"][0:8, t0 + 128 * tt:t0 + 128 * (tt + 1)],
                                                      p.bank(2)[0:8, 0:128]),
                       inc=1)
            dbg = ms.get("dbg", {})
            if half == 0 and "ct" in dbg:
                p.step("sync", [("dve", p.v("dve"))], lambda e: e.dma_start(out=dbg["ct"], in_=ms["ct"][:, 0:1024]), inc=16)
            if half == 0 and "xn" in dbg:
                p.step("sync", [("dve", p.v("dve"))], lambda e: e.dma_start(out=dbg["xn"], in_=Xn[:]), inc=16)

        st = ms["st"]
        st["dve_after_y"] = p.v("dve")
        n_routed = sum(0 if ex.get("shared") else 1 for ex in cfg.experts)
        st["ctr_sy"] = {0: p.step("sync", [("pe", st["pe_bc"]), ("dve", p.v("dve"))],
                                  lambda e: e.dma_start(out=ms["ctrow2"][0][:], in_=ms["ct"][0:1, t0:t0 + 1024]),
                                  inc=16)}
        for ei, ex in enumerate(cfg.experts):
            shared = ex.get("shared", False)
            bcb = ms["bc"]
            if ei + 1 < n_routed:
                st["ctr_sy"][ei + 1] = p.step("sync", [("pe", st["pe_bc"]), ("dve", p.v("dve"))],
                                              lambda e, ei=ei: e.dma_start(out=ms["ctrow2"][(ei + 1) % 2][:],
                                                                           in_=ms["ct"][ei + 1:ei + 2, t0:t0 + 1024]),
                                              inc=16)
            if shared:
                v_bcd = p.step("vector", [("dve", st["dve_after_y"])],
                               lambda e, bcb=bcb: e.memset(bcb[:], cfg.scale), inc=1)
            else:
                ctr = ms["ctrow2"][ei % 2]
                v_mm0 = p.multi("tensor", [("sy", st["ctr_sy"][ei]), ("dve", st["bc_guard"][ei % 2]),
                                           ("dve", st["bank4_dve"])], [
                    (lambda e, ctr=ctr: e.matmul(p.bank(4)[:, 0:512], cm["ones_row"][:],
                                                 ctr[0:1, 0:512], start=True, stop=True), 0),
                    (lambda e, ctr=ctr: e.matmul(p.bank(4)[:, 512:1024] if False else p.bank(4)[:, 0:512],
                                                 cm["ones_row"][:], ctr[0:1, 512:1024],
                                                 start=True, stop=True), 1),
                ]) if False else None
                v_mm0 = p.step("tensor", [("sy", st["ctr_sy"][ei]), ("dve", st["dve_after_y"]),
                                          ("dve", st["bank4_dve"])],
                               lambda e, ctr=ctr: e.matmul(p.bank(4)[:, 0:512], cm["ones_row"][:],
                                                           ctr[0:1, 0:512], start=True, stop=True), inc=1)
                v_e0 = p.step("vector", [("pe", v_mm0)],
                              lambda e, bcb=bcb: e.tensor_copy(bcb[:, 0:512], p.bank(4)[:, 0:512]), inc=1)
                v_mm1 = p.step("tensor", [("dve", v_e0)],
                               lambda e, ctr=ctr: e.matmul(p.bank(4)[:, 0:512], cm["ones_row"][:],
                                                           ctr[0:1, 512:1024], start=True, stop=True), inc=1)
                v_bcd = p.step("vector", [("pe", v_mm1)],
                               lambda e, bcb=bcb: e.tensor_copy(bcb[:, 512:1024], p.bank(4)[:, 0:512]), inc=1)
                st["bank4_dve"] = v_bcd
            st["pe_bc"] = p.v("pe")

            def dma_win(w):
                buf = w % 2
                wait = [("pe", st["w13_pe"][buf])]
                src1 = ex["w1"][:, 256 * w:256 * w + wsz[w]].rearrange("(kc p) m -> p kc m", p=128)
                src3 = ex["w3"][:, 256 * w:256 * w + wsz[w]].rearrange("(kc p) m -> p kc m", p=128)
                d1 = ms["w1t"][buf][:, 0:8 * wsz[w]].rearrange("p (kc m) -> p kc m", kc=8)
                d3 = ms["w3t"][buf][:, 0:8 * wsz[w]].rearrange("p (kc m) -> p kc m", kc=8)
                return p.multi("gpsimd", wait, [
                    (lambda e: e.dma_start(out=d1, in_=src1), 16),
                    (lambda e: e.dma_start(out=d3, in_=src3), 16),
                ])
            dma_w13 = {0: dma_win(0)}
            if nwin > 1:
                dma_w13[1] = dma_win(1)
            gi0 = st["gi"]
            for w in range(nwin):
                buf = w % 2
                nmc = wsz[w] // 128
                for mcw in range(nmc):
                    mc = 2 * w + mcw
                    for ts in range(2):
                        gi = st["gi"]
                        bg = gi % 2
                        waits = [("pool", dma_w13[w]), ("dve", st["gu_evac"][bg])]
                        items = []
                        for kc in range(FC):
                            items.append((lambda e, kc=kc, buf=buf, mcw=mcw, ts=ts, bg=bg, w=w: e.matmul(
                                p.bank(bg)[:, 0:512],
                                ms["w1t"][buf][:, 0:8 * wsz[w]].rearrange("p (kc m) -> p kc m", kc=8)[:, kc:kc + 1, 128 * mcw:128 * (mcw + 1)].squeeze(),
                                Xn[:, 1024 * kc + 512 * ts:1024 * kc + 512 * (ts + 1)],
                                start=(kc == 0), stop=(kc == FC - 1)), 0))
                        for kc in range(FC):
                            items.append((lambda e, kc=kc, buf=buf, mcw=mcw, ts=ts, bg=bg, w=w: e.matmul(
                                p.bank(2 + bg)[:, 0:512],
                                ms["w3t"][buf][:, 0:8 * wsz[w]].rearrange("p (kc m) -> p kc m", kc=8)[:, kc:kc + 1, 128 * mcw:128 * (mcw + 1)].squeeze(),
                                Xn[:, 1024 * kc + 512 * ts:1024 * kc + 512 * (ts + 1)],
                                start=(kc == 0), stop=(kc == FC - 1)), 1 if kc == FC - 1 else 0))
                        vmm = p.multi("tensor", waits, items)
                        sbuf_si = gi % 2
                        va = p.step("scalar", [("pe", vmm), ("dve", st["si_read"][sbuf_si])],
                                    lambda e, bg=bg, sbuf_si=sbuf_si: e.activation(
                                        ms["scr512"][sbuf_si][:], p.bank(bg)[:, 0:512], AF.Silu),
                                    inc=1)
                        st["gu_evac"][bg] = p.step("vector", [("act", va)],
                                                   lambda e, mc=mc, ts=ts, bg=bg, sbuf_si=sbuf_si: e.tensor_tensor(
                                                       h[:, 1024 * mc + 512 * ts:1024 * mc + 512 * (ts + 1)],
                                                       ms["scr512"][sbuf_si][:], p.bank(2 + bg)[:, 0:512], ALU.mult),
                                                   inc=1)
                        st["si_read"][sbuf_si] = st["gu_evac"][bg]
                        st["gi"] += 1
                st["w13_pe"][buf] = p.v("pe")
                if w + 2 < nwin:
                    dma_w13[w + 2] = dma_win(w + 2)
            v_h_done = p.v("dve")
            dbg = ms.get("dbg", {})
            if half == 0 and ei == 0 and "h" in dbg:
                p.step("sync", [("dve", v_h_done)], lambda e: e.dma_start(out=dbg["h"], in_=h[:, 0:IC * 1024]), inc=16)
            if half == 0 and ei == 0 and "bc" in dbg:
                p.step("sync", [("dve", v_h_done)], lambda e: e.dma_start(out=dbg["bc"], in_=ms["bc"][:]), inc=16)

            def dma_hc(hc):
                buf = hc % 2
                src = ex["w2r"][hc:hc + 1].squeeze()
                dd = ms["w2t"][buf][:, 0:IC * 128].rearrange("p (kc m) -> p kc m", m=128)
                return p.step("gpsimd", [("pe", st["w2_pe"][buf])],
                              lambda e: e.dma_start(out=dd, in_=src), inc=16)
            dma_w2 = {0: dma_hc(0), 1: dma_hc(1)}
            yi = st["yi"]
            for hc in range(FC):
                buf = hc % 2
                for ts in range(2):
                    yb = 5 + (yi % 2)
                    waits = [("pool", dma_w2[hc]), ("dve", st["y_bank_evac"][yb])]
                    if hc == 0 and ts == 0:
                        waits.append(("dve", v_h_done))
                    items = []
                    for kc in range(IC):
                        items.append((lambda e, kc=kc, hc=hc, buf=buf, ts=ts, yb=yb: e.matmul(
                            p.bank(yb)[:, 0:512],
                            ms["w2t"][buf][:, 0:IC * 128].rearrange("p (kc m) -> p kc m", m=128)[:, kc:kc + 1, :].squeeze(),
                            h[:, 1024 * kc + 512 * ts:1024 * kc + 512 * (ts + 1)],
                            start=(kc == 0), stop=(kc == IC - 1)),
                            1 if kc == IC - 1 else 0))
                    v_y = p.multi("tensor", waits, items)
                    st["y_bank_evac"][yb] = p.multi("vector", [("pe", v_y)], [
                        (lambda e, hc=hc, ts=ts, yb=yb, bcb=bcb: e.tensor_tensor(ln["scr"][:, 0:512], p.bank(yb)[:, 0:512],
                                                                        bcb[:, 512 * ts:512 * (ts + 1)], ALU.mult), 0),
                        (lambda e, hc=hc, ts=ts: e.tensor_tensor(X[:, T * hc + t0 + 512 * ts:T * hc + t0 + 512 * (ts + 1)],
                                                                 X[:, T * hc + t0 + 512 * ts:T * hc + t0 + 512 * (ts + 1)],
                                                                 ln["scr"][:, 0:512], ALU.add), 1),
                    ])
                    yi += 1
                st["w2_pe"][buf] = p.v("pe")
                if hc + 2 < FC:
                    dma_w2[hc + 2] = dma_hc(hc + 2)
            st["yi"] = yi
            st["dve_after_y"] = p.v("dve")
    return



SEG = 256
NCH = SEG // 32
GLA_STAGE = 4
GLA_SUB = {'A', 'O', 'S'}


def alloc_gla(p, cm):
    gs = {}
    for n in ["bA", "bB", "bC", "bD", "bE", "bF", "v32"]:
        gs[n] = p.sb("gla_" + n, [128, SEG * FC], F32)
    for n in ["qt", "ub", "G", "q32"]:
        gs[n] = p.sb("gla_" + n, [128, SEG * FC], BF16)
    gs["Xn"] = p.sb("gla_xn", [128, SEG * FC], BF16)
    gs["h"] = p.sb("gla_h", [128, SEG * FC], BF16)
    gs["vtm"] = p.sb("gla_vtm", [64, 2 * 1024], BF16)
    gs["ktm"] = p.sb("gla_ktm", [64, 2 * 1024], BF16)
    gs["Asb"] = p.sb("gla_asb", [64, 512], BF16)
    gs["hstg"] = p.sb("gla_hstg", [128, 256], F32)
    gs["i64"] = p.sb("gla_i64", [128, 64], F32)
    gs["S"] = p.sb("gla_S", [128, 512], F32)
    gs["Sbf"] = p.sb("gla_Sbf", [128, 512], BF16)
    gs["D32"] = p.sb("gla_D32", [128, 64], F32)
    gs["m16"] = p.sb("gla_m16", [128, SEG], BF16)
    gs["m32"] = p.sb("gla_m32", [128, SEG], BF16)
    gs["amask"] = p.sb("gla_amask", [32, 512], F32)
    gs["kb"] = p.sb("gla_kb", [128, 8], F32)
    gs["ob"] = p.sb("gla_ob", [128, 8], F32)
    gs["lneps"] = p.sb("gla_lneps", [128, 1], F32)
    gs["wbuf"] = [p.sb("gla_wb0", [128, 8 * 512], BF16), p.sb("gla_wb1", [128, 8 * 512], BF16)]
    return gs


def emit_gla(p, cm, ln, gs, W):
    X = cm["X"]
    p.multi("sync", [], [
        (lambda e: e.dma_start(out=gs["amask"][:], in_=W["c_amask"]), 16),
        (lambda e: e.dma_start(out=gs["kb"][:], in_=W["key_b_r"]), 16),
        (lambda e: e.dma_start(out=gs["ob"][:], in_=W["out_b_r"]), 16),
        (lambda e: e.dma_start(out=gs["i64"][:], in_=W["c_ident64"]), 16),
    ])
    p.multi("gpsimd", [], [
        (lambda e: e.dma_start(out=gs["m16"][:], in_=W["c_mask16"][:, 0:SEG]), 16),
        (lambda e: e.dma_start(out=gs["m32"][:], in_=W["c_mask32"][:, 0:SEG]), 16),
    ])
    init_sy = p.v("sy")
    init_pool = p.v("pool")
    p.multi("vector", [("dve", p.v("dve")), ("pe", p.v("pe"))], [
        (lambda e: e.memset(gs["S"][:], 0.0), 0),
        (lambda e: e.memset(gs["Sbf"][:], 0.0), 0),
        (lambda e: e.memset(gs["lneps"][:], 1e-35), 0),
        (lambda e: e.memset(p.psum[0:32, 0:512], 0.0), 0),
        (lambda e: e.memset(gs["vtm"][32:64, :], 0.0), 0),
        (lambda e: e.memset(gs["ktm"][32:64, :], 0.0), 0),
        (lambda e: e.memset(gs["Asb"][32:64, :], 0.0), 1),
    ])
    init_dve = p.v("dve")
    st = {"wb_pe": [0, 0], "A_evac": 0, "O_evac": 0, "S_evac": 0, "Sbf_cast": 0,
          "tr_evac": {4: 0, 5: 0, 6: 0, 7: 0}, "proj_evac": [0, 0], "proj_act": [0, 0]}

    def dma_w(src_ap, mh, buf, wait_pe):
        src = src_ap[:, 512 * mh:512 * (mh + 1)].rearrange("(kc p) m -> p kc m", p=128)
        dd = gs["wbuf"][buf][:].rearrange("p (kc m) -> p kc m", m=512)
        return p.step("gpsimd", [("pe", wait_pe)], lambda e: e.dma_start(out=dd, in_=src), inc=16)

    for seg in range(FC):
        t0 = SEG * seg
        emit_ln_apply(p, cm, ln, t0, SEG, gs["Xn"])
        v_xn = p.v("dve")

        d0 = dma_w(W["key_wT"], 0, 0, st["wb_pe"][0])
        d1 = dma_w(W["key_wT"], 1, 1, st["wb_pe"][1])
        dmav = {0: d0, 1: d1}

        for mc in range(FC):
            buf = mc // 4
            bk = mc % 2
            waits = [("pool", dmav[buf]), ("act", st["proj_act"][bk]), ("dve", v_xn), ("sy", init_sy),
                     ("pool", init_pool)]
            items = []
            for kc in range(FC):
                items.append((lambda e, kc=kc, buf=buf, mc=mc, bk=bk: e.matmul(
                    p.bank(bk)[:, 0:SEG],
                    gs["wbuf"][buf][:].rearrange("p (kc m) -> p kc m", m=512)[:, kc:kc + 1, 128 * (mc % 4):128 * (mc % 4 + 1)].squeeze(),
                    gs["Xn"][:, SEG * kc:SEG * (kc + 1)],
                    start=(kc == 0), stop=(kc == FC - 1)), 1 if kc == FC - 1 else 0))
            vmm = p.multi("tensor", waits, items)
            st["proj_act"][bk] = p.step("scalar", [("pe", vmm)],
                                        lambda e, mc=mc, bk=bk: e.activation(gs["bA"][:, SEG * mc:SEG * (mc + 1)],
                                                                             p.bank(bk)[:, 0:SEG], AF.Sigmoid,
                                                                             bias=gs["kb"][:, mc:mc + 1], scale=1.0),
                                        inc=1)
        for half4 in range(4):
            buf = half4 % 2
            wait_pe_users = p.v("pe")
            dmav[buf] = dma_w(W["vq_wT"], half4, buf, wait_pe_users)
            for sub in range(4):
                mc = half4 * 4 + sub
                bk = mc % 2
                dst = gs["bE"] if mc < 8 else gs["bF"]
                mco = mc % 8
                waits = [("pool", dmav[buf]), ("act", st["proj_act"][bk])]
                items = []
                for kc in range(FC):
                    items.append((lambda e, kc=kc, buf=buf, sub=sub, bk=bk: e.matmul(
                        p.bank(bk)[:, 0:SEG],
                        gs["wbuf"][buf][:].rearrange("p (kc m) -> p kc m", m=512)[:, kc:kc + 1, 128 * sub:128 * (sub + 1)].squeeze(),
                        gs["Xn"][:, SEG * kc:SEG * (kc + 1)],
                        start=(kc == 0), stop=(kc == FC - 1)), 1 if kc == FC - 1 else 0))
                vmm = p.multi("tensor", waits, items)
                st["proj_act"][bk] = p.step("scalar", [("pe", vmm)],
                                            lambda e, dst=dst, mco=mco, bk=bk: e.activation(
                                                dst[:, SEG * mco:SEG * (mco + 1)], p.bank(bk)[:, 0:SEG], AF.Copy),
                                            inc=1)
        v_proj = p.v("act")
        if GLA_STAGE < 1:
            continue

        k_, u_, es, cl32 = gs["bA"], gs["bB"], gs["bC"], gs["bD"]
        vv, q_ = gs["bE"], gs["bF"]
        dD, k32 = gs["bF"], gs["v32"]
        v1 = p.step("vector", [("act", v_proj)],
                    lambda e: e.tensor_scalar(u_[:], k_[:], -1.0, 1.0, ALU.mult, ALU.add), inc=1)
        v2 = p.step("scalar", [("dve", v1)],
                    lambda e: e.activation(es[:], k_[:], AF.Ln, bias=gs["lneps"][:], scale=1.0), inc=1)
        fns = []
        for fc in range(FC):
            fns.append(lambda e, fc=fc: e.tensor_tensor_scan(
                k_[:, SEG * fc:SEG * (fc + 1)],
                gs["m16"][:], es[:, SEG * fc:SEG * (fc + 1)], 0.0, ALU.mult, ALU.add))
        for fc in range(FC):
            fns.append(lambda e, fc=fc: e.tensor_tensor_scan(
                cl32[:, SEG * fc:SEG * (fc + 1)],
                gs["m32"][:], es[:, SEG * fc:SEG * (fc + 1)], 0.0, ALU.mult, ALU.add))
        v3 = p.chain("vector", [("act", v2)], fns)
        cl16 = k_
        v4 = p.step("scalar", [("dve", v3)], lambda e: e.activation(es[:], cl16[:], AF.Exp), inc=1)
        v5 = p.step("vector", [("act", v4)], lambda e: e.tensor_tensor(gs["qt"][:], q_[:], es[:], ALU.mult), inc=1)
        v6 = p.step("scalar", [("dve", v5)], lambda e: e.activation(es[:], cl32[:], AF.Exp), inc=1)
        v7 = p.step("vector", [("act", v6)], lambda e: e.tensor_tensor(gs["q32"][:], q_[:], es[:], ALU.mult), inc=1)
        cl32e = cl32[:].rearrange("p (a s) -> p a s", s=32)[:, :, 31]
        v8 = p.step("scalar", [("dve", v3)], lambda e: e.activation(gs["D32"][:], cl32e, AF.Exp), inc=1)
        v9 = p.step("scalar", [("dve", v7)], lambda e: e.activation(es[:], cl16[:], AF.Exp, scale=-1.0), inc=1)
        v10 = p.chain("vector", [("act", v9)], [
            lambda e: e.tensor_tensor(gs["ub"][:], u_[:], es[:], ALU.mult),
            lambda e: e.tensor_copy(gs["G"][:].rearrange("p (a s) -> p a s", s=32)[:, :, 16:32],
                                    gs["ub"][:].rearrange("p (a s) -> p a s", s=32)[:, :, 16:32]),
        ])
        cl16_blk = cl16[:].rearrange("p (a s) -> p a s", s=16)
        bcast16 = cl16_blk[:, :, 15].unsqueeze(2).broadcast_to((128, SEG * FC // 16, 16))
        v11 = p.step("vector", [("dve", v10)],
                     lambda e: e.tensor_tensor(dD[:].rearrange("p (a s) -> p a s", s=16), bcast16, cl16_blk,
                                               ALU.subtract), inc=1)
        v12 = p.step("scalar", [("dve", v11)], lambda e: e.activation(es[:], dD[:], AF.Exp), inc=1)
        v13 = p.step("vector", [("act", v12)],
                     lambda e: e.tensor_tensor(gs["G"][:].rearrange("p (a s) -> p a s", s=32)[:, :, 0:16],
                                               u_[:].rearrange("p (a s) -> p a s", s=32)[:, :, 0:16],
                                               es[:].rearrange("p (a s) -> p a s", s=32)[:, :, 0:16],
                                               ALU.mult), inc=1)
        cl32_blk = cl32[:].rearrange("p (a s) -> p a s", s=32)
        bcast32 = cl32_blk[:, :, 31].unsqueeze(2).broadcast_to((128, SEG * FC // 32, 32))
        v14 = p.step("vector", [("dve", v13)],
                     lambda e: e.tensor_tensor(dD[:].rearrange("p (a s) -> p a s", s=32), bcast32, cl32_blk,
                                               ALU.subtract), inc=1)
        v15 = p.step("scalar", [("dve", v14)], lambda e: e.activation(es[:], dD[:], AF.Exp), inc=1)
        v16 = p.step("vector", [("act", v15)], lambda e: e.tensor_tensor(k32[:], u_[:], es[:], ALU.mult), inc=1)
        v_prep = p.v("dve")
        if GLA_STAGE < 2:
            continue

        for c in range(NCH):
            sl = c % 2
            so = 1024 * sl
            for bgrp, r0 in ((0, 0), (1, 64)):
                items = []
                for (tsrc, pb) in ((vv, 4 + bgrp), (k32, 6 + bgrp)):
                    for fc in range(FC):
                        items.append((lambda e, fc=fc, c=c, tsrc=tsrc, pb=pb, r0=r0: e.transpose(
                            p.psum[0:32, 512 * pb + 64 * fc:512 * pb + 64 * (fc + 1)],
                            tsrc[r0:r0 + 64, SEG * fc + 32 * c:SEG * fc + 32 * (c + 1)],
                            gs["i64"][r0:r0 + 64, 0:64],
                            tile_position=(r0, 0)),
                            1 if (tsrc is k32 and fc == FC - 1) else 0))
                w = [("dve", st["tr_evac"][4 + bgrp]), ("pe", p.v("pe"))]
                if c == 0:
                    w.append(("dve", v_prep))
                vt = p.multi("tensor", w, items)
                ev_items = []
                for (packs, pb) in (("vtm", 4 + bgrp), ("ktm", 6 + bgrp)):
                    ev_items.append((lambda e, so=so, pb=pb, packs=packs, bgrp=bgrp:
                                     e.tensor_copy(gs[packs][0:32, so + 512 * bgrp:so + 512 * (bgrp + 1)],
                                                   p.psum[0:32, 512 * pb:512 * (pb + 1)]), 1))
                st["tr_evac"][4 + bgrp] = p.multi("vector", [("pe", vt), ("pe", st.setdefault("slot_pe", [0, 0])[sl])],
                                                  ev_items)
            v_packs = p.v("dve")
            vtm = gs["vtm"]
            ktm = gs["ktm"]
            if "A" not in GLA_SUB:
                continue
            vA = 0
            for hw in range(2):
                par = 64 * hw
                co = 256 * hw
                items = []
                for hh in range(8):
                    fc = hh
                    lo = SEG * fc + 32 * c
                    items.append((lambda e, hh=hh, par=par, lo=lo, co=co: e.matmul(
                        p.psum[0:16, co + 32 * hh:co + 32 * hh + 16],
                        gs["ub"][par:par + 64, lo:lo + 16],
                        gs["qt"][par:par + 64, lo:lo + 16], start=True, stop=True,
                        tile_position=(par, 0)), 0))
                    items.append((lambda e, hh=hh, par=par, lo=lo, co=co: e.matmul(
                        p.psum[0:32, co + 32 * hh + 16:co + 32 * hh + 32],
                        gs["G"][par:par + 64, lo:lo + 32],
                        gs["qt"][par:par + 64, lo + 16:lo + 32], start=True, stop=True,
                        tile_position=(par, 0)),
                        1 if hh == 7 else 0))
                w = [("pe", p.v("pe"))]
                if hw == 0:
                    w += [("dve", st["A_evac"] if st["A_evac"] else init_dve), ("dve", v_packs)]
                vA = p.multi("tensor", w, items)
            st["A_evac"] = p.step("vector", [("pe", vA)],
                                  lambda e: e.tensor_tensor(gs["Asb"][0:32, :], p.psum[0:32, 0:512],
                                                            gs["amask"][:], ALU.mult), inc=1)
            if "O" not in GLA_SUB:
                continue
            items = []
            for hh in range(8):
                lo = SEG * hh + 32 * c
                items.append((lambda e, hh=hh, so=so: e.matmul(
                    p.bank(2)[0:64, 32 * hh:32 * (hh + 1)],
                    vtm[0:64, so + 64 * hh:so + 64 * (hh + 1)],
                    gs["Asb"][0:64, 32 * hh:32 * (hh + 1)], start=True, stop=False,
                    tile_position=(0, 0)), 0))
                items.append((lambda e, hh=hh, lo=lo: e.matmul(
                    p.bank(2)[0:64, 32 * hh:32 * (hh + 1)],
                    gs["Sbf"][0:64, 64 * hh:64 * (hh + 1)],
                    gs["q32"][0:64, lo:lo + 32], start=False, stop=True,
                    tile_position=(0, 0)),
                    0))
            for hh in range(8):
                items.append((lambda e, hh=hh, so=so: e.matmul(
                    p.bank(2)[64:128, 32 * hh:32 * (hh + 1)],
                    vtm[0:64, so + 512 + 64 * hh:so + 512 + 64 * (hh + 1)],
                    gs["Asb"][0:64, 256 + 32 * hh:256 + 32 * (hh + 1)], start=True, stop=True,
                    tile_position=(0, 64)), 1))
            w = [("dve", st["A_evac"]), ("dve", st["O_evac"]), ("dve", st["Sbf_cast"]), ("pe", p.v("pe"))]
            vO1 = p.multi("tensor", w, items)
            items = []
            for hh in range(8):
                lo = SEG * hh + 32 * c
                items.append((lambda e, hh=hh, lo=lo: e.matmul(
                    p.bank(2)[64:128, 256 + 32 * hh:256 + 32 * (hh + 1)],
                    gs["Sbf"][64:128, 64 * hh:64 * (hh + 1)],
                    gs["q32"][64:128, lo:lo + 32], start=True, stop=True,
                    tile_position=(64, 64)),
                    1 if hh == 7 else 0))
            vO = p.multi("tensor", [("pe", vO1)], items)
            st["O_evac"] = p.chain("vector", [("pe", vO)], [
                lambda e, c=c: e.tensor_copy(
                    gs["h"][0:64, :].rearrange("p (a s) -> p a s", s=SEG)[:, :, 32 * c:32 * (c + 1)],
                    p.bank(2)[0:64, 0:256].rearrange("p (a s) -> p a s", s=32)),
                lambda e: e.tensor_copy(gs["hstg"][64:128, :], p.bank(2)[64:128, 0:256]),
                lambda e, c=c: e.tensor_tensor(
                    gs["h"][64:128, :].rearrange("p (a s) -> p a s", s=SEG)[:, :, 32 * c:32 * (c + 1)],
                    gs["hstg"][64:128, :].rearrange("p (a s) -> p a s", s=32),
                    p.bank(2)[64:128, 256:512].rearrange("p (a s) -> p a s", s=32), ALU.add),
            ])
            if "S" not in GLA_SUB:
                continue
            items = []
            for hw in range(2):
                for hh in range(8):
                    items.append((lambda e, hh=hh, hw=hw, so=so: e.matmul(
                        p.bank(3)[64 * hw:64 * hw + 64, 64 * hh:64 * (hh + 1)],
                        ktm[0:64, so + 512 * hw + 64 * hh:so + 512 * hw + 64 * (hh + 1)],
                        vtm[0:64, so + 512 * hw + 64 * hh:so + 512 * hw + 64 * (hh + 1)],
                        start=True, stop=True, tile_position=(0, 64 * hw)),
                        1 if (hw == 1 and hh == 7) else 0))
            vS = p.multi("tensor", [("dve", st["S_evac"]), ("pe", p.v("pe"))], items)
            fns = []
            for hw in range(2):
                par = 64 * hw
                for hh in range(8):
                    fns.append(lambda e, hh=hh, par=par, c=c: e.scalar_tensor_tensor(
                        gs["S"][par:par + 64, 64 * hh:64 * (hh + 1)], gs["S"][par:par + 64, 64 * hh:64 * (hh + 1)],
                        gs["D32"][par:par + 64, 8 * hh + c:8 * hh + c + 1],
                        p.bank(3)[par:par + 64, 64 * hh:64 * (hh + 1)],
                        ALU.mult, ALU.add))
            fns.append(lambda e: e.tensor_copy(gs["Sbf"][:], gs["S"][:]))
            st["S_evac"] = p.chain("vector", [("pe", vS)], fns)
            st["Sbf_cast"] = st["S_evac"]
            st["slot_pe"][sl] = p.v("pe")

        v_h = p.v("dve")
        if GLA_STAGE < 3:
            continue
        d0 = dma_w(W["out_wT"], 0, 0, p.v("pe"))
        d1 = dma_w(W["out_wT"], 1, 1, p.v("pe"))
        dmav = {0: d0, 1: d1}
        for mc in range(FC):
            buf = mc // 4
            bk = mc % 2
            waits = [("pool", dmav[buf]), ("dve", st["proj_evac"][bk]), ("act", st["proj_act"][bk])]
            if mc == 0:
                waits.append(("dve", v_h))
            items = []
            for kc in range(FC):
                items.append((lambda e, kc=kc, buf=buf, mc=mc, bk=bk: e.matmul(
                    p.bank(bk)[:, 0:SEG],
                    gs["wbuf"][buf][:].rearrange("p (kc m) -> p kc m", m=512)[:, kc:kc + 1, 128 * (mc % 4):128 * (mc % 4 + 1)].squeeze(),
                    gs["h"][:, SEG * kc:SEG * (kc + 1)],
                    start=(kc == 0), stop=(kc == FC - 1)), 1 if kc == FC - 1 else 0))
            vmm = p.multi("tensor", waits, items)
            st["proj_evac"][bk] = p.step("vector", [("pe", vmm)],
                                         lambda e, mc=mc, bk=bk, t0=t0: e.scalar_tensor_tensor(
                                             X[:, T * mc + t0:T * mc + t0 + SEG],
                                             p.bank(bk)[:, 0:SEG], gs["ob"][:, mc:mc + 1],
                                             X[:, T * mc + t0:T * mc + t0 + SEG], ALU.add, ALU.add), inc=1)
            st["proj_act"][bk] = st["proj_act"][bk]
        st["wb_pe"] = [p.v("pe"), p.v("pe")]
    return



SEG1 = 512
LAMBDA_INIT = 0.8 - 0.6 * float(np.exp(-0.3 * 12))


def alloc_attn1(p, cm):
    at = {}
    at["Xn"] = p.sb("at_xn", [128, FC * SEG1], BF16)
    at["q"] = p.sb("at_q", [128, 4 * D], F32)
    at["k"] = p.sb("at_k", [128, 4 * D], F32)
    at["v"] = p.sb("at_v", [128, 4 * D], BF16)
    at["attn"] = p.sb("at_attn", [128, 4 * D], F32)
    at["af"] = p.sb("at_af", [128, FC * SEG1], BF16)
    at["scr"] = p.sb("at_scr", [128, D], F32)
    at["scr2"] = p.sb("at_scr2", [128, D], F32)
    at["scr3"] = p.sb("at_scr3", [128, D], F32)
    at["sc"] = p.sb("at_sc", [128, 128], F32)
    at["mx"] = p.sb("at_mx", [128, 16], F32)
    at["rstd"] = p.sb("at_rstd", [128, 8], F32)
    at["wm"] = p.sb("at_wm", [128, 64], F32)
    at["cs_cos"] = p.sb("at_cos", [128, 4 * 64], F32)
    at["cs_sin"] = p.sb("at_sin", [128, 4 * 64], F32)
    at["lnw"] = p.sb("at_lnw", [128, 128], F32)
    at["lnb"] = p.sb("at_lnb", [128, 128], F32)
    at["lam"] = p.sb("at_lam", [1, 256], F32)
    at["lams"] = p.sb("at_lams", [1, 128], F32)
    at["neglam"] = p.sb("at_neglam", [128, 1], F32)
    at["eps8"] = p.sb("at_eps8", [128, 1], F32)
    at["wbuf"] = [p.sb("at_wb0", [128, 8 * 512], BF16), p.sb("at_wb1", [128, 8 * 512], BF16)]
    return at


def emit_attn1(p, cm, ln, at, W):
    X = cm["X"]
    HDQ = float(64.0 ** -0.25)
    p.multi("sync", [], [
        (lambda e: e.dma_start(out=at["lam"][0:1, 0:64], in_=W["lambda_q1"].unsqueeze(0)), 16),
        (lambda e: e.dma_start(out=at["lam"][0:1, 64:128], in_=W["lambda_k1"].unsqueeze(0)), 16),
        (lambda e: e.dma_start(out=at["lam"][0:1, 128:192], in_=W["lambda_q2"].unsqueeze(0)), 16),
        (lambda e: e.dma_start(out=at["lam"][0:1, 192:256], in_=W["lambda_k2"].unsqueeze(0)), 16),
        (lambda e: e.dma_start(out=at["lnw"][:], in_=W["ln_w_rep"]), 16),
        (lambda e: e.dma_start(out=at["lnb"][:], in_=W["ln_b_rep"]), 16),
    ])
    v_ld = p.v("sy")
    vx = p.chain("vector", [("sy", v_ld), ("pe", p.v("pe")), ("dve", p.v("dve"))], [
        lambda e: e.memset(at["eps8"][:], 1e-8),
        lambda e: e.tensor_tensor(at["lams"][0:1, 0:64], at["lam"][0:1, 0:64], at["lam"][0:1, 64:128], ALU.mult),
        lambda e: e.tensor_tensor(at["lams"][0:1, 64:128], at["lam"][0:1, 128:192], at["lam"][0:1, 192:256], ALU.mult),
        lambda e: e.tensor_reduce(at["lams"][0:1, 0:2], at["lams"][0:1, 0:128].rearrange("p (a d) -> p a d", d=64),
                                  AX.X, ALU.add),
    ])
    va = p.step("scalar", [("dve", vx)], lambda e: e.activation(at["lams"][0:1, 4:6], at["lams"][0:1, 0:2], AF.Exp),
                inc=1)
    vx = p.chain("vector", [("act", va)], [
        lambda e: e.tensor_tensor(at["lams"][0:1, 8:9], at["lams"][0:1, 4:5], at["lams"][0:1, 5:6], ALU.subtract),
        lambda e: e.tensor_scalar(at["lams"][0:1, 8:9], at["lams"][0:1, 8:9], LAMBDA_INIT, -1.0,
                                  ALU.add, ALU.mult),
        lambda e: e.tensor_scalar(at["lnw"][:], at["lnw"][:], 1.0 - LAMBDA_INIT, None, ALU.mult, ALU.bypass),
        lambda e: e.tensor_scalar(at["lnb"][:], at["lnb"][:], 1.0 - LAMBDA_INIT, None, ALU.mult, ALU.bypass),
    ])
    vneg = p.step("tensor", [("dve", vx)],
                  lambda e: e.matmul(p.bank(2)[:, 0:1], cm["ones_row"][:], at["lams"][0:1, 8:9],
                                     start=True, stop=True), inc=1)
    v_lam = p.step("vector", [("pe", vneg)], lambda e: e.tensor_copy(at["neglam"][:], p.bank(2)[:, 0:1]), inc=1)

    st = {"bk_act": [0, 0], "bk_dve": [0, 0], "wb_pe": [0, 0], "tr_dve": {6: 0, 7: 0}}

    def dma_w(src_ap, mh, buf):
        src = src_ap[:, 512 * mh:512 * (mh + 1)].rearrange("(kc p) m -> p kc m", p=128)
        dd = at["wbuf"][buf][:].rearrange("p (kc m) -> p kc m", m=512)
        return p.step("gpsimd", [("pe", st["wb_pe"][buf])], lambda e: e.dma_start(out=dd, in_=src), inc=16)

    for seg in range(4):
        t0 = SEG1 * seg
        emit_ln_apply(p, cm, ln, t0, SEG1, at["Xn"])
        v_xn = p.v("dve")
        v_cs = p.multi("gpsimd", [("dve", st["tr_dve"][6])], [
            (lambda e: e.dma_start(out=at["cs_cos"][:].rearrange("p (a d) -> p a d", d=64),
                                   in_=W["c_cos"][t0:t0 + SEG1, :].rearrange("(a p) d -> p a d", p=128)), 16),
            (lambda e: e.dma_start(out=at["cs_sin"][:].rearrange("p (a d) -> p a d", d=64),
                                   in_=W["c_sin"][t0:t0 + SEG1, :].rearrange("(a p) d -> p a d", p=128)), 16),
        ])
        for mi, (wd, dst, dt_) in enumerate(((W["Wq"], at["q"], F32), (W["Wk"], at["k"], F32),
                                             (W["Wv"], at["v"], BF16))):
            d0 = dma_w(wd, 0, 0)
            d1 = dma_w(wd, 1, 1)
            dmav = {0: d0, 1: d1}
            for tt in range(4):
                for win in range(2):
                    bk = (tt * 2 + win) % 2
                    waits = [("pool", dmav[win]), ("dve", st["bk_dve"][bk]), ("act", st["bk_act"][bk])]
                    if mi == 0 and tt == 0 and win == 0:
                        waits.append(("dve", v_xn))
                    items = []
                    for kc in range(FC):
                        items.append((lambda e, kc=kc, tt=tt, win=win, bk=bk: e.matmul(
                            p.bank(bk)[:, 0:512],
                            at["Xn"][:, SEG1 * kc + 128 * tt:SEG1 * kc + 128 * (tt + 1)],
                            at["wbuf"][win][:].rearrange("p (kc m) -> p kc m", m=512)[:, kc:kc + 1, :].squeeze(),
                            start=(kc == 0), stop=(kc == FC - 1)), 1 if kc == FC - 1 else 0))
                    vmm = p.multi("tensor", waits, items)
                    st["bk_dve"][bk] = p.step("vector", [("pe", vmm)],
                                              lambda e, tt=tt, win=win, bk=bk, dst=dst: e.tensor_copy(
                                                  dst[:, D * tt + 512 * win:D * tt + 512 * (win + 1)],
                                                  p.bank(bk)[:, 0:512]), inc=1)
            st["wb_pe"] = [p.v("pe"), p.v("pe")]
        v_qkv = p.v("dve")

        for tt in range(4):
            q_t = at["q"][:, D * tt:D * (tt + 1)]
            k_t = at["k"][:, D * tt:D * (tt + 1)]
            v_t = at["v"][:, D * tt:D * (tt + 1)]
            a_t = at["attn"][:, D * tt:D * (tt + 1)]
            cosAP = at["cs_cos"][:, 64 * tt:64 * (tt + 1)].unsqueeze(1).broadcast_to((128, 16, 64))
            sinAP = at["cs_sin"][:, 64 * tt:64 * (tt + 1)].unsqueeze(1).broadcast_to((128, 16, 64))
            for rt in (q_t, k_t):
                rview = rt.rearrange("p (a two) -> p a two", two=2)
                p.chain("vector", [("dve", p.v("dve")), ("sy", p.v("sy")), ("pool", p.v("pool"))], [
                    lambda e, rt=rt: e.tensor_scalar(at["scr"][:].rearrange("p (a two) -> p a two", two=2)[:, :, 0:1],
                                                     rt.rearrange("p (a two) -> p a two", two=2)[:, :, 1:2],
                                                     -1.0, None, ALU.mult, ALU.bypass),
                    lambda e, rt=rt: e.tensor_copy(at["scr"][:].rearrange("p (a two) -> p a two", two=2)[:, :, 1:2],
                                                   rt.rearrange("p (a two) -> p a two", two=2)[:, :, 0:1]),
                    lambda e, rt=rt: e.tensor_tensor(at["scr2"][:].rearrange("p (hh d) -> p hh d", d=64),
                                                     rt.rearrange("p (hh d) -> p hh d", d=64), cosAP, ALU.mult),
                    lambda e: e.tensor_tensor(at["scr"][:].rearrange("p (hh d) -> p hh d", d=64),
                                              at["scr"][:].rearrange("p (hh d) -> p hh d", d=64), sinAP, ALU.mult),
                    lambda e, rt=rt: e.tensor_tensor(rt, at["scr2"][:], at["scr"][:], ALU.add),
                ])
            items = []
            for g in range(2):
                for n in range(8):
                    for m in range(8):
                        col = 64 * g + 8 * n + m
                        qs = q_t[:, 64 * (2 * n + g):64 * (2 * n + g + 1)]
                        ks = k_t[:, 64 * (2 * m + g):64 * (2 * m + g + 1)]
                        items.append((lambda e, qs=qs, ks=ks, col=col: e.scalar_tensor_tensor(
                            at["scr"][:, 0:64], qs, HDQ, ks, ALU.mult, ALU.mult,
                            accum_out=at["sc"][:, col:col + 1]), 1))
            p.multi("vector", [("dve", p.v("dve"))], items)
            scv = at["sc"][:].rearrange("p (gn m) -> p gn m", m=8)
            vsm = p.chain("vector", [], [
                lambda e: e.tensor_reduce(at["mx"][:], scv, AX.X, ALU.max),
                lambda e: e.tensor_scalar(at["mx"][:], at["mx"][:], -1.0, None, ALU.mult, ALU.bypass),
                lambda e: e.tensor_tensor(scv, scv, at["mx"][:].unsqueeze(2).broadcast_to((128, 16, 8)), ALU.add),
            ])
            vexp = p.step("scalar", [("dve", vsm)], lambda e: e.activation(at["sc"][:], at["sc"][:], AF.Exp), inc=1)
            vsm = p.chain("vector", [("act", vexp)], [
                lambda e: e.tensor_reduce(at["mx"][:], scv, AX.X, ALU.add),
                lambda e: e.reciprocal(at["mx"][:], at["mx"][:]),
                lambda e: e.tensor_tensor(scv, scv, at["mx"][:].unsqueeze(2).broadcast_to((128, 16, 8)), ALU.mult),
                lambda e: e.scalar_tensor_tensor(at["wm"][:], at["sc"][:, 64:128], at["neglam"][:, 0:1],
                                                 at["sc"][:, 0:64], ALU.mult, ALU.add),
            ])
            for n in range(8):
                fns = [(lambda e, n=n: e.tensor_scalar(a_t[:, 128 * n:128 * (n + 1)],
                                                       v_t[:, 0:128], at["wm"][:, 8 * n:8 * n + 1],
                                                       None, ALU.mult, ALU.bypass))]
                for m in range(1, 8):
                    fns.append(lambda e, n=n, m=m: e.scalar_tensor_tensor(
                        a_t[:, 128 * n:128 * (n + 1)], v_t[:, 128 * m:128 * (m + 1)],
                        at["wm"][:, 8 * n + m:8 * n + m + 1], a_t[:, 128 * n:128 * (n + 1)],
                        ALU.mult, ALU.add))
                p.chain("vector", [] if n else [("dve", vsm)], fns)
            vsq = p.step("scalar", [("dve", p.v("dve"))],
                         lambda e: e.activation(at["scr"][:], a_t, AF.Square), inc=1)
            vr = p.chain("vector", [("act", vsq)], [
                lambda e: e.tensor_reduce(at["rstd"][:], at["scr"][:].rearrange("p (n e2) -> p n e2", e2=128),
                                          AX.X, ALU.add),
                lambda e: e.tensor_scalar(at["rstd"][:], at["rstd"][:], 1.0 / 128, None, ALU.mult, ALU.bypass),
            ])
            vsq = p.step("scalar", [("dve", vr)],
                         lambda e: e.activation(at["rstd"][:], at["rstd"][:], AF.Sqrt, bias=at["eps8"][:],
                                                scale=1.0), inc=1)
            p.chain("vector", [("act", vsq)], [
                lambda e: e.reciprocal(at["rstd"][:], at["rstd"][:]),
                lambda e: e.tensor_tensor(a_t.rearrange("p (n e2) -> p n e2", e2=128),
                                          a_t.rearrange("p (n e2) -> p n e2", e2=128),
                                          at["rstd"][:].unsqueeze(2).broadcast_to((128, 8, 128)), ALU.mult),
                lambda e: e.tensor_tensor(a_t.rearrange("p (n e2) -> p n e2", e2=128),
                                          a_t.rearrange("p (n e2) -> p n e2", e2=128),
                                          at["lnw"][:].unsqueeze(1).broadcast_to((128, 8, 128)), ALU.mult),
                lambda e: e.tensor_tensor(a_t.rearrange("p (n e2) -> p n e2", e2=128),
                                          a_t.rearrange("p (n e2) -> p n e2", e2=128),
                                          at["lnb"][:].unsqueeze(1).broadcast_to((128, 8, 128)), ALU.add),
            ])
        v_attn = p.v("dve")

        for tt in range(4):
            for fc in range(FC):
                pb = 6 + (fc % 2)
                w = [("dve", v_attn if (tt == 0 and fc == 0) else 0), ("dve", st["tr_dve"][pb])]
                vt = p.step("tensor", w,
                            lambda e, tt=tt, fc=fc, pb=pb: e.transpose(
                                p.bank(pb)[:, 0:128],
                                at["attn"][:, D * tt + 128 * fc:D * tt + 128 * (fc + 1)],
                                cm["ident"][:]), inc=1)
                st["tr_dve"][pb] = p.step("vector", [("pe", vt)],
                                          lambda e, tt=tt, fc=fc, pb=pb: e.tensor_copy(
                                              at["af"][:, SEG1 * fc + 128 * tt:SEG1 * fc + 128 * (tt + 1)],
                                              p.bank(pb)[:, 0:128]), inc=1)
        v_af = p.v("dve")

        d0 = dma_w(W["Wo"], 0, 0)
        d1 = dma_w(W["Wo"], 1, 1)
        dmav = {0: d0, 1: d1}
        for mc in range(FC):
            buf = mc // 4
            bk = mc % 2
            waits = [("pool", dmav[buf]), ("dve", st["bk_dve"][bk])]
            if mc == 0:
                waits.append(("dve", v_af))
            items = []
            for kc in range(FC):
                items.append((lambda e, kc=kc, mc=mc, bk=bk, buf=buf: e.matmul(
                    p.bank(bk)[:, 0:512],
                    at["wbuf"][buf][:].rearrange("p (kc m) -> p kc m", m=512)[:, kc:kc + 1, 128 * (mc % 4):128 * (mc % 4 + 1)].squeeze(),
                    at["af"][:, SEG1 * kc:SEG1 * (kc + 1)],
                    start=(kc == 0), stop=(kc == FC - 1)), 1 if kc == FC - 1 else 0))
            vmm = p.multi("tensor", waits, items)
            st["bk_dve"][bk] = p.step("vector", [("pe", vmm)],
                                      lambda e, mc=mc, bk=bk, t0=t0: e.tensor_tensor(
                                          X[:, T * mc + t0:T * mc + t0 + SEG1],
                                          X[:, T * mc + t0:T * mc + t0 + SEG1],
                                          p.bank(bk)[:, 0:512], ALU.add), inc=1)
        st["wb_pe"] = [p.v("pe"), p.v("pe")]
    return



def build_ffn_program(layer, x_in_kind, x_out_kind="fm"):
    p = Prog(f"ffn{layer}")
    if x_in_kind == "tm":
        x_in = p.din("x_in", [T, D])
    else:
        x_in = p.din("x_in", [D, T])
    x_out = p.dout("x_out", [T, D] if x_out_kind == "tm" else [D, T])
    ident = p.din("c_ident", [128, 128])
    if layer == 0:
        rw = p.din("L0_ffn_w_router", [D, NE])
        w1 = p.din("L0_ffn_w1", [NE, D, 2816])
        w2r = p.din("w2r_l0", [NE, FC, 128, 22, 128])
        w3 = p.din("L0_ffn_w3", [NE, D, 2816])
        experts = [{"w1": w1[e:e + 1].squeeze(), "w3": w3[e:e + 1].squeeze(), "w2r": w2r[e:e + 1].squeeze()}
                   for e in range(NE)]
        cfg = MoECfg(2816, experts, rw, 0.0, 0.5)
        maxI = 2816
    else:
        rw = p.din("L1_ffn_Wr", [D, NE])
        wg = p.din("L1_ffn_Wg", [NE, D, 2048])
        wu = p.din("L1_ffn_Wu", [NE, D, 2048])
        wdr = p.din("w2r_l1", [NE, FC, 128, 16, 128])
        sg = p.din("L1_ffn_Sg", [D, 2048])
        su = p.din("L1_ffn_Su", [D, 2048])
        sdr = p.din("w2r_sd", [FC, 128, 16, 128])
        experts = [{"w1": wg[e:e + 1].squeeze(), "w3": wu[e:e + 1].squeeze(), "w2r": wdr[e:e + 1].squeeze()}
                   for e in range(NE)]
        experts.append({"w1": sg, "w3": su, "w2r": sdr, "shared": True})
        cfg = MoECfg(2048, experts, rw, 1e-20, 0.5)
        maxI = 2048
    p.start()
    cm = alloc_common(p)
    ln = alloc_ln(p, cm)
    ms = alloc_moe(p, cm, maxI)
    emit_init_consts(p, cm, ident)
    if x_in_kind == "tm":
        emit_load_x_tm(p, cm, x_in)
    else:
        emit_load_x_fm(p, cm, x_in)
    emit_ln_stats(p, cm, ln)
    emit_moe(p, cm, ln, ms, cfg)
    if x_out_kind == "tm":
        emit_store_x_tm(p, cm, x_out)
    else:
        emit_store_x_fm(p, cm, x_out)
    p.finalize()
    return p


def build_gla_program(x_in_kind="fm"):
    p = Prog("gla")
    x_in = p.din("x_in", [T, D] if x_in_kind == "tm" else [D, T])
    x_out = p.dout("x_out", [D, T])
    ident = p.din("c_ident", [128, 128])
    W = {}
    W["c_amask"] = p.din("c_amask", [32, 512])
    W["c_mask16"] = p.din("c_mask16", [128, T])
    W["c_mask32"] = p.din("c_mask32", [128, T])
    W["key_wT"] = p.din("key_wT", [D, D])
    W["vq_wT"] = p.din("vq_wT", [D, 2 * D])
    W["out_wT"] = p.din("out_wT", [D, D])
    W["key_b_r"] = p.din("key_b_r", [128, 8])
    W["out_b_r"] = p.din("out_b_r", [128, 8])
    W["c_ident64"] = p.din("c_ident64", [128, 64])
    p.start()
    cm = alloc_common(p)
    ln = alloc_ln(p, cm)
    gs = alloc_gla(p, cm)
    emit_init_consts(p, cm, ident)
    if x_in_kind == "tm":
        emit_load_x_tm(p, cm, x_in)
    else:
        emit_load_x_fm(p, cm, x_in)
    emit_ln_stats(p, cm, ln)
    emit_gla(p, cm, ln, gs, W)
    emit_store_x_fm(p, cm, x_out)
    p.finalize()
    return p


def gla_perm():
    perm = np.zeros(1024, np.int64)
    i = 0
    for fc in range(8):
        for s in range(2):
            for e in range(64):
                perm[i] = 64 * (fc + 8 * s) + e
                i += 1
    return perm


def gla_host_inputs(inp):
    pm = gla_perm()
    f = {}
    f["key_wT"] = np.ascontiguousarray(inp["L0_attn_key_w"].T[:, pm])
    vqT = inp["L0_attn_vq_w"].T
    f["vq_wT"] = np.ascontiguousarray(np.concatenate([vqT[:, :1024][:, pm], vqT[:, 1024:][:, pm]], axis=1))
    f["out_wT"] = np.ascontiguousarray(inp["L0_attn_out_w"].T[pm, :])
    f["key_b_r"] = np.ascontiguousarray(inp["L0_attn_key_b"][pm].reshape(8, 128).T)
    f["out_b_r"] = np.ascontiguousarray(inp["L0_attn_out_b"].reshape(8, 128).T)
    return f



def build_attn1_program():
    p = Prog("attn1")
    x_in = p.din("x_in", [D, T])
    x_out = p.dout("x_out", [D, T])
    ident = p.din("c_ident", [128, 128])
    W = {}
    W["Wq"] = p.din("L1_attn_Wq", [D, D])
    W["Wk"] = p.din("L1_attn_Wk", [D, D])
    W["Wv"] = p.din("L1_attn_Wv", [D, D])
    W["Wo"] = p.din("L1_attn_Wo", [D, D])
    W["lambda_q1"] = p.din("L1_attn_lambda_q1", [64])
    W["lambda_k1"] = p.din("L1_attn_lambda_k1", [64])
    W["lambda_q2"] = p.din("L1_attn_lambda_q2", [64])
    W["lambda_k2"] = p.din("L1_attn_lambda_k2", [64])
    W["ln_w_rep"] = p.din("ln_w_rep", [128, 128])
    W["ln_b_rep"] = p.din("ln_b_rep", [128, 128])
    W["c_cos"] = p.din("c_cos", [T, 64])
    W["c_sin"] = p.din("c_sin", [T, 64])
    p.start()
    cm = alloc_common(p)
    ln = alloc_ln(p, cm)
    at = alloc_attn1(p, cm)
    emit_init_consts(p, cm, ident)
    emit_load_x_fm(p, cm, x_in)
    emit_ln_stats(p, cm, ln)
    emit_attn1(p, cm, ln, at, W)
    emit_store_x_fm(p, cm, x_out)
    p.finalize()
    return p


def attn1_host_inputs(inp):
    return {"ln_w_rep": np.ascontiguousarray(np.broadcast_to(inp["L1_attn_ln_w"], (128, 128))),
            "ln_b_rep": np.ascontiguousarray(np.broadcast_to(inp["L1_attn_ln_b"], (128, 128)))}



def build_fused_program():
    p = Prog("fused")
    x_in = p.din("x_in", [T, D])
    x_out = p.dout("x_out", [T, D])
    ident = p.din("c_ident", [128, 128])
    rw0 = p.din("L0_ffn_w_router", [D, NE])
    w1_0 = p.din("L0_ffn_w1", [NE, D, 2816])
    w2r_0 = p.din("w2r_l0", [NE, FC, 128, 22, 128])
    w3_0 = p.din("L0_ffn_w3", [NE, D, 2816])
    experts0 = [{"w1": w1_0[e:e + 1].squeeze(), "w3": w3_0[e:e + 1].squeeze(), "w2r": w2r_0[e:e + 1].squeeze()}
                for e in range(NE)]
    cfg0 = MoECfg(2816, experts0, rw0, 0.0, 0.5)
    rw1 = p.din("L1_ffn_Wr", [D, NE])
    wg = p.din("L1_ffn_Wg", [NE, D, 2048])
    wu = p.din("L1_ffn_Wu", [NE, D, 2048])
    wdr = p.din("w2r_l1", [NE, FC, 128, 16, 128])
    sg = p.din("L1_ffn_Sg", [D, 2048])
    su = p.din("L1_ffn_Su", [D, 2048])
    sdr = p.din("w2r_sd", [FC, 128, 16, 128])
    experts1 = [{"w1": wg[e:e + 1].squeeze(), "w3": wu[e:e + 1].squeeze(), "w2r": wdr[e:e + 1].squeeze()}
                for e in range(NE)]
    experts1.append({"w1": sg, "w3": su, "w2r": sdr, "shared": True})
    cfg1 = MoECfg(2048, experts1, rw1, 1e-20, 0.5)
    GW = {}
    GW["c_amask"] = p.din("c_amask", [32, 512])
    GW["c_mask16"] = p.din("c_mask16", [128, T])
    GW["c_mask32"] = p.din("c_mask32", [128, T])
    GW["key_wT"] = p.din("key_wT", [D, D])
    GW["vq_wT"] = p.din("vq_wT", [D, 2 * D])
    GW["out_wT"] = p.din("out_wT", [D, D])
    GW["key_b_r"] = p.din("key_b_r", [128, 8])
    GW["out_b_r"] = p.din("out_b_r", [128, 8])
    GW["c_ident64"] = p.din("c_ident64", [128, 64])
    AW = {}
    AW["Wq"] = p.din("L1_attn_Wq", [D, D])
    AW["Wk"] = p.din("L1_attn_Wk", [D, D])
    AW["Wv"] = p.din("L1_attn_Wv", [D, D])
    AW["Wo"] = p.din("L1_attn_Wo", [D, D])
    AW["lambda_q1"] = p.din("L1_attn_lambda_q1", [64])
    AW["lambda_k1"] = p.din("L1_attn_lambda_k1", [64])
    AW["lambda_q2"] = p.din("L1_attn_lambda_q2", [64])
    AW["lambda_k2"] = p.din("L1_attn_lambda_k2", [64])
    AW["ln_w_rep"] = p.din("ln_w_rep", [128, 128])
    AW["ln_b_rep"] = p.din("ln_b_rep", [128, 128])
    AW["c_cos"] = p.din("c_cos", [T, 64])
    AW["c_sin"] = p.din("c_sin", [T, 64])

    p.start()
    cm = alloc_common(p)
    ln = alloc_ln(p, cm)
    emit_init_consts(p, cm, ident)
    p.push_scope()
    emit_load_x_tm(p, cm, x_in)
    p.barrier()
    p.pop_scope()
    p.new_epoch()

    def section(emitfn, allocfn):
        p.push_scope()
        buf = allocfn()
        emit_ln_stats(p, cm, ln)
        emitfn(buf)
        p.barrier()
        p.pop_scope()
        p.new_epoch()

    section(lambda ms: emit_moe(p, cm, ln, ms, cfg0), lambda: alloc_moe(p, cm, 2816))
    section(lambda gs: emit_gla(p, cm, ln, gs, GW), lambda: alloc_gla(p, cm))
    section(lambda ms: emit_moe(p, cm, ln, ms, cfg0), lambda: alloc_moe(p, cm, 2816))
    section(lambda ms: emit_moe(p, cm, ln, ms, cfg1), lambda: alloc_moe(p, cm, 2048))
    section(lambda at: emit_attn1(p, cm, ln, at, AW), lambda: alloc_attn1(p, cm))
    section(lambda ms: emit_moe(p, cm, ln, ms, cfg1), lambda: alloc_moe(p, cm, 2048))

    p.push_scope()
    emit_store_x_tm(p, cm, x_out)
    p.pop_scope()
    p.finalize()
    return p


def kernel_fused(inputs):
    consts = host_constants()
    extra = host_extra(inputs)
    x = inputs["L0_attn_x"]
    prog = build_fused_program()
    in_maps = []
    for i in range(8):
        in_maps.append(make_feeds(prog, inputs, consts, extra, np.ascontiguousarray(x[i])))
    res = run_bass_kernel_spmd(prog.nc, in_maps, core_ids=list(range(8)))
    return np.stack([np.asarray(r["x_out"]) for r in res.results], axis=0).astype(np.float32)



def _run(p: Prog, per_core_inputs):
    res = run_bass_kernel_spmd(p.nc, per_core_inputs, core_ids=list(range(len(per_core_inputs))))
    return res.results


def make_feeds(p, inputs, consts, extra, x_shard):
    feed = {}
    for n in p.in_names:
        if n == "x_in":
            feed[n] = x_shard
        elif n in extra:
            feed[n] = extra[n]
        elif n in consts:
            feed[n] = consts[n]
        elif n in inputs:
            feed[n] = inputs[n]
        else:
            raise KeyError(n)
    return feed


def host_extra(inputs):
    extra = {}
    extra["w2r_l0"] = relayout_w2(inputs["L0_ffn_w2"])
    extra["w2r_l1"] = relayout_w2(inputs["L1_ffn_Wd"])
    extra["w2r_sd"] = relayout_w2(inputs["L1_ffn_Sd"])
    extra.update(gla_host_inputs(inputs))
    extra.update(attn1_host_inputs(inputs))
    return extra


def kernel(**inputs):
    inputs = {k: np.ascontiguousarray(v, dtype=np.float32) for k, v in inputs.items()}
    return kernel_fused(inputs)


def kernel_unfused(**inputs):
    inputs = {k: np.ascontiguousarray(v, dtype=np.float32) for k, v in inputs.items()}
    consts = host_constants()
    extra = host_extra(inputs)
    x = inputs["L0_attn_x"]
    shards = [np.ascontiguousarray(x[i]) for i in range(8)]
    progs = [
        build_ffn_program(0, "tm"),
        build_gla_program("fm"),
        build_ffn_program(0, "fm"),
        build_ffn_program(1, "fm"),
        build_attn1_program(),
        build_ffn_program(1, "fm", "tm"),
    ]
    cur = shards
    for prog in progs:
        in_maps = [make_feeds(prog, inputs, consts, extra, cur[i]) for i in range(8)]
        res = run_bass_kernel_spmd(prog.nc, in_maps, core_ids=list(range(8)))
        cur = [np.asarray(r["x_out"]) for r in res.results]
    return np.stack(cur, axis=0).astype(np.float32)


# version 07
# speedup vs baseline: 1.1271x; 1.0471x over previous
import numpy as np
import concourse.bass as bass
import concourse.mybir as mybir
from concourse.bass_utils import run_bass_kernel_spmd

F32 = mybir.dt.float32
BF16 = mybir.dt.bfloat16
AF = mybir.ActivationFunctionType
ALU = mybir.AluOpType
AX = mybir.AxisListType

T = 2048
D = 1024
FC = 8
NE = 8

SEM_OF = {"tensor": "pe", "vector": "dve", "scalar": "act", "gpsimd": "pool", "sync": "sy"}


def relayout_w2(w2):
    sh = w2.shape
    I, Dd = sh[-2], sh[-1]
    lead = sh[:-2]
    r = w2.reshape(lead + (I // 128, 128, Dd // 128, 128))
    perm = tuple(range(len(lead))) + (len(lead) + 2, len(lead) + 1, len(lead), len(lead) + 3)
    return np.ascontiguousarray(r.transpose(perm))


def host_constants():
    c = {}
    c["c_ident"] = np.eye(128, dtype=np.float32)
    d = 64
    inv_freq = 1.0 / (10000.0 ** (np.arange(0, d, 2, dtype=np.float32) / d))
    pos = np.arange(T, dtype=np.float32)
    freqs = np.repeat(pos[:, None] * inv_freq[None, :], 2, axis=1)
    c["c_cos"] = np.cos(freqs).astype(np.float32)
    c["c_sin"] = np.sin(freqs).astype(np.float32)
    m16 = np.ones((128, T), np.float32); m16[:, 0::16] = 0.0
    m32 = np.ones((128, T), np.float32); m32[:, 0::32] = 0.0
    c["c_mask16"] = m16
    c["c_mask32"] = m32
    tri = (np.arange(16)[:, None] <= np.arange(16)[None, :]).astype(np.float32)
    am = np.zeros((32, 32), np.float32)
    am[0:16, 0:16] = tri
    am[0:16, 16:32] = 1.0
    am[16:32, 16:32] = tri
    c["c_amask"] = np.tile(am, (1, 16)).astype(np.float32)
    i64 = np.zeros((128, 64), np.float32)
    i64[0:64] = np.eye(64); i64[64:128] = np.eye(64)
    c["c_ident64"] = i64
    return c


class Prog:
    def __init__(self, name="prog"):
        self.nc = bass.Bass("TRN2", target_bir_lowering=False)
        self.name = name
        self._ctx = []
        self.sem = {}
        self.cnt = {}
        self.out_names = []
        self.in_names = []

    def din(self, name, shape, dtype=F32):
        self.in_names.append(name)
        return self.nc.dram_tensor(name, list(shape), dtype, kind="ExternalInput").ap()

    def dout(self, name, shape, dtype=F32):
        self.out_names.append(name)
        return self.nc.dram_tensor(name, list(shape), dtype, kind="ExternalOutput").ap()

    def dtmp(self, name, shape, dtype=F32):
        return self.nc.dram_tensor(name, list(shape), dtype, kind="Internal").ap()

    def sb(self, name, shape, dtype=F32):
        self._name_ctr = getattr(self, "_name_ctr", 0) + 1
        h = self.nc.sbuf_tensor(f"{name}_{self._name_ctr}", list(shape), dtype)
        t = h.__enter__(); self._ctx.append(h)
        return t

    def start(self):
        self._sem_ctx = []
        self._scopes = []
        self._epoch = 0
        self.new_epoch(first=True)
        h = self.nc.psum_tensor("ps_arena", [128, 4096], F32)
        self.psum = h.__enter__(); self._ctx.append(h)
        h = self.nc.Block()
        self.block = h.__enter__(); self._ctx.append(h)

    def new_epoch(self, first=False):
        for s in ["pe", "dve", "act", "pool", "sy"]:
            h = self.nc.semaphore(f"{s}{self._epoch}")
            self.sem[s] = h.__enter__(); self._sem_ctx.append(h)
            self.cnt[s] = 0
        self._epoch += 1

    def push_scope(self):
        self._scopes.append(len(self._ctx))

    def pop_scope(self):
        i = self._scopes.pop()
        for h in reversed(self._ctx[i:]):
            h.__exit__(None, None, None)
        del self._ctx[i:]

    def bank(self, b, n=1):
        return self.psum[:, 512 * b:512 * (b + n)]

    def finalize(self):
        for h in reversed(self._ctx):
            h.__exit__(None, None, None)
        self._ctx = []
        for h in reversed(self._sem_ctx):
            h.__exit__(None, None, None)
        self._sem_ctx = []

    def step(self, eng, waits, fn, inc=1):
        own = SEM_OF[eng]
        sem = self.sem
        def body(e):
            for s, v in waits:
                if v > 0:
                    e.wait_ge(sem[s], v)
            last = fn(e)
            if inc:
                last.then_inc(sem[own], inc)
        getattr(self.block, eng)(body)
        if inc:
            self.cnt[own] += inc
        return self.cnt[own]

    def multi(self, eng, waits, items):
        own = SEM_OF[eng]
        sem = self.sem
        def body(e):
            for s, v in waits:
                if v > 0:
                    e.wait_ge(sem[s], v)
            for fn, inc in items:
                inst = fn(e)
                if inc:
                    inst.then_inc(sem[own], inc)
        getattr(self.block, eng)(body)
        self.cnt[own] += sum(i for _, i in items)
        return self.cnt[own]

    def chain(self, eng, waits, fns, inc_last=1):
        own = SEM_OF[eng]
        sem = self.sem
        base = self.cnt[own]
        n = len(fns)
        def body(e):
            for s, v in waits:
                if v > 0:
                    e.wait_ge(sem[s], v)
            for i, fn in enumerate(fns):
                if i > 0:
                    e.wait_ge(sem[own], base + i)
                fn(e).then_inc(sem[own], 1)
        getattr(self.block, eng)(body)
        self.cnt[own] += n
        return self.cnt[own]

    def v(self, s):
        return self.cnt[s]

    def barrier(self):
        snap = dict(self.cnt)
        for eng in ["tensor", "vector", "scalar", "gpsimd", "sync"]:
            own = SEM_OF[eng]
            waits = [(s, v) for s, v in snap.items() if s != own and v > 0]
            self.step(eng, waits, lambda e: e.nop(), inc=1)

    def wait_all(self, exclude=()):
        return [(s, self.cnt[s]) for s in ["pe", "dve", "act", "pool", "sy"] if s not in exclude and self.cnt[s] > 0]



def alloc_common(p):
    cm = {}
    cm["X"] = p.sb("X", [128, FC * T], F32)
    cm["ident"] = p.sb("ident", [128, 128], F32)
    cm["ones_col_bf"] = p.sb("ones_col_bf", [128, 1], BF16)
    cm["ones_row"] = p.sb("ones_row", [1, 128], F32)
    cm["eps_ln"] = p.sb("eps_ln", [1, 1], F32)
    return cm


def emit_init_consts(p, cm, ident_dram):
    p.step("sync", [], lambda e: e.dma_start(out=cm["ident"][:], in_=ident_dram), inc=16)
    p.multi("vector", [], [
        (lambda e: e.memset(cm["ones_col_bf"][:], 1.0), 0),
        (lambda e: e.memset(cm["ones_row"][:], 1.0), 0),
        (lambda e: e.memset(cm["eps_ln"][:], 1e-5), 1),
    ])


def emit_load_x_fm(p, cm, x_fm):
    items = [(lambda e, c=c: e.dma_start(out=cm["X"][:, T * c:T * (c + 1)],
                                         in_=x_fm[128 * c:128 * (c + 1), :]), 16) for c in range(FC)]
    return p.multi("sync", [], items)


def emit_store_x_fm(p, cm, x_fm):
    items = [(lambda e, c=c: e.dma_start(out=x_fm[128 * c:128 * (c + 1), :],
                                         in_=cm["X"][:, T * c:T * (c + 1)]), 16) for c in range(FC)]
    v = p.multi("sync", p.wait_all(exclude=("sy",)), items)
    p.step("sync", [("sy", v)], lambda e: e.nop(), inc=0)


def emit_load_x_tm(p, cm, x_tm):
    stg = [p.sb("xin_stg0", [128, D], F32), p.sb("xin_stg1", [128, D], F32)]
    dma_v = {}
    pe_after_tile = {}
    for tt in range(16):
        b = tt % 2
        w = [("pe", pe_after_tile[tt - 2])] if tt >= 2 else []
        dma_v[tt] = p.step("sync", w,
                           lambda e, tt=tt, b=b: e.dma_start(out=stg[b][:], in_=x_tm[128 * tt:128 * (tt + 1), :]),
                           inc=16)
        for c in range(FC):
            pb = 6 + (c % 2)
            w = [("sy", dma_v[tt])]
            w.append(("dve", max(0, p.v("dve") - 1)))
            vpe = p.step("tensor", w,
                         lambda e, c=c, b=b, pb=pb: e.transpose(p.bank(pb)[:, 0:128],
                                                                stg[b][:, 128 * c:128 * (c + 1)], cm["ident"][:]),
                         inc=1)
            p.step("vector", [("pe", vpe)],
                   lambda e, c=c, tt=tt, pb=pb: e.tensor_copy(cm["X"][:, T * c + 128 * tt:T * c + 128 * (tt + 1)],
                                                              p.bank(pb)[:, 0:128]),
                   inc=1)
        pe_after_tile[tt] = p.v("pe")
    return p.v("dve")


def emit_store_x_tm(p, cm, x_tm):
    stg = [p.sb("xout_stg0", [128, D], F32), p.sb("xout_stg1", [128, D], F32)]
    first_waits = p.wait_all(exclude=("pe",))
    dma_v = {}
    for tt in range(16):
        b = tt % 2
        for c in range(FC):
            pb = 6 + (c % 2)
            w = list(first_waits) if (tt == 0 and c == 0) else []
            w.append(("dve", max(0, p.v("dve") - 1)))
            if tt >= 2 and c == 0:
                w.append(("sy", dma_v[tt - 2]))
            vpe = p.step("tensor", w,
                         lambda e, c=c, tt=tt, pb=pb: e.transpose(p.bank(pb)[:, 0:128],
                                                                  cm["X"][:, T * c + 128 * tt:T * c + 128 * (tt + 1)],
                                                                  cm["ident"][:]),
                         inc=1)
            p.step("vector", [("pe", vpe)],
                   lambda e, c=c, b=b, pb=pb: e.tensor_copy(stg[b][:, 128 * c:128 * (c + 1)], p.bank(pb)[:, 0:128]),
                   inc=1)
        dma_v[tt] = p.step("sync", [("dve", p.v("dve"))],
                           lambda e, tt=tt, b=b: e.dma_start(out=x_tm[128 * tt:128 * (tt + 1), :], in_=stg[b][:]),
                           inc=16)
    p.step("sync", [("sy", p.v("sy"))], lambda e: e.nop(), inc=0)



def alloc_ln(p, cm):
    ln = {}
    ln["a_row"] = p.sb("ln_a", [1, T], F32)
    ln["b_row"] = p.sb("ln_b", [1, T], F32)
    ln["scr"] = p.sb("ln_scr", [128, 1024], F32)
    return ln


def emit_ln_stats(p, cm, ln):
    X = cm["X"]
    ps_s = p.psum[0:1, 0:T]
    ps_q = p.psum[0:1, T:2 * T]
    p.push_scope()
    ln["xb"] = [p.sb(f"ln_xb{i}", [128, 512], BF16) for i in range(4)]
    ln["xsq"] = [p.sb(f"ln_xsq{i}", [128, 512], BF16) for i in range(4)]
    first = True
    for w4 in range(4):
        wv = p.wait_all(exclude=("dve",)) if first else [("pe", p.v("pe"))]
        first = False
        for hb in range(2):
            cs = range(4 * hb, 4 * hb + 4)
            wv2 = wv if hb == 0 else [("pe", p.v("pe"))]
            cast_items = [(lambda e, c=c, w4=w4: e.tensor_copy(ln["xb"][c % 4][:],
                                                               X[:, T * c + 512 * w4:T * c + 512 * (w4 + 1)]),
                           1) for c in cs]
            vc = p.multi("vector", wv2, cast_items)
            mm_items = []
            for c in cs:
                mm_items.append((lambda e, c=c, w4=w4: e.matmul(ps_s[:, 512 * w4:512 * (w4 + 1)],
                                                                cm["ones_col_bf"][:], ln["xb"][c % 4][:],
                                                                start=(c == 0), stop=(c == FC - 1)), 1))
            p.multi("tensor", [("dve", vc)], mm_items)
        for hb in range(2):
            cs = range(4 * hb, 4 * hb + 4)
            sq_items = [(lambda e, c=c, w4=w4: e.activation(ln["xsq"][c % 4][:],
                                                            X[:, T * c + 512 * w4:T * c + 512 * (w4 + 1)],
                                                            AF.Square), 1) for c in cs]
            vs = p.multi("scalar", [("pe", p.v("pe") - (4 if hb == 0 else 0))], sq_items)
            mm_items = []
            for c in cs:
                mm_items.append((lambda e, c=c, w4=w4: e.matmul(ps_q[:, 512 * w4:512 * (w4 + 1)],
                                                                cm["ones_col_bf"][:], ln["xsq"][c % 4][:],
                                                                start=(c == 0), stop=(c == FC - 1)), 1))
            p.multi("tensor", [("act", vs)], mm_items)
    vpe = p.v("pe")
    a_row, b_row = ln["a_row"], ln["b_row"]
    vv = p.chain("vector", [("pe", vpe)], [
        lambda e: e.tensor_scalar(ps_s, ps_s, 1.0 / D, None, ALU.mult, ALU.bypass),
        lambda e: e.tensor_scalar(ps_q, ps_q, 1.0 / D, None, ALU.mult, ALU.bypass),
        lambda e: e.tensor_copy(b_row[:], ps_s),
        lambda e: e.tensor_tensor(a_row[:], b_row[:], b_row[:], ALU.mult),
        lambda e: e.tensor_tensor(ps_q, ps_q, a_row[:], ALU.subtract),
    ])
    va = p.step("scalar", [("dve", vv)],
                lambda e: e.activation(a_row[:], ps_q, AF.Sqrt, bias=cm["eps_ln"][0:1, :], scale=1.0), inc=1)
    p.chain("vector", [("act", va)], [
        lambda e: e.reciprocal(a_row[:], a_row[:]),
        lambda e: e.scalar_tensor_tensor(b_row[:], b_row[:], -1.0, a_row[:], ALU.mult, ALU.mult),
    ])
    p.pop_scope()
    return p.v("dve")


def emit_ln_apply(p, cm, ln, t0, L, Xn, xn_stride=None):
    X = cm["X"]
    xn_stride = L if xn_stride is None else xn_stride
    assert L <= 512
    a_bc = p.bank(4)[:, 0:L]
    b_bc = p.bank(5)[:, 0:L]
    items = [
        (lambda e: e.matmul(a_bc, cm["ones_row"][:], ln["a_row"][0:1, t0:t0 + L], start=True, stop=True), 0),
        (lambda e: e.matmul(b_bc, cm["ones_row"][:], ln["b_row"][0:1, t0:t0 + L], start=True, stop=True), 1),
    ]
    vbc = p.multi("tensor", [("dve", p.v("dve"))], items)
    for c in range(FC):
        p.chain("vector", [("pe", vbc)] if c == 0 else [], [
            (lambda e, c=c: e.tensor_tensor(ln["scr"][:, 0:L], X[:, T * c + t0:T * c + t0 + L], a_bc, ALU.mult)),
            (lambda e, c=c: e.tensor_tensor(Xn[:, xn_stride * c:xn_stride * c + L], ln["scr"][:, 0:L], b_bc, ALU.add)),
        ])
    return p.v("dve")


def emit_ln_apply_off(p, cm, ln, t0, L, Xn, dst_off, xn_stride):
    X = cm["X"]
    a_bc = p.bank(4)[:, 0:L]
    b_bc = p.bank(5)[:, 0:L]
    items = [
        (lambda e: e.matmul(a_bc, cm["ones_row"][:], ln["a_row"][0:1, t0:t0 + L], start=True, stop=True), 0),
        (lambda e: e.matmul(b_bc, cm["ones_row"][:], ln["b_row"][0:1, t0:t0 + L], start=True, stop=True), 1),
    ]
    vbc = p.multi("tensor", [("dve", p.v("dve"))], items)
    for c in range(FC):
        p.chain("vector", [("pe", vbc)] if c == 0 else [], [
            (lambda e, c=c: e.tensor_tensor(ln["scr"][:, 0:L], X[:, T * c + t0:T * c + t0 + L], a_bc, ALU.mult)),
            (lambda e, c=c: e.tensor_tensor(Xn[:, xn_stride * c + dst_off:xn_stride * c + dst_off + L],
                                            ln["scr"][:, 0:L], b_bc, ALU.add)),
        ])
    return p.v("dve")



class MoECfg:
    def __init__(self, I, experts, router_w, eps_router, scale):
        self.I = I
        self.IC = I // 128
        self.experts = experts
        self.router_w = router_w
        self.eps_router = eps_router
        self.scale = scale


def alloc_moe(p, cm, maxI):
    ms = {}
    ms["Xn"] = p.sb("moe_xn", [128, FC * 1024], BF16)
    ms["h"] = p.sb("moe_h", [128, (maxI // 128) * 1024], BF16)
    ms["w1t"] = [p.sb("moe_w1t0", [128, 8 * 256], BF16), p.sb("moe_w1t1", [128, 8 * 256], BF16)]
    ms["w3t"] = [p.sb("moe_w3t0", [128, 8 * 256], BF16), p.sb("moe_w3t1", [128, 8 * 256], BF16)]
    ms["w2t"] = [p.sb("moe_w2t0", [128, (maxI // 128) * 128], BF16), p.sb("moe_w2t1", [128, (maxI // 128) * 128], BF16)]
    ms["bc"] = p.sb("moe_bc", [128, 1024], F32)
    ms["ctrow2"] = [p.sb("moe_ctrow0", [1, 1024], F32), p.sb("moe_ctrow1", [1, 1024], F32)]
    ms["scr512"] = [p.sb("moe_scr512a", [128, 512], BF16), p.sb("moe_scr512b", [128, 512], BF16)]
    ms["st"] = {"gi": 0, "gu_evac": {0: 0, 1: 0}, "si_read": [0, 0],
                "w13_pe": {0: 0, 1: 0}, "w2_pe": {0: 0, 1: 0}, "yi": 0,
                "y_bank_evac": {5: 0, 6: 0}, "pe_bc": 0, "dve_after_y": 0,
                "bc_guard": {0: 0, 1: 0}, "bank4_dve": 0}
    ms["rw"] = p.sb("moe_rw", [128, 8 * 8], BF16)
    ms["sc_sb"] = p.sb("moe_scsb", [8, 1024], F32)
    ms["cw"] = p.sb("moe_cw", [128, 64], F32)
    ms["p_sb"] = p.sb("moe_p", [128, 64], F32)
    ms["t_sb"] = p.sb("moe_t", [128, 64], F32)
    ms["m_sb"] = p.sb("moe_m", [128, 32], F32)
    ms["ct"] = p.sb("moe_ct", [8, T], F32)
    return ms


def emit_moe(p, cm, ln, ms, cfg: MoECfg):
    X = cm["X"]
    IC = cfg.IC
    Xn = ms["Xn"]
    h = ms["h"]
    nwin = (cfg.I + 255) // 256
    wsz = [min(256, cfg.I - 256 * w) for w in range(nwin)]
    ngrp = (IC + 3) // 4
    gsz = [min(4, IC - 4 * g) for g in range(ngrp)]

    if cfg.router_w is not None:
        rw3 = cfg.router_w.rearrange("(kc p) e -> p kc e", p=128)
        v_rw = p.step("gpsimd", [], lambda e: e.dma_start(out=ms["rw"][:].rearrange("p (kc e) -> p kc e", e=8), in_=rw3),
                      inc=16)

    for half in range(2):
        t0 = half * 1024
        emit_ln_apply(p, cm, ln, t0, 512, Xn, xn_stride=1024)
        emit_ln_apply_off(p, cm, ln, t0 + 512, 512, Xn, 512, xn_stride=1024)
        v_xn = p.v("dve")

        if cfg.router_w is not None:
            ps_sc = p.psum[0:8, 0:1024]
            items = []
            for w in range(2):
                for kc in range(FC):
                    items.append((lambda e, w=w, kc=kc: e.matmul(
                        ps_sc[:, 512 * w:512 * (w + 1)],
                        ms["rw"][:].rearrange("p (kc e) -> p kc e", e=8)[:, kc:kc + 1, :].squeeze(),
                        Xn[:, 1024 * kc + 512 * w:1024 * kc + 512 * (w + 1)],
                        start=(kc == 0), stop=(kc == FC - 1)), 1 if (kc == FC - 1 and w == 1) else 0))
            v_sc = p.multi("tensor", [("dve", v_xn), ("pool", v_rw)], items)
            v_ev = p.step("vector", [("pe", v_sc)], lambda e: e.tensor_copy(ms["sc_sb"][:], ps_sc), inc=1)
            items = []
            for tt in range(8):
                items.append((lambda e, tt=tt: e.transpose(p.bank(2)[0:128, 8 * tt:8 * (tt + 1)],
                                                           ms["sc_sb"][0:8, 128 * tt:128 * (tt + 1)],
                                                           cm["ident"][0:8, 0:8]), 1 if tt == 7 else 0))
            vpe = p.multi("tensor", [("dve", v_ev)], items)
            pb, tb, mb, cwb = ms["p_sb"], ms["t_sb"], ms["m_sb"], ms["cw"]
            pbv = pb[:].rearrange("p (t m) -> p t m", m=8)
            tbv = tb[:].rearrange("p (t m) -> p t m", m=8)
            def bc8(ap):
                return ap.unsqueeze(2).broadcast_to((128, 8, 8))
            mx, m1, m2, den = mb[:, 0:8], mb[:, 8:16], mb[:, 16:24], mb[:, 24:32]
            p.chain("vector", [("pe", vpe)], [
                lambda e: e.tensor_copy(pb[:], p.bank(2)[0:128, 0:64]),
                lambda e: e.tensor_reduce(mx, pbv, AX.X, ALU.max),
                lambda e: e.tensor_tensor(pbv, pbv, bc8(mx), ALU.subtract),
            ])
            va = p.step("scalar", [("dve", p.v("dve"))],
                        lambda e: e.activation(pb[:], pb[:], AF.Exp), inc=1)
            p.chain("vector", [("act", va)], [
                lambda e: e.tensor_reduce(m1, pbv, AX.X, ALU.add),
                lambda e: e.reciprocal(m1, m1),
                lambda e: e.tensor_tensor(pbv, pbv, bc8(m1), ALU.mult),
                lambda e: e.tensor_reduce(m1, pbv, AX.X, ALU.max),
                lambda e: e.tensor_tensor(tbv, pbv, bc8(m1), ALU.is_equal),
                lambda e: e.scalar_tensor_tensor(tb[:], tb[:], -1e30, pb[:], ALU.mult, ALU.add),
                lambda e: e.tensor_reduce(m2, tbv, AX.X, ALU.max),
                lambda e: e.tensor_tensor(tbv, pbv, bc8(m2), ALU.is_ge),
                lambda e: e.tensor_tensor(den, m1, m2, ALU.add),
                lambda e: e.tensor_scalar(den, den, cfg.eps_router, None, ALU.add, ALU.bypass),
                lambda e: e.reciprocal(den, den),
                lambda e: e.tensor_scalar(den, den, cfg.scale, None, ALU.mult, ALU.bypass),
                lambda e: e.tensor_tensor(pbv, pbv, bc8(den), ALU.mult),
                lambda e: e.tensor_tensor(cwb[:], pb[:], tb[:], ALU.mult),
            ])
            for tt in range(8):
                vpe2 = p.step("tensor", [("dve", p.v("dve"))],
                              lambda e, tt=tt: e.transpose(p.bank(2)[0:8, 0:128], cwb[:, 8 * tt:8 * (tt + 1)],
                                                           cm["ident"][:]), inc=1)
                p.step("vector", [("pe", vpe2)],
                       lambda e, tt=tt: e.tensor_copy(ms["ct"][0:8, t0 + 128 * tt:t0 + 128 * (tt + 1)],
                                                      p.bank(2)[0:8, 0:128]),
                       inc=1)
            dbg = ms.get("dbg", {})
            if half == 0 and "ct" in dbg:
                p.step("sync", [("dve", p.v("dve"))], lambda e: e.dma_start(out=dbg["ct"], in_=ms["ct"][:, 0:1024]), inc=16)
            if half == 0 and "xn" in dbg:
                p.step("sync", [("dve", p.v("dve"))], lambda e: e.dma_start(out=dbg["xn"], in_=Xn[:]), inc=16)

        st = ms["st"]
        st["dve_after_y"] = p.v("dve")
        n_routed = sum(0 if ex.get("shared") else 1 for ex in cfg.experts)
        st["ctr_sy"] = {0: p.step("sync", [("pe", st["pe_bc"]), ("dve", p.v("dve"))],
                                  lambda e: e.dma_start(out=ms["ctrow2"][0][:], in_=ms["ct"][0:1, t0:t0 + 1024]),
                                  inc=16)}
        for ei, ex in enumerate(cfg.experts):
            shared = ex.get("shared", False)
            bcb = ms["bc"]
            if ei + 1 < n_routed:
                st["ctr_sy"][ei + 1] = p.step("sync", [("pe", st["pe_bc"]), ("dve", p.v("dve"))],
                                              lambda e, ei=ei: e.dma_start(out=ms["ctrow2"][(ei + 1) % 2][:],
                                                                           in_=ms["ct"][ei + 1:ei + 2, t0:t0 + 1024]),
                                              inc=16)
            if shared:
                v_bcd = p.step("vector", [("dve", st["dve_after_y"])],
                               lambda e, bcb=bcb: e.memset(bcb[:], cfg.scale), inc=1)
            else:
                ctr = ms["ctrow2"][ei % 2]
                v_mm0 = p.multi("tensor", [("sy", st["ctr_sy"][ei]), ("dve", st["bc_guard"][ei % 2]),
                                           ("dve", st["bank4_dve"])], [
                    (lambda e, ctr=ctr: e.matmul(p.bank(4)[:, 0:512], cm["ones_row"][:],
                                                 ctr[0:1, 0:512], start=True, stop=True), 0),
                    (lambda e, ctr=ctr: e.matmul(p.bank(4)[:, 512:1024] if False else p.bank(4)[:, 0:512],
                                                 cm["ones_row"][:], ctr[0:1, 512:1024],
                                                 start=True, stop=True), 1),
                ]) if False else None
                v_mm0 = p.step("tensor", [("sy", st["ctr_sy"][ei]), ("dve", st["dve_after_y"]),
                                          ("dve", st["bank4_dve"])],
                               lambda e, ctr=ctr: e.matmul(p.bank(4)[:, 0:512], cm["ones_row"][:],
                                                           ctr[0:1, 0:512], start=True, stop=True), inc=1)
                v_e0 = p.step("vector", [("pe", v_mm0)],
                              lambda e, bcb=bcb: e.tensor_copy(bcb[:, 0:512], p.bank(4)[:, 0:512]), inc=1)
                v_mm1 = p.step("tensor", [("dve", v_e0)],
                               lambda e, ctr=ctr: e.matmul(p.bank(4)[:, 0:512], cm["ones_row"][:],
                                                           ctr[0:1, 512:1024], start=True, stop=True), inc=1)
                v_bcd = p.step("vector", [("pe", v_mm1)],
                               lambda e, bcb=bcb: e.tensor_copy(bcb[:, 512:1024], p.bank(4)[:, 0:512]), inc=1)
                st["bank4_dve"] = v_bcd
            st["pe_bc"] = p.v("pe")

            def dma_win(w):
                buf = w % 2
                wait = [("pe", st["w13_pe"][buf])]
                src1 = ex["w1"][:, 256 * w:256 * w + wsz[w]].rearrange("(kc p) m -> p kc m", p=128)
                src3 = ex["w3"][:, 256 * w:256 * w + wsz[w]].rearrange("(kc p) m -> p kc m", p=128)
                d1 = ms["w1t"][buf][:, 0:8 * wsz[w]].rearrange("p (kc m) -> p kc m", kc=8)
                d3 = ms["w3t"][buf][:, 0:8 * wsz[w]].rearrange("p (kc m) -> p kc m", kc=8)
                return p.multi("gpsimd", wait, [
                    (lambda e: e.dma_start(out=d1, in_=src1), 16),
                    (lambda e: e.dma_start(out=d3, in_=src3), 16),
                ])
            dma_w13 = {0: dma_win(0)}
            if nwin > 1:
                dma_w13[1] = dma_win(1)
            gi0 = st["gi"]
            for w in range(nwin):
                buf = w % 2
                nmc = wsz[w] // 128
                for mcw in range(nmc):
                    mc = 2 * w + mcw
                    for ts in range(2):
                        gi = st["gi"]
                        bg = gi % 2
                        waits = [("pool", dma_w13[w]), ("dve", st["gu_evac"][bg])]
                        items = []
                        for kc in range(FC):
                            items.append((lambda e, kc=kc, buf=buf, mcw=mcw, ts=ts, bg=bg, w=w: e.matmul(
                                p.bank(bg)[:, 0:512],
                                ms["w1t"][buf][:, 0:8 * wsz[w]].rearrange("p (kc m) -> p kc m", kc=8)[:, kc:kc + 1, 128 * mcw:128 * (mcw + 1)].squeeze(),
                                Xn[:, 1024 * kc + 512 * ts:1024 * kc + 512 * (ts + 1)],
                                start=(kc == 0), stop=(kc == FC - 1)), 0))
                        for kc in range(FC):
                            items.append((lambda e, kc=kc, buf=buf, mcw=mcw, ts=ts, bg=bg, w=w: e.matmul(
                                p.bank(2 + bg)[:, 0:512],
                                ms["w3t"][buf][:, 0:8 * wsz[w]].rearrange("p (kc m) -> p kc m", kc=8)[:, kc:kc + 1, 128 * mcw:128 * (mcw + 1)].squeeze(),
                                Xn[:, 1024 * kc + 512 * ts:1024 * kc + 512 * (ts + 1)],
                                start=(kc == 0), stop=(kc == FC - 1)), 1 if kc == FC - 1 else 0))
                        vmm = p.multi("tensor", waits, items)
                        sbuf_si = gi % 2
                        va = p.step("scalar", [("pe", vmm), ("dve", st["si_read"][sbuf_si])],
                                    lambda e, bg=bg, sbuf_si=sbuf_si: e.activation(
                                        ms["scr512"][sbuf_si][:], p.bank(bg)[:, 0:512], AF.Silu),
                                    inc=1)
                        st["gu_evac"][bg] = p.step("vector", [("act", va)],
                                                   lambda e, mc=mc, ts=ts, bg=bg, sbuf_si=sbuf_si: e.tensor_tensor(
                                                       h[:, 1024 * mc + 512 * ts:1024 * mc + 512 * (ts + 1)],
                                                       ms["scr512"][sbuf_si][:], p.bank(2 + bg)[:, 0:512], ALU.mult),
                                                   inc=1)
                        st["si_read"][sbuf_si] = st["gu_evac"][bg]
                        st["gi"] += 1
                st["w13_pe"][buf] = p.v("pe")
                if w + 2 < nwin:
                    dma_w13[w + 2] = dma_win(w + 2)
            v_h_done = p.v("dve")
            dbg = ms.get("dbg", {})
            if half == 0 and ei == 0 and "h" in dbg:
                p.step("sync", [("dve", v_h_done)], lambda e: e.dma_start(out=dbg["h"], in_=h[:, 0:IC * 1024]), inc=16)
            if half == 0 and ei == 0 and "bc" in dbg:
                p.step("sync", [("dve", v_h_done)], lambda e: e.dma_start(out=dbg["bc"], in_=ms["bc"][:]), inc=16)

            def dma_hc(hc):
                buf = hc % 2
                src = ex["w2r"][hc:hc + 1].squeeze()
                dd = ms["w2t"][buf][:, 0:IC * 128].rearrange("p (kc m) -> p kc m", m=128)
                return p.step("gpsimd", [("pe", st["w2_pe"][buf])],
                              lambda e: e.dma_start(out=dd, in_=src), inc=16)
            dma_w2 = {0: dma_hc(0), 1: dma_hc(1)}
            yi = st["yi"]
            for hc in range(FC):
                buf = hc % 2
                for ts in range(2):
                    yb = 5 + (yi % 2)
                    waits = [("pool", dma_w2[hc]), ("dve", st["y_bank_evac"][yb])]
                    if hc == 0 and ts == 0:
                        waits.append(("dve", v_h_done))
                    items = []
                    for kc in range(IC):
                        items.append((lambda e, kc=kc, hc=hc, buf=buf, ts=ts, yb=yb: e.matmul(
                            p.bank(yb)[:, 0:512],
                            ms["w2t"][buf][:, 0:IC * 128].rearrange("p (kc m) -> p kc m", m=128)[:, kc:kc + 1, :].squeeze(),
                            h[:, 1024 * kc + 512 * ts:1024 * kc + 512 * (ts + 1)],
                            start=(kc == 0), stop=(kc == IC - 1)),
                            1 if kc == IC - 1 else 0))
                    v_y = p.multi("tensor", waits, items)
                    st["y_bank_evac"][yb] = p.multi("vector", [("pe", v_y)], [
                        (lambda e, hc=hc, ts=ts, yb=yb, bcb=bcb: e.tensor_tensor(ln["scr"][:, 0:512], p.bank(yb)[:, 0:512],
                                                                        bcb[:, 512 * ts:512 * (ts + 1)], ALU.mult), 0),
                        (lambda e, hc=hc, ts=ts: e.tensor_tensor(X[:, T * hc + t0 + 512 * ts:T * hc + t0 + 512 * (ts + 1)],
                                                                 X[:, T * hc + t0 + 512 * ts:T * hc + t0 + 512 * (ts + 1)],
                                                                 ln["scr"][:, 0:512], ALU.add), 1),
                    ])
                    yi += 1
                st["w2_pe"][buf] = p.v("pe")
                if hc + 2 < FC:
                    dma_w2[hc + 2] = dma_hc(hc + 2)
            st["yi"] = yi
            st["dve_after_y"] = p.v("dve")
    return



SEG = 256
NCH = SEG // 32
GLA_STAGE = 4
GLA_SUB = {'A', 'O', 'S'}


def alloc_gla(p, cm):
    gs = {}
    for n in ["bA", "bB", "bC", "bD", "bE", "bF", "v32"]:
        gs[n] = p.sb("gla_" + n, [128, SEG * FC], F32)
    for n in ["qt", "ub", "G", "q32"]:
        gs[n] = p.sb("gla_" + n, [128, SEG * FC], BF16)
    gs["Xn"] = p.sb("gla_xn", [128, SEG * FC], BF16)
    gs["h"] = p.sb("gla_h", [128, SEG * FC], BF16)
    gs["vtm"] = p.sb("gla_vtm", [64, 2 * 1024], BF16)
    gs["ktm"] = p.sb("gla_ktm", [64, 2 * 1024], BF16)
    gs["Asb"] = p.sb("gla_asb", [64, 512], BF16)
    gs["hstg"] = p.sb("gla_hstg", [128, 256], F32)
    gs["i64"] = p.sb("gla_i64", [128, 64], F32)
    gs["S"] = p.sb("gla_S", [128, 512], F32)
    gs["Sbf"] = p.sb("gla_Sbf", [128, 512], BF16)
    gs["D32"] = p.sb("gla_D32", [128, 64], F32)
    gs["m16"] = p.sb("gla_m16", [128, SEG], BF16)
    gs["m32"] = p.sb("gla_m32", [128, SEG], BF16)
    gs["amask"] = p.sb("gla_amask", [32, 512], F32)
    gs["kb"] = p.sb("gla_kb", [128, 8], F32)
    gs["ob"] = p.sb("gla_ob", [128, 8], F32)
    gs["lneps"] = p.sb("gla_lneps", [128, 1], F32)
    gs["wbuf"] = [p.sb("gla_wb0", [128, 8 * 512], BF16), p.sb("gla_wb1", [128, 8 * 512], BF16)]
    return gs


def emit_gla(p, cm, ln, gs, W):
    X = cm["X"]
    p.multi("sync", [], [
        (lambda e: e.dma_start(out=gs["amask"][:], in_=W["c_amask"]), 16),
        (lambda e: e.dma_start(out=gs["kb"][:], in_=W["key_b_r"]), 16),
        (lambda e: e.dma_start(out=gs["ob"][:], in_=W["out_b_r"]), 16),
        (lambda e: e.dma_start(out=gs["i64"][:], in_=W["c_ident64"]), 16),
    ])
    p.multi("gpsimd", [], [
        (lambda e: e.dma_start(out=gs["m16"][:], in_=W["c_mask16"][:, 0:SEG]), 16),
        (lambda e: e.dma_start(out=gs["m32"][:], in_=W["c_mask32"][:, 0:SEG]), 16),
    ])
    init_sy = p.v("sy")
    init_pool = p.v("pool")
    p.multi("vector", [("dve", p.v("dve")), ("pe", p.v("pe"))], [
        (lambda e: e.memset(gs["S"][:], 0.0), 0),
        (lambda e: e.memset(gs["Sbf"][:], 0.0), 0),
        (lambda e: e.memset(gs["lneps"][:], 1e-35), 0),
        (lambda e: e.memset(p.psum[0:32, 0:512], 0.0), 0),
        (lambda e: e.memset(gs["vtm"][32:64, :], 0.0), 0),
        (lambda e: e.memset(gs["ktm"][32:64, :], 0.0), 0),
        (lambda e: e.memset(gs["Asb"][32:64, :], 0.0), 1),
    ])
    init_dve = p.v("dve")
    st = {"wb_pe": [0, 0], "A_evac": 0, "O_evac": 0, "S_evac": 0, "Sbf_cast": 0,
          "tr_evac": {4: 0, 5: 0, 6: 0, 7: 0}, "proj_evac": [0, 0], "proj_act": [0, 0]}

    def dma_w(src_ap, mh, buf, wait_pe):
        src = src_ap[:, 512 * mh:512 * (mh + 1)].rearrange("(kc p) m -> p kc m", p=128)
        dd = gs["wbuf"][buf][:].rearrange("p (kc m) -> p kc m", m=512)
        return p.step("gpsimd", [("pe", wait_pe)], lambda e: e.dma_start(out=dd, in_=src), inc=16)

    for seg in range(FC):
        t0 = SEG * seg
        emit_ln_apply(p, cm, ln, t0, SEG, gs["Xn"])
        v_xn = p.v("dve")

        d0 = dma_w(W["key_wT"], 0, 0, st["wb_pe"][0])
        d1 = dma_w(W["key_wT"], 1, 1, st["wb_pe"][1])
        dmav = {0: d0, 1: d1}

        for mc in range(FC):
            buf = mc // 4
            bk = mc % 2
            waits = [("pool", dmav[buf]), ("act", st["proj_act"][bk]), ("dve", v_xn), ("sy", init_sy),
                     ("pool", init_pool)]
            items = []
            for kc in range(FC):
                items.append((lambda e, kc=kc, buf=buf, mc=mc, bk=bk: e.matmul(
                    p.bank(bk)[:, 0:SEG],
                    gs["wbuf"][buf][:].rearrange("p (kc m) -> p kc m", m=512)[:, kc:kc + 1, 128 * (mc % 4):128 * (mc % 4 + 1)].squeeze(),
                    gs["Xn"][:, SEG * kc:SEG * (kc + 1)],
                    start=(kc == 0), stop=(kc == FC - 1)), 1 if kc == FC - 1 else 0))
            vmm = p.multi("tensor", waits, items)
            st["proj_act"][bk] = p.step("scalar", [("pe", vmm)],
                                        lambda e, mc=mc, bk=bk: e.activation(gs["bA"][:, SEG * mc:SEG * (mc + 1)],
                                                                             p.bank(bk)[:, 0:SEG], AF.Sigmoid,
                                                                             bias=gs["kb"][:, mc:mc + 1], scale=1.0),
                                        inc=1)
        for half4 in range(4):
            buf = half4 % 2
            wait_pe_users = p.v("pe")
            dmav[buf] = dma_w(W["vq_wT"], half4, buf, wait_pe_users)
            for sub in range(4):
                mc = half4 * 4 + sub
                bk = mc % 2
                dst = gs["bE"] if mc < 8 else gs["bF"]
                mco = mc % 8
                waits = [("pool", dmav[buf]), ("act", st["proj_act"][bk])]
                items = []
                for kc in range(FC):
                    items.append((lambda e, kc=kc, buf=buf, sub=sub, bk=bk: e.matmul(
                        p.bank(bk)[:, 0:SEG],
                        gs["wbuf"][buf][:].rearrange("p (kc m) -> p kc m", m=512)[:, kc:kc + 1, 128 * sub:128 * (sub + 1)].squeeze(),
                        gs["Xn"][:, SEG * kc:SEG * (kc + 1)],
                        start=(kc == 0), stop=(kc == FC - 1)), 1 if kc == FC - 1 else 0))
                vmm = p.multi("tensor", waits, items)
                st["proj_act"][bk] = p.step("scalar", [("pe", vmm)],
                                            lambda e, dst=dst, mco=mco, bk=bk: e.activation(
                                                dst[:, SEG * mco:SEG * (mco + 1)], p.bank(bk)[:, 0:SEG], AF.Copy),
                                            inc=1)
        v_proj = p.v("act")
        if GLA_STAGE < 1:
            continue

        k_, u_, es, cl32 = gs["bA"], gs["bB"], gs["bC"], gs["bD"]
        vv, q_ = gs["bE"], gs["bF"]
        dD, k32 = gs["bF"], gs["v32"]
        v1 = p.step("vector", [("act", v_proj)],
                    lambda e: e.tensor_scalar(u_[:], k_[:], -1.0, 1.0, ALU.mult, ALU.add), inc=1)
        v2 = p.step("scalar", [("dve", v1)],
                    lambda e: e.activation(es[:], k_[:], AF.Ln, bias=gs["lneps"][:], scale=1.0), inc=1)
        fns = []
        for fc in range(FC):
            fns.append(lambda e, fc=fc: e.tensor_tensor_scan(
                k_[:, SEG * fc:SEG * (fc + 1)],
                gs["m16"][:], es[:, SEG * fc:SEG * (fc + 1)], 0.0, ALU.mult, ALU.add))
        for fc in range(FC):
            fns.append(lambda e, fc=fc: e.tensor_tensor_scan(
                cl32[:, SEG * fc:SEG * (fc + 1)],
                gs["m32"][:], es[:, SEG * fc:SEG * (fc + 1)], 0.0, ALU.mult, ALU.add))
        v3 = p.chain("vector", [("act", v2)], fns)
        cl16 = k_
        v4 = p.step("scalar", [("dve", v3)], lambda e: e.activation(es[:], cl16[:], AF.Exp), inc=1)
        v5 = p.step("vector", [("act", v4)], lambda e: e.tensor_tensor(gs["qt"][:], q_[:], es[:], ALU.mult), inc=1)
        v6 = p.step("scalar", [("dve", v5)], lambda e: e.activation(es[:], cl32[:], AF.Exp), inc=1)
        v7 = p.step("vector", [("act", v6)], lambda e: e.tensor_tensor(gs["q32"][:], q_[:], es[:], ALU.mult), inc=1)
        cl32e = cl32[:].rearrange("p (a s) -> p a s", s=32)[:, :, 31]
        v8 = p.step("scalar", [("dve", v3)], lambda e: e.activation(gs["D32"][:], cl32e, AF.Exp), inc=1)
        v9 = p.step("scalar", [("dve", v7)], lambda e: e.activation(es[:], cl16[:], AF.Exp, scale=-1.0), inc=1)
        v10 = p.chain("vector", [("act", v9)], [
            lambda e: e.tensor_tensor(gs["ub"][:], u_[:], es[:], ALU.mult),
            lambda e: e.tensor_copy(gs["G"][:].rearrange("p (a s) -> p a s", s=32)[:, :, 16:32],
                                    gs["ub"][:].rearrange("p (a s) -> p a s", s=32)[:, :, 16:32]),
        ])
        cl16_blk = cl16[:].rearrange("p (a s) -> p a s", s=16)
        bcast16 = cl16_blk[:, :, 15].unsqueeze(2).broadcast_to((128, SEG * FC // 16, 16))
        v11 = p.step("vector", [("dve", v10)],
                     lambda e: e.tensor_tensor(dD[:].rearrange("p (a s) -> p a s", s=16), bcast16, cl16_blk,
                                               ALU.subtract), inc=1)
        v12 = p.step("scalar", [("dve", v11)], lambda e: e.activation(es[:], dD[:], AF.Exp), inc=1)
        v13 = p.step("vector", [("act", v12)],
                     lambda e: e.tensor_tensor(gs["G"][:].rearrange("p (a s) -> p a s", s=32)[:, :, 0:16],
                                               u_[:].rearrange("p (a s) -> p a s", s=32)[:, :, 0:16],
                                               es[:].rearrange("p (a s) -> p a s", s=32)[:, :, 0:16],
                                               ALU.mult), inc=1)
        cl32_blk = cl32[:].rearrange("p (a s) -> p a s", s=32)
        bcast32 = cl32_blk[:, :, 31].unsqueeze(2).broadcast_to((128, SEG * FC // 32, 32))
        v14 = p.step("vector", [("dve", v13)],
                     lambda e: e.tensor_tensor(dD[:].rearrange("p (a s) -> p a s", s=32), bcast32, cl32_blk,
                                               ALU.subtract), inc=1)
        v15 = p.step("scalar", [("dve", v14)], lambda e: e.activation(es[:], dD[:], AF.Exp), inc=1)
        v16 = p.step("vector", [("act", v15)], lambda e: e.tensor_tensor(k32[:], u_[:], es[:], ALU.mult), inc=1)
        v_prep = p.v("dve")
        if GLA_STAGE < 2:
            continue

        for c in range(NCH):
            sl = c % 2
            so = 1024 * sl
            for bgrp, r0 in ((0, 0), (1, 64)):
                items = []
                for (tsrc, pb) in ((vv, 4 + bgrp), (k32, 6 + bgrp)):
                    for fc in range(FC):
                        items.append((lambda e, fc=fc, c=c, tsrc=tsrc, pb=pb, r0=r0: e.transpose(
                            p.psum[0:32, 512 * pb + 64 * fc:512 * pb + 64 * (fc + 1)],
                            tsrc[r0:r0 + 64, SEG * fc + 32 * c:SEG * fc + 32 * (c + 1)],
                            gs["i64"][r0:r0 + 64, 0:64],
                            tile_position=(r0, 0)),
                            1 if (tsrc is k32 and fc == FC - 1) else 0))
                w = [("dve", st["tr_evac"][4 + bgrp]), ("pe", p.v("pe"))]
                if c == 0:
                    w.append(("dve", v_prep))
                vt = p.multi("tensor", w, items)
                ev_items = []
                for (packs, pb) in (("vtm", 4 + bgrp), ("ktm", 6 + bgrp)):
                    ev_items.append((lambda e, so=so, pb=pb, packs=packs, bgrp=bgrp:
                                     e.tensor_copy(gs[packs][0:32, so + 512 * bgrp:so + 512 * (bgrp + 1)],
                                                   p.psum[0:32, 512 * pb:512 * (pb + 1)]), 1))
                st["tr_evac"][4 + bgrp] = p.multi("vector", [("pe", vt), ("pe", st.setdefault("slot_pe", [0, 0])[sl])],
                                                  ev_items)
            v_packs = p.v("dve")
            vtm = gs["vtm"]
            ktm = gs["ktm"]
            if "A" not in GLA_SUB:
                continue
            vA = 0
            for hw in range(2):
                par = 64 * hw
                co = 256 * hw
                items = []
                for hh in range(8):
                    fc = hh
                    lo = SEG * fc + 32 * c
                    items.append((lambda e, hh=hh, par=par, lo=lo, co=co: e.matmul(
                        p.psum[0:16, co + 32 * hh:co + 32 * hh + 16],
                        gs["ub"][par:par + 64, lo:lo + 16],
                        gs["qt"][par:par + 64, lo:lo + 16], start=True, stop=True,
                        tile_position=(par, 0)), 0))
                    items.append((lambda e, hh=hh, par=par, lo=lo, co=co: e.matmul(
                        p.psum[0:32, co + 32 * hh + 16:co + 32 * hh + 32],
                        gs["G"][par:par + 64, lo:lo + 32],
                        gs["qt"][par:par + 64, lo + 16:lo + 32], start=True, stop=True,
                        tile_position=(par, 0)),
                        1 if hh == 7 else 0))
                w = [("pe", p.v("pe"))]
                if hw == 0:
                    w += [("dve", st["A_evac"] if st["A_evac"] else init_dve), ("dve", v_packs)]
                vA = p.multi("tensor", w, items)
            st["A_evac"] = p.step("vector", [("pe", vA)],
                                  lambda e: e.tensor_tensor(gs["Asb"][0:32, :], p.psum[0:32, 0:512],
                                                            gs["amask"][:], ALU.mult), inc=1)
            if "O" not in GLA_SUB:
                continue
            items = []
            for hh in range(8):
                lo = SEG * hh + 32 * c
                items.append((lambda e, hh=hh, so=so: e.matmul(
                    p.bank(2)[0:64, 32 * hh:32 * (hh + 1)],
                    vtm[0:64, so + 64 * hh:so + 64 * (hh + 1)],
                    gs["Asb"][0:64, 32 * hh:32 * (hh + 1)], start=True, stop=False,
                    tile_position=(0, 0)), 0))
                items.append((lambda e, hh=hh, lo=lo: e.matmul(
                    p.bank(2)[0:64, 32 * hh:32 * (hh + 1)],
                    gs["Sbf"][0:64, 64 * hh:64 * (hh + 1)],
                    gs["q32"][0:64, lo:lo + 32], start=False, stop=True,
                    tile_position=(0, 0)),
                    0))
            for hh in range(8):
                items.append((lambda e, hh=hh, so=so: e.matmul(
                    p.bank(2)[64:128, 32 * hh:32 * (hh + 1)],
                    vtm[0:64, so + 512 + 64 * hh:so + 512 + 64 * (hh + 1)],
                    gs["Asb"][0:64, 256 + 32 * hh:256 + 32 * (hh + 1)], start=True, stop=True,
                    tile_position=(0, 64)), 1))
            w = [("dve", st["A_evac"]), ("dve", st["O_evac"]), ("dve", st["Sbf_cast"]), ("pe", p.v("pe"))]
            vO1 = p.multi("tensor", w, items)
            items = []
            for hh in range(8):
                lo = SEG * hh + 32 * c
                items.append((lambda e, hh=hh, lo=lo: e.matmul(
                    p.bank(2)[64:128, 256 + 32 * hh:256 + 32 * (hh + 1)],
                    gs["Sbf"][64:128, 64 * hh:64 * (hh + 1)],
                    gs["q32"][64:128, lo:lo + 32], start=True, stop=True,
                    tile_position=(64, 64)),
                    1 if hh == 7 else 0))
            vO = p.multi("tensor", [("pe", vO1)], items)
            st["O_evac"] = p.chain("vector", [("pe", vO)], [
                lambda e, c=c: e.tensor_copy(
                    gs["h"][0:64, :].rearrange("p (a s) -> p a s", s=SEG)[:, :, 32 * c:32 * (c + 1)],
                    p.bank(2)[0:64, 0:256].rearrange("p (a s) -> p a s", s=32)),
                lambda e: e.tensor_copy(gs["hstg"][64:128, :], p.bank(2)[64:128, 0:256]),
                lambda e, c=c: e.tensor_tensor(
                    gs["h"][64:128, :].rearrange("p (a s) -> p a s", s=SEG)[:, :, 32 * c:32 * (c + 1)],
                    gs["hstg"][64:128, :].rearrange("p (a s) -> p a s", s=32),
                    p.bank(2)[64:128, 256:512].rearrange("p (a s) -> p a s", s=32), ALU.add),
            ])
            if "S" not in GLA_SUB:
                continue
            items = []
            for hw in range(2):
                for hh in range(8):
                    items.append((lambda e, hh=hh, hw=hw, so=so: e.matmul(
                        p.bank(3)[64 * hw:64 * hw + 64, 64 * hh:64 * (hh + 1)],
                        ktm[0:64, so + 512 * hw + 64 * hh:so + 512 * hw + 64 * (hh + 1)],
                        vtm[0:64, so + 512 * hw + 64 * hh:so + 512 * hw + 64 * (hh + 1)],
                        start=True, stop=True, tile_position=(0, 64 * hw)),
                        1 if (hw == 1 and hh == 7) else 0))
            vS = p.multi("tensor", [("dve", st["S_evac"]), ("pe", p.v("pe"))], items)
            items = []
            for hw in range(2):
                par = 64 * hw
                for hh in range(8):
                    items.append((lambda e, hh=hh, par=par, c=c: e.scalar_tensor_tensor(
                        gs["S"][par:par + 64, 64 * hh:64 * (hh + 1)], gs["S"][par:par + 64, 64 * hh:64 * (hh + 1)],
                        gs["D32"][par:par + 64, 8 * hh + c:8 * hh + c + 1],
                        p.bank(3)[par:par + 64, 64 * hh:64 * (hh + 1)],
                        ALU.mult, ALU.add), 1))
            vstt = p.multi("vector", [("pe", vS)], items)
            st["S_evac"] = p.step("vector", [("dve", vstt)],
                                  lambda e: e.tensor_copy(gs["Sbf"][:], gs["S"][:]), inc=1)
            st["Sbf_cast"] = st["S_evac"]
            st["slot_pe"][sl] = p.v("pe")

        v_h = p.v("dve")
        if GLA_STAGE < 3:
            continue
        d0 = dma_w(W["out_wT"], 0, 0, p.v("pe"))
        d1 = dma_w(W["out_wT"], 1, 1, p.v("pe"))
        dmav = {0: d0, 1: d1}
        for mc in range(FC):
            buf = mc // 4
            bk = mc % 2
            waits = [("pool", dmav[buf]), ("dve", st["proj_evac"][bk]), ("act", st["proj_act"][bk])]
            if mc == 0:
                waits.append(("dve", v_h))
            items = []
            for kc in range(FC):
                items.append((lambda e, kc=kc, buf=buf, mc=mc, bk=bk: e.matmul(
                    p.bank(bk)[:, 0:SEG],
                    gs["wbuf"][buf][:].rearrange("p (kc m) -> p kc m", m=512)[:, kc:kc + 1, 128 * (mc % 4):128 * (mc % 4 + 1)].squeeze(),
                    gs["h"][:, SEG * kc:SEG * (kc + 1)],
                    start=(kc == 0), stop=(kc == FC - 1)), 1 if kc == FC - 1 else 0))
            vmm = p.multi("tensor", waits, items)
            st["proj_evac"][bk] = p.step("vector", [("pe", vmm)],
                                         lambda e, mc=mc, bk=bk, t0=t0: e.scalar_tensor_tensor(
                                             X[:, T * mc + t0:T * mc + t0 + SEG],
                                             p.bank(bk)[:, 0:SEG], gs["ob"][:, mc:mc + 1],
                                             X[:, T * mc + t0:T * mc + t0 + SEG], ALU.add, ALU.add), inc=1)
            st["proj_act"][bk] = st["proj_act"][bk]
        st["wb_pe"] = [p.v("pe"), p.v("pe")]
    return



SEG1 = 512
LAMBDA_INIT = 0.8 - 0.6 * float(np.exp(-0.3 * 12))


def alloc_attn1(p, cm):
    at = {}
    at["Xn"] = p.sb("at_xn", [128, FC * SEG1], BF16)
    at["q"] = p.sb("at_q", [128, 4 * D], F32)
    at["k"] = p.sb("at_k", [128, 4 * D], F32)
    at["v"] = p.sb("at_v", [128, 4 * D], BF16)
    at["attn"] = p.sb("at_attn", [128, 4 * D], F32)
    at["af"] = p.sb("at_af", [128, FC * SEG1], BF16)
    at["scr"] = p.sb("at_scr", [128, D], F32)
    at["scr2"] = p.sb("at_scr2", [128, D], F32)
    at["scr3"] = p.sb("at_scr3", [128, D], F32)
    at["sc"] = p.sb("at_sc", [128, 512], F32)
    at["mx"] = p.sb("at_mx", [128, 64], F32)
    at["rstd"] = p.sb("at_rstd", [128, 32], F32)
    at["wm"] = p.sb("at_wm", [128, 256], F32)
    at["cs_cos"] = p.sb("at_cos", [128, 4 * 64], F32)
    at["cs_sin"] = p.sb("at_sin", [128, 4 * 64], F32)
    at["lnw"] = p.sb("at_lnw", [128, 128], F32)
    at["lnb"] = p.sb("at_lnb", [128, 128], F32)
    at["lam"] = p.sb("at_lam", [1, 256], F32)
    at["lams"] = p.sb("at_lams", [1, 128], F32)
    at["neglam"] = p.sb("at_neglam", [128, 1], F32)
    at["eps8"] = p.sb("at_eps8", [128, 1], F32)
    at["wbuf"] = [p.sb("at_wb0", [128, 8 * 512], BF16), p.sb("at_wb1", [128, 8 * 512], BF16)]
    return at


def emit_attn1(p, cm, ln, at, W):
    X = cm["X"]
    HDQ = float(64.0 ** -0.25)
    p.multi("sync", [], [
        (lambda e: e.dma_start(out=at["lam"][0:1, 0:64], in_=W["lambda_q1"].unsqueeze(0)), 16),
        (lambda e: e.dma_start(out=at["lam"][0:1, 64:128], in_=W["lambda_k1"].unsqueeze(0)), 16),
        (lambda e: e.dma_start(out=at["lam"][0:1, 128:192], in_=W["lambda_q2"].unsqueeze(0)), 16),
        (lambda e: e.dma_start(out=at["lam"][0:1, 192:256], in_=W["lambda_k2"].unsqueeze(0)), 16),
        (lambda e: e.dma_start(out=at["lnw"][:], in_=W["ln_w_rep"]), 16),
        (lambda e: e.dma_start(out=at["lnb"][:], in_=W["ln_b_rep"]), 16),
    ])
    v_ld = p.v("sy")
    vx = p.chain("vector", [("sy", v_ld), ("pe", p.v("pe")), ("dve", p.v("dve"))], [
        lambda e: e.memset(at["eps8"][:], 1e-8),
        lambda e: e.tensor_tensor(at["lams"][0:1, 0:64], at["lam"][0:1, 0:64], at["lam"][0:1, 64:128], ALU.mult),
        lambda e: e.tensor_tensor(at["lams"][0:1, 64:128], at["lam"][0:1, 128:192], at["lam"][0:1, 192:256], ALU.mult),
        lambda e: e.tensor_reduce(at["lams"][0:1, 0:2], at["lams"][0:1, 0:128].rearrange("p (a d) -> p a d", d=64),
                                  AX.X, ALU.add),
    ])
    va = p.step("scalar", [("dve", vx)], lambda e: e.activation(at["lams"][0:1, 4:6], at["lams"][0:1, 0:2], AF.Exp),
                inc=1)
    vx = p.chain("vector", [("act", va)], [
        lambda e: e.tensor_tensor(at["lams"][0:1, 8:9], at["lams"][0:1, 4:5], at["lams"][0:1, 5:6], ALU.subtract),
        lambda e: e.tensor_scalar(at["lams"][0:1, 8:9], at["lams"][0:1, 8:9], LAMBDA_INIT, -1.0,
                                  ALU.add, ALU.mult),
        lambda e: e.tensor_scalar(at["lnw"][:], at["lnw"][:], 1.0 - LAMBDA_INIT, None, ALU.mult, ALU.bypass),
        lambda e: e.tensor_scalar(at["lnb"][:], at["lnb"][:], 1.0 - LAMBDA_INIT, None, ALU.mult, ALU.bypass),
    ])
    vneg = p.step("tensor", [("dve", vx)],
                  lambda e: e.matmul(p.bank(2)[:, 0:1], cm["ones_row"][:], at["lams"][0:1, 8:9],
                                     start=True, stop=True), inc=1)
    v_lam = p.step("vector", [("pe", vneg)], lambda e: e.tensor_copy(at["neglam"][:], p.bank(2)[:, 0:1]), inc=1)

    st = {"bk_act": [0, 0], "bk_dve": [0, 0], "wb_pe": [0, 0], "tr_dve": {6: 0, 7: 0}}

    def dma_w(src_ap, mh, buf):
        src = src_ap[:, 512 * mh:512 * (mh + 1)].rearrange("(kc p) m -> p kc m", p=128)
        dd = at["wbuf"][buf][:].rearrange("p (kc m) -> p kc m", m=512)
        return p.step("gpsimd", [("pe", st["wb_pe"][buf])], lambda e: e.dma_start(out=dd, in_=src), inc=16)

    for seg in range(4):
        t0 = SEG1 * seg
        emit_ln_apply(p, cm, ln, t0, SEG1, at["Xn"])
        v_xn = p.v("dve")
        v_cs = p.multi("gpsimd", [("dve", st["tr_dve"][6])], [
            (lambda e: e.dma_start(out=at["cs_cos"][:].rearrange("p (a d) -> p a d", d=64),
                                   in_=W["c_cos"][t0:t0 + SEG1, :].rearrange("(a p) d -> p a d", p=128)), 16),
            (lambda e: e.dma_start(out=at["cs_sin"][:].rearrange("p (a d) -> p a d", d=64),
                                   in_=W["c_sin"][t0:t0 + SEG1, :].rearrange("(a p) d -> p a d", p=128)), 16),
        ])
        for mi, (wd, dst, dt_) in enumerate(((W["Wq"], at["q"], F32), (W["Wk"], at["k"], F32),
                                             (W["Wv"], at["v"], BF16))):
            d0 = dma_w(wd, 0, 0)
            d1 = dma_w(wd, 1, 1)
            dmav = {0: d0, 1: d1}
            for tt in range(4):
                for win in range(2):
                    bk = (tt * 2 + win) % 2
                    waits = [("pool", dmav[win]), ("dve", st["bk_dve"][bk]), ("act", st["bk_act"][bk])]
                    if mi == 0 and tt == 0 and win == 0:
                        waits.append(("dve", v_xn))
                    items = []
                    for kc in range(FC):
                        items.append((lambda e, kc=kc, tt=tt, win=win, bk=bk: e.matmul(
                            p.bank(bk)[:, 0:512],
                            at["Xn"][:, SEG1 * kc + 128 * tt:SEG1 * kc + 128 * (tt + 1)],
                            at["wbuf"][win][:].rearrange("p (kc m) -> p kc m", m=512)[:, kc:kc + 1, :].squeeze(),
                            start=(kc == 0), stop=(kc == FC - 1)), 1 if kc == FC - 1 else 0))
                    vmm = p.multi("tensor", waits, items)
                    st["bk_dve"][bk] = p.step("vector", [("pe", vmm)],
                                              lambda e, tt=tt, win=win, bk=bk, dst=dst: e.tensor_copy(
                                                  dst[:, D * tt + 512 * win:D * tt + 512 * (win + 1)],
                                                  p.bank(bk)[:, 0:512]), inc=1)
            st["wb_pe"] = [p.v("pe"), p.v("pe")]
        v_qkv = p.v("dve")

        def q_t(tt): return at["q"][:, D * tt:D * (tt + 1)]
        def k_t(tt): return at["k"][:, D * tt:D * (tt + 1)]
        def v_t(tt): return at["v"][:, D * tt:D * (tt + 1)]
        def a_t(tt): return at["attn"][:, D * tt:D * (tt + 1)]
        def sc_t(tt): return at["sc"][:, 128 * tt:128 * (tt + 1)]
        def mx_t(tt): return at["mx"][:, 16 * tt:16 * (tt + 1)]
        def wm_t(tt): return at["wm"][:, 64 * tt:64 * (tt + 1)]
        def rs_t(tt): return at["rstd"][:, 8 * tt:8 * (tt + 1)]
        for tt in range(4):
            cosAP = at["cs_cos"][:, 64 * tt:64 * (tt + 1)].unsqueeze(1).broadcast_to((128, 16, 64))
            sinAP = at["cs_sin"][:, 64 * tt:64 * (tt + 1)].unsqueeze(1).broadcast_to((128, 16, 64))
            for rt in (q_t(tt), k_t(tt)):
                p.chain("vector", [("dve", p.v("dve")), ("sy", p.v("sy")), ("pool", p.v("pool"))], [
                    lambda e, rt=rt: e.tensor_scalar(at["scr"][:].rearrange("p (a two) -> p a two", two=2)[:, :, 0:1],
                                                     rt.rearrange("p (a two) -> p a two", two=2)[:, :, 1:2],
                                                     -1.0, None, ALU.mult, ALU.bypass),
                    lambda e, rt=rt: e.tensor_copy(at["scr"][:].rearrange("p (a two) -> p a two", two=2)[:, :, 1:2],
                                                   rt.rearrange("p (a two) -> p a two", two=2)[:, :, 0:1]),
                    lambda e, rt=rt, cosAP=cosAP: e.tensor_tensor(at["scr2"][:].rearrange("p (hh d) -> p hh d", d=64),
                                                                  rt.rearrange("p (hh d) -> p hh d", d=64), cosAP, ALU.mult),
                    lambda e, sinAP=sinAP: e.tensor_tensor(at["scr"][:].rearrange("p (hh d) -> p hh d", d=64),
                                                           at["scr"][:].rearrange("p (hh d) -> p hh d", d=64), sinAP, ALU.mult),
                    lambda e, rt=rt: e.tensor_tensor(rt, at["scr2"][:], at["scr"][:], ALU.add),
                ])
        for tt in range(4):
            items = []
            for g in range(2):
                for n in range(8):
                    for m in range(8):
                        col = 64 * g + 8 * n + m
                        qs = q_t(tt)[:, 64 * (2 * n + g):64 * (2 * n + g + 1)]
                        ks = k_t(tt)[:, 64 * (2 * m + g):64 * (2 * m + g + 1)]
                        items.append((lambda e, qs=qs, ks=ks, col=col, tt=tt: e.scalar_tensor_tensor(
                            at["scr"][:, 0:64], qs, HDQ, ks, ALU.mult, ALU.mult,
                            accum_out=sc_t(tt)[:, col:col + 1]), 1))
            p.multi("vector", [], items)
            scv = sc_t(tt).rearrange("p (gn m) -> p gn m", m=8)
            p.chain("vector", [], [
                lambda e, tt=tt, scv=scv: e.tensor_reduce(mx_t(tt), scv, AX.X, ALU.max),
                lambda e, tt=tt: e.tensor_scalar(mx_t(tt), mx_t(tt), -1.0, None, ALU.mult, ALU.bypass),
                lambda e, tt=tt, scv=scv: e.tensor_tensor(scv, scv, mx_t(tt).unsqueeze(2).broadcast_to((128, 16, 8)),
                                                          ALU.add),
            ])
        v_pre = p.v("dve")
        vexp = 0
        for tt in range(4):
            vexp = p.step("scalar", [("dve", v_pre)] if tt == 0 else [],
                          lambda e, tt=tt: e.activation(sc_t(tt), sc_t(tt), AF.Exp), inc=1)
        for tt in range(4):
            scv = sc_t(tt).rearrange("p (gn m) -> p gn m", m=8)
            p.chain("vector", [("act", vexp)] if tt == 0 else [], [
                lambda e, tt=tt, scv=scv: e.tensor_reduce(mx_t(tt), scv, AX.X, ALU.add),
                lambda e, tt=tt: e.reciprocal(mx_t(tt), mx_t(tt)),
                lambda e, tt=tt, scv=scv: e.tensor_tensor(scv, scv, mx_t(tt).unsqueeze(2).broadcast_to((128, 16, 8)),
                                                          ALU.mult),
                lambda e, tt=tt: e.scalar_tensor_tensor(wm_t(tt), sc_t(tt)[:, 64:128], at["neglam"][:, 0:1],
                                                        sc_t(tt)[:, 0:64], ALU.mult, ALU.add),
            ])
            for n in range(8):
                fns = [(lambda e, n=n, tt=tt: e.tensor_scalar(a_t(tt)[:, 128 * n:128 * (n + 1)],
                                                              v_t(tt)[:, 0:128], wm_t(tt)[:, 8 * n:8 * n + 1],
                                                              None, ALU.mult, ALU.bypass))]
                for m in range(1, 8):
                    fns.append(lambda e, n=n, m=m, tt=tt: e.scalar_tensor_tensor(
                        a_t(tt)[:, 128 * n:128 * (n + 1)], v_t(tt)[:, 128 * m:128 * (m + 1)],
                        wm_t(tt)[:, 8 * n + m:8 * n + m + 1], a_t(tt)[:, 128 * n:128 * (n + 1)],
                        ALU.mult, ALU.add))
                p.chain("vector", [], fns)
        v_comb = p.v("dve")
        vsq = 0
        for tt in range(4):
            vsq = p.step("scalar", [("dve", v_comb)] if tt == 0 else [],
                         lambda e, tt=tt: e.activation(at["af"][:, 1024 * tt:1024 * (tt + 1)], a_t(tt), AF.Square),
                         inc=1)
        for tt in range(4):
            p.chain("vector", [("act", vsq)] if tt == 0 else [], [
                lambda e, tt=tt: e.tensor_reduce(rs_t(tt),
                                                 at["af"][:, 1024 * tt:1024 * (tt + 1)].rearrange("p (n e2) -> p n e2", e2=128),
                                                 AX.X, ALU.add),
                lambda e, tt=tt: e.tensor_scalar(rs_t(tt), rs_t(tt), 1.0 / 128, None, ALU.mult, ALU.bypass),
            ])
        v_red = p.v("dve")
        vsq2 = 0
        for tt in range(4):
            vsq2 = p.step("scalar", [("dve", v_red)] if tt == 0 else [],
                          lambda e, tt=tt: e.activation(rs_t(tt), rs_t(tt), AF.Sqrt, bias=at["eps8"][:], scale=1.0),
                          inc=1)
        for tt in range(4):
            p.chain("vector", [("act", vsq2)] if tt == 0 else [], [
                lambda e, tt=tt: e.reciprocal(rs_t(tt), rs_t(tt)),
                lambda e, tt=tt: e.tensor_tensor(a_t(tt).rearrange("p (n e2) -> p n e2", e2=128),
                                                 a_t(tt).rearrange("p (n e2) -> p n e2", e2=128),
                                                 rs_t(tt).unsqueeze(2).broadcast_to((128, 8, 128)), ALU.mult),
                lambda e, tt=tt: e.tensor_tensor(a_t(tt).rearrange("p (n e2) -> p n e2", e2=128),
                                                 a_t(tt).rearrange("p (n e2) -> p n e2", e2=128),
                                                 at["lnw"][:].unsqueeze(1).broadcast_to((128, 8, 128)), ALU.mult),
                lambda e, tt=tt: e.tensor_tensor(a_t(tt).rearrange("p (n e2) -> p n e2", e2=128),
                                                 a_t(tt).rearrange("p (n e2) -> p n e2", e2=128),
                                                 at["lnb"][:].unsqueeze(1).broadcast_to((128, 8, 128)), ALU.add),
            ])
        v_attn = p.v("dve")

        for tt in range(4):
            for fc in range(FC):
                pb = 6 + (fc % 2)
                w = [("dve", v_attn if (tt == 0 and fc == 0) else 0), ("dve", st["tr_dve"][pb])]
                vt = p.step("tensor", w,
                            lambda e, tt=tt, fc=fc, pb=pb: e.transpose(
                                p.bank(pb)[:, 0:128],
                                at["attn"][:, D * tt + 128 * fc:D * tt + 128 * (fc + 1)],
                                cm["ident"][:]), inc=1)
                st["tr_dve"][pb] = p.step("vector", [("pe", vt)],
                                          lambda e, tt=tt, fc=fc, pb=pb: e.tensor_copy(
                                              at["af"][:, SEG1 * fc + 128 * tt:SEG1 * fc + 128 * (tt + 1)],
                                              p.bank(pb)[:, 0:128]), inc=1)
        v_af = p.v("dve")

        d0 = dma_w(W["Wo"], 0, 0)
        d1 = dma_w(W["Wo"], 1, 1)
        dmav = {0: d0, 1: d1}
        for mc in range(FC):
            buf = mc // 4
            bk = mc % 2
            waits = [("pool", dmav[buf]), ("dve", st["bk_dve"][bk])]
            if mc == 0:
                waits.append(("dve", v_af))
            items = []
            for kc in range(FC):
                items.append((lambda e, kc=kc, mc=mc, bk=bk, buf=buf: e.matmul(
                    p.bank(bk)[:, 0:512],
                    at["wbuf"][buf][:].rearrange("p (kc m) -> p kc m", m=512)[:, kc:kc + 1, 128 * (mc % 4):128 * (mc % 4 + 1)].squeeze(),
                    at["af"][:, SEG1 * kc:SEG1 * (kc + 1)],
                    start=(kc == 0), stop=(kc == FC - 1)), 1 if kc == FC - 1 else 0))
            vmm = p.multi("tensor", waits, items)
            st["bk_dve"][bk] = p.step("vector", [("pe", vmm)],
                                      lambda e, mc=mc, bk=bk, t0=t0: e.tensor_tensor(
                                          X[:, T * mc + t0:T * mc + t0 + SEG1],
                                          X[:, T * mc + t0:T * mc + t0 + SEG1],
                                          p.bank(bk)[:, 0:512], ALU.add), inc=1)
        st["wb_pe"] = [p.v("pe"), p.v("pe")]
    return



def build_ffn_program(layer, x_in_kind, x_out_kind="fm"):
    p = Prog(f"ffn{layer}")
    if x_in_kind == "tm":
        x_in = p.din("x_in", [T, D])
    else:
        x_in = p.din("x_in", [D, T])
    x_out = p.dout("x_out", [T, D] if x_out_kind == "tm" else [D, T])
    ident = p.din("c_ident", [128, 128])
    if layer == 0:
        rw = p.din("L0_ffn_w_router", [D, NE])
        w1 = p.din("L0_ffn_w1", [NE, D, 2816])
        w2r = p.din("w2r_l0", [NE, FC, 128, 22, 128])
        w3 = p.din("L0_ffn_w3", [NE, D, 2816])
        experts = [{"w1": w1[e:e + 1].squeeze(), "w3": w3[e:e + 1].squeeze(), "w2r": w2r[e:e + 1].squeeze()}
                   for e in range(NE)]
        cfg = MoECfg(2816, experts, rw, 0.0, 0.5)
        maxI = 2816
    else:
        rw = p.din("L1_ffn_Wr", [D, NE])
        wg = p.din("L1_ffn_Wg", [NE, D, 2048])
        wu = p.din("L1_ffn_Wu", [NE, D, 2048])
        wdr = p.din("w2r_l1", [NE, FC, 128, 16, 128])
        sg = p.din("L1_ffn_Sg", [D, 2048])
        su = p.din("L1_ffn_Su", [D, 2048])
        sdr = p.din("w2r_sd", [FC, 128, 16, 128])
        experts = [{"w1": wg[e:e + 1].squeeze(), "w3": wu[e:e + 1].squeeze(), "w2r": wdr[e:e + 1].squeeze()}
                   for e in range(NE)]
        experts.append({"w1": sg, "w3": su, "w2r": sdr, "shared": True})
        cfg = MoECfg(2048, experts, rw, 1e-20, 0.5)
        maxI = 2048
    p.start()
    cm = alloc_common(p)
    ln = alloc_ln(p, cm)
    ms = alloc_moe(p, cm, maxI)
    emit_init_consts(p, cm, ident)
    if x_in_kind == "tm":
        emit_load_x_tm(p, cm, x_in)
    else:
        emit_load_x_fm(p, cm, x_in)
    emit_ln_stats(p, cm, ln)
    emit_moe(p, cm, ln, ms, cfg)
    if x_out_kind == "tm":
        emit_store_x_tm(p, cm, x_out)
    else:
        emit_store_x_fm(p, cm, x_out)
    p.finalize()
    return p


def build_gla_program(x_in_kind="fm"):
    p = Prog("gla")
    x_in = p.din("x_in", [T, D] if x_in_kind == "tm" else [D, T])
    x_out = p.dout("x_out", [D, T])
    ident = p.din("c_ident", [128, 128])
    W = {}
    W["c_amask"] = p.din("c_amask", [32, 512])
    W["c_mask16"] = p.din("c_mask16", [128, T])
    W["c_mask32"] = p.din("c_mask32", [128, T])
    W["key_wT"] = p.din("key_wT", [D, D])
    W["vq_wT"] = p.din("vq_wT", [D, 2 * D])
    W["out_wT"] = p.din("out_wT", [D, D])
    W["key_b_r"] = p.din("key_b_r", [128, 8])
    W["out_b_r"] = p.din("out_b_r", [128, 8])
    W["c_ident64"] = p.din("c_ident64", [128, 64])
    p.start()
    cm = alloc_common(p)
    ln = alloc_ln(p, cm)
    gs = alloc_gla(p, cm)
    emit_init_consts(p, cm, ident)
    if x_in_kind == "tm":
        emit_load_x_tm(p, cm, x_in)
    else:
        emit_load_x_fm(p, cm, x_in)
    emit_ln_stats(p, cm, ln)
    emit_gla(p, cm, ln, gs, W)
    emit_store_x_fm(p, cm, x_out)
    p.finalize()
    return p


def gla_perm():
    perm = np.zeros(1024, np.int64)
    i = 0
    for fc in range(8):
        for s in range(2):
            for e in range(64):
                perm[i] = 64 * (fc + 8 * s) + e
                i += 1
    return perm


def gla_host_inputs(inp):
    pm = gla_perm()
    f = {}
    f["key_wT"] = np.ascontiguousarray(inp["L0_attn_key_w"].T[:, pm])
    vqT = inp["L0_attn_vq_w"].T
    f["vq_wT"] = np.ascontiguousarray(np.concatenate([vqT[:, :1024][:, pm], vqT[:, 1024:][:, pm]], axis=1))
    f["out_wT"] = np.ascontiguousarray(inp["L0_attn_out_w"].T[pm, :])
    f["key_b_r"] = np.ascontiguousarray(inp["L0_attn_key_b"][pm].reshape(8, 128).T)
    f["out_b_r"] = np.ascontiguousarray(inp["L0_attn_out_b"].reshape(8, 128).T)
    return f



def build_attn1_program():
    p = Prog("attn1")
    x_in = p.din("x_in", [D, T])
    x_out = p.dout("x_out", [D, T])
    ident = p.din("c_ident", [128, 128])
    W = {}
    W["Wq"] = p.din("L1_attn_Wq", [D, D])
    W["Wk"] = p.din("L1_attn_Wk", [D, D])
    W["Wv"] = p.din("L1_attn_Wv", [D, D])
    W["Wo"] = p.din("L1_attn_Wo", [D, D])
    W["lambda_q1"] = p.din("L1_attn_lambda_q1", [64])
    W["lambda_k1"] = p.din("L1_attn_lambda_k1", [64])
    W["lambda_q2"] = p.din("L1_attn_lambda_q2", [64])
    W["lambda_k2"] = p.din("L1_attn_lambda_k2", [64])
    W["ln_w_rep"] = p.din("ln_w_rep", [128, 128])
    W["ln_b_rep"] = p.din("ln_b_rep", [128, 128])
    W["c_cos"] = p.din("c_cos", [T, 64])
    W["c_sin"] = p.din("c_sin", [T, 64])
    p.start()
    cm = alloc_common(p)
    ln = alloc_ln(p, cm)
    at = alloc_attn1(p, cm)
    emit_init_consts(p, cm, ident)
    emit_load_x_fm(p, cm, x_in)
    emit_ln_stats(p, cm, ln)
    emit_attn1(p, cm, ln, at, W)
    emit_store_x_fm(p, cm, x_out)
    p.finalize()
    return p


def attn1_host_inputs(inp):
    return {"ln_w_rep": np.ascontiguousarray(np.broadcast_to(inp["L1_attn_ln_w"], (128, 128))),
            "ln_b_rep": np.ascontiguousarray(np.broadcast_to(inp["L1_attn_ln_b"], (128, 128)))}



def build_fused_program():
    p = Prog("fused")
    x_in = p.din("x_in", [T, D])
    x_out = p.dout("x_out", [T, D])
    ident = p.din("c_ident", [128, 128])
    rw0 = p.din("L0_ffn_w_router", [D, NE])
    w1_0 = p.din("L0_ffn_w1", [NE, D, 2816])
    w2r_0 = p.din("w2r_l0", [NE, FC, 128, 22, 128])
    w3_0 = p.din("L0_ffn_w3", [NE, D, 2816])
    experts0 = [{"w1": w1_0[e:e + 1].squeeze(), "w3": w3_0[e:e + 1].squeeze(), "w2r": w2r_0[e:e + 1].squeeze()}
                for e in range(NE)]
    cfg0 = MoECfg(2816, experts0, rw0, 0.0, 0.5)
    rw1 = p.din("L1_ffn_Wr", [D, NE])
    wg = p.din("L1_ffn_Wg", [NE, D, 2048])
    wu = p.din("L1_ffn_Wu", [NE, D, 2048])
    wdr = p.din("w2r_l1", [NE, FC, 128, 16, 128])
    sg = p.din("L1_ffn_Sg", [D, 2048])
    su = p.din("L1_ffn_Su", [D, 2048])
    sdr = p.din("w2r_sd", [FC, 128, 16, 128])
    experts1 = [{"w1": wg[e:e + 1].squeeze(), "w3": wu[e:e + 1].squeeze(), "w2r": wdr[e:e + 1].squeeze()}
                for e in range(NE)]
    experts1.append({"w1": sg, "w3": su, "w2r": sdr, "shared": True})
    cfg1 = MoECfg(2048, experts1, rw1, 1e-20, 0.5)
    GW = {}
    GW["c_amask"] = p.din("c_amask", [32, 512])
    GW["c_mask16"] = p.din("c_mask16", [128, T])
    GW["c_mask32"] = p.din("c_mask32", [128, T])
    GW["key_wT"] = p.din("key_wT", [D, D])
    GW["vq_wT"] = p.din("vq_wT", [D, 2 * D])
    GW["out_wT"] = p.din("out_wT", [D, D])
    GW["key_b_r"] = p.din("key_b_r", [128, 8])
    GW["out_b_r"] = p.din("out_b_r", [128, 8])
    GW["c_ident64"] = p.din("c_ident64", [128, 64])
    AW = {}
    AW["Wq"] = p.din("L1_attn_Wq", [D, D])
    AW["Wk"] = p.din("L1_attn_Wk", [D, D])
    AW["Wv"] = p.din("L1_attn_Wv", [D, D])
    AW["Wo"] = p.din("L1_attn_Wo", [D, D])
    AW["lambda_q1"] = p.din("L1_attn_lambda_q1", [64])
    AW["lambda_k1"] = p.din("L1_attn_lambda_k1", [64])
    AW["lambda_q2"] = p.din("L1_attn_lambda_q2", [64])
    AW["lambda_k2"] = p.din("L1_attn_lambda_k2", [64])
    AW["ln_w_rep"] = p.din("ln_w_rep", [128, 128])
    AW["ln_b_rep"] = p.din("ln_b_rep", [128, 128])
    AW["c_cos"] = p.din("c_cos", [T, 64])
    AW["c_sin"] = p.din("c_sin", [T, 64])

    p.start()
    cm = alloc_common(p)
    ln = alloc_ln(p, cm)
    emit_init_consts(p, cm, ident)
    p.push_scope()
    emit_load_x_tm(p, cm, x_in)
    p.barrier()
    p.pop_scope()
    p.new_epoch()

    def section(emitfn, allocfn):
        p.push_scope()
        buf = allocfn()
        emit_ln_stats(p, cm, ln)
        emitfn(buf)
        p.barrier()
        p.pop_scope()
        p.new_epoch()

    section(lambda ms: emit_moe(p, cm, ln, ms, cfg0), lambda: alloc_moe(p, cm, 2816))
    section(lambda gs: emit_gla(p, cm, ln, gs, GW), lambda: alloc_gla(p, cm))
    section(lambda ms: emit_moe(p, cm, ln, ms, cfg0), lambda: alloc_moe(p, cm, 2816))
    section(lambda ms: emit_moe(p, cm, ln, ms, cfg1), lambda: alloc_moe(p, cm, 2048))
    section(lambda at: emit_attn1(p, cm, ln, at, AW), lambda: alloc_attn1(p, cm))
    section(lambda ms: emit_moe(p, cm, ln, ms, cfg1), lambda: alloc_moe(p, cm, 2048))

    p.push_scope()
    emit_store_x_tm(p, cm, x_out)
    p.pop_scope()
    p.finalize()
    return p


def kernel_fused(inputs):
    consts = host_constants()
    extra = host_extra(inputs)
    x = inputs["L0_attn_x"]
    prog = build_fused_program()
    in_maps = []
    for i in range(8):
        in_maps.append(make_feeds(prog, inputs, consts, extra, np.ascontiguousarray(x[i])))
    res = run_bass_kernel_spmd(prog.nc, in_maps, core_ids=list(range(8)))
    return np.stack([np.asarray(r["x_out"]) for r in res.results], axis=0).astype(np.float32)



def _run(p: Prog, per_core_inputs):
    res = run_bass_kernel_spmd(p.nc, per_core_inputs, core_ids=list(range(len(per_core_inputs))))
    return res.results


def make_feeds(p, inputs, consts, extra, x_shard):
    feed = {}
    for n in p.in_names:
        if n == "x_in":
            feed[n] = x_shard
        elif n in extra:
            feed[n] = extra[n]
        elif n in consts:
            feed[n] = consts[n]
        elif n in inputs:
            feed[n] = inputs[n]
        else:
            raise KeyError(n)
    return feed


def host_extra(inputs):
    extra = {}
    extra["w2r_l0"] = relayout_w2(inputs["L0_ffn_w2"])
    extra["w2r_l1"] = relayout_w2(inputs["L1_ffn_Wd"])
    extra["w2r_sd"] = relayout_w2(inputs["L1_ffn_Sd"])
    extra.update(gla_host_inputs(inputs))
    extra.update(attn1_host_inputs(inputs))
    return extra


def kernel(**inputs):
    inputs = {k: np.ascontiguousarray(v, dtype=np.float32) for k, v in inputs.items()}
    return kernel_fused(inputs)


def kernel_unfused(**inputs):
    inputs = {k: np.ascontiguousarray(v, dtype=np.float32) for k, v in inputs.items()}
    consts = host_constants()
    extra = host_extra(inputs)
    x = inputs["L0_attn_x"]
    shards = [np.ascontiguousarray(x[i]) for i in range(8)]
    progs = [
        build_ffn_program(0, "tm"),
        build_gla_program("fm"),
        build_ffn_program(0, "fm"),
        build_ffn_program(1, "fm"),
        build_attn1_program(),
        build_ffn_program(1, "fm", "tm"),
    ]
    cur = shards
    for prog in progs:
        in_maps = [make_feeds(prog, inputs, consts, extra, cur[i]) for i in range(8)]
        res = run_bass_kernel_spmd(prog.nc, in_maps, core_ids=list(range(8)))
        cur = [np.asarray(r["x_out"]) for r in res.results]
    return np.stack(cur, axis=0).astype(np.float32)
